# version 16
# baseline (speedup 1.0000x reference)
import numpy as np
import ml_dtypes
import concourse.bass as bass
import concourse.mybir as mybir
from concourse.bass_utils import run_bass_kernel_spmd

F32 = mybir.dt.float32
BF16 = mybir.dt.bfloat16
I32 = mybir.dt.int32
AF = mybir.ActivationFunctionType
ALU = mybir.AluOpType
AX = mybir.AxisListType

T = 2048
NT = 16
NCH = 16
D = 1024
NI = 8
DK = 1024
DV = 2048
NH = 4
INNER = 4096
GRP = 256
NG = T // GRP
LN16 = float(np.log(16.0))

C_HALF, C_NHALF, C_P16, C_N16, C_NLN16, C_TINY, C_ONE, C_EPS = range(8)

_TAPS = {}

ENGS = ("T", "V", "S", "G", "Y")
QUEUES = ("QS", "QA", "QG", "QO")
OWNERS = ENGS + QUEUES
QISSUER = {"QS": "Y", "QA": "S", "QG": "G", "QO": "G"}


class Buf:
    __slots__ = ("name", "writer", "readers")

    def __init__(self, name):
        self.name = name
        self.writer = None
        self.readers = []


class Op:
    __slots__ = ("build", "waits", "marked", "inc", "fd")

    def __init__(self, build, fd):
        self.build = build
        self.fd = fd
        self.waits = []
        self.marked = False
        self.inc = 0


class Plan:
    def __init__(self):
        self.ops = {o: [] for o in OWNERS}
        self.estream = {e: [] for e in ENGS}
        self.synced = {e: {} for e in ENGS}
        self.snaps = {o: [] for o in OWNERS}

    def emit(self, eng, build, reads=(), writes=(), queue=None, fd=1024):
        owner = queue if queue else eng
        k = len(self.ops[owner]) + 1
        op = Op(build, fd)

        deps = set()
        for b in reads:
            w = b.writer
            if w is None:
                continue
            o2, k2 = w
            if o2 == eng and queue is None:
                if eng in ("V", "G") and self.ops[o2][k2 - 1].fd < 512:
                    deps.add(w)
                continue
            deps.add(w)
        for b in writes:
            w = b.writer
            if w is not None:
                o2, k2 = w
                if not (o2 == eng and queue is None):
                    deps.add(w)
                elif eng in ("V", "G") and self.ops[o2][k2 - 1].fd < 512:
                    deps.add(w)
            for r in b.readers:
                o2, k2 = r
                if o2 == eng and queue is None:
                    continue
                deps.add(r)

        syn = self.synced[eng]
        final = []
        for (o2, k2) in sorted(deps):
            if syn.get(o2, 0) >= k2:
                continue
            final.append((o2, k2))
            if syn.get(o2, 0) < k2:
                syn[o2] = k2
            for o3, k3 in self.snaps[o2][k2 - 1].items():
                if syn.get(o3, 0) < k3:
                    syn[o3] = k3
            self.ops[o2][k2 - 1].marked = True

        op.waits = final
        self.ops[owner].append(op)
        self.estream[eng].append((owner, k))
        snap = dict(syn)
        snap[owner] = k
        self.snaps[owner].append(snap)

        for b in writes:
            b.writer = (owner, k)
            b.readers = []
        for b in reads:
            b.readers.append((owner, k))
        return op

    def finalize(self):
        incno = {}
        for o in OWNERS:
            cnt = 0
            lst = []
            for op in self.ops[o]:
                if o in QUEUES:
                    op.marked = True
                if op.marked:
                    cnt += 1
                op.inc = cnt
                lst.append(cnt)
            incno[o] = lst
        self.incno = incno
        self.totals = {o: (incno[o][-1] if incno[o] else 0) for o in OWNERS}


def bf16(x):
    return np.asarray(x, dtype=ml_dtypes.bfloat16)


def build_program():
    nc = bass.Bass("TRN2", target_bir_lowering=False,
                   detect_race_conditions=False)
    P = Plan()

    def din(name, shape, dt=F32):
        return nc.dram_tensor(name, list(shape), dt, kind="ExternalInput").ap()

    x_own = din("x_own", [T, D])
    x_prev = din("x_prev", [T, D])
    Wq = din("L0_attn_Wq", [D, DK])
    Wk = din("L0_attn_Wk", [D, DK])
    Wv = din("L0_attn_Wv", [D, DV])
    Wg = din("L0_attn_Wg", [D, DV])
    Wgk1 = din("wgk1pad", [D, 128])
    Wgk2 = din("L0_attn_Wgk2", [16, DK])
    bgk2 = din("L0_attn_bgk2", [DK])
    gnw = din("L0_attn_g_norm_w", [512])
    Wo = din("L0_attn_Wo", [DV, D])
    wgate = din("L0_ffn_w_gate", [INNER, D])
    wup = din("L0_ffn_w_up", [INNER, D])
    wdown = din("L0_ffn_w_down", [D, INNER])
    cbD = din("cb", [128, 16])
    identD = din("ident", [128, 128], BF16)
    utriD = din("utri", [128, 128], BF16)
    onescolD = din("onescol", [128, 1])
    onesrowD = din("onesrow", [1, 128])
    allonesD = din("allones", [128, 128])
    yD = nc.dram_tensor("y", [T, D], F32, kind="ExternalOutput").ap()

    spk = "ExternalOutput" if "_spills" in _TAPS else "Internal"
    SP_KI = nc.dram_tensor("sp_ki", [NCH, NI, 128, 128], BF16, kind=spk).ap()
    SP_KS = nc.dram_tensor("sp_ks", [NCH, 128, DK], BF16, kind=spk).ap()
    SP_V = nc.dram_tensor("sp_v", [NCH, 128, DV], BF16, kind=spk).ap()
    SP_GS = nc.dram_tensor("sp_gs", [NCH, 128, DV], BF16, kind=spk).ap()
    SP_X2 = nc.dram_tensor("sp_x2", [NCH, 128, D], F32, kind=spk).ap()
    SP_TG = nc.dram_tensor("sp_tg", [NI, 128, INNER], BF16, kind=spk).ap()
    SP_TU = nc.dram_tensor("sp_tu", [NI, 128, INNER], BF16, kind=spk).ap()
    SP_TD = nc.dram_tensor("sp_td", [32, 128, D], BF16, kind=spk).ap()
    SP_GQ = nc.dram_tensor("sp_gq", [NT, 32, 128, 128], BF16, kind=spk).ap()

    BASE = (nc.sbuf_base + 512 + 31) & ~31
    off = [BASE]

    _offs = {}

    def place(name, shape, dt, at=None):
        nbytes = int(np.prod(shape[1:])) * mybir.dt.size(dt)
        if at is None:
            at = off[0]
            off[0] = (at + nbytes + 31) & ~31
        _offs[name] = at
        return nc.alloc_sbuf_tensor_at(name, list(shape), dt, offset=at)

    cb = place("cb", [128, 16], F32)
    bgk_sb = place("bgk", [128, 8], F32)
    gnwb = place("gnwb", [128, 512], F32)
    ident = place("ident", [128, 128], BF16)
    utri = place("utri", [128, 128], BF16)
    onescol = place("onescol", [128, 1], F32)
    onesrow = place("onesrow", [1, 128], F32)
    wgk1b = place("wgk1b", [128, 8, 128], BF16)
    allones = place("allones", [128, 128], F32)
    wgk2b = place("wgk2b", [16, DK], BF16)
    t1b = place("t1b", [16, GRP], BF16)
    gnw_sb = place("gnw_sb", [1, 512], F32)
    S_sb = place("S", [128, 8, 512], F32)
    S_bf = place("S_bf", [128, 8, 512], BF16)
    expbl = place("expbl", [128, 8, 16], F32)
    blastb = place("blastb", [128, 8, 2], F32)
    sums = place("sums", [128, 16], F32)
    sumsq = place("sumsq", [128, 16], F32)
    meanb = place("meanb", [128, 16], F32)
    aepsb = place("aepsb", [128, 16], F32)
    rstdb = place("rstdb", [128, 16], F32)
    nta = place("nta", [128, 16], F32)
    sinv = place("sinv", [128, 16], F32)
    s2inv = place("s2inv", [128, 16], F32)
    dqg = place("dqg", [128, 16], F32)
    dqu = place("dqu", [128, 16], F32)
    dqd = place("dqd", [128, 16], F32)
    mbc = place("mbc", [128, 4], F32)
    mibc = place("mibc", [128, 4], F32)
    part4 = place("part4", [128, 16], F32)
    tot1 = place("tot1", [128, 1], F32)
    t11 = place("t11", [1, 4], F32)
    ssq4 = place("ssq4", [128, 4], F32)
    a4 = place("a4", [128, 4], F32)
    sd4 = place("sd4", [128, 4], F32)
    rs4 = place("rs4", [128, 4], F32)
    n4 = place("n4", [128, 4], F32)
    cst = place("cst", [128, 12], F32)
    sc_sb = place("sc_sb", [128, 128], BF16)
    ternTst = place("ternTst", [128, 4, 128], BF16)
    ternTstB = place("ternTstB", [128, 4, 128], BF16, at=_offs["gnwb"])

    WOFF = off[0]
    wqk = place("wqk", [128, 16, 1024], BF16, at=WOFF)
    wvb = place("wvb", [128, 8, 2048], BF16, at=WOFF)
    wgb = place("wgb", [128, 8, 2048], BF16, at=WOFF)
    wob = place("wob", [128, 16, 1024], BF16, at=WOFF)
    tgs = place("tgs", [128, 2, 8, 512], BF16, at=WOFF)
    tus = place("tus", [128, 2, 8, 512], BF16, at=WOFF + 16384)
    off[0] = WOFF + 32768

    AOFF = off[0]
    xld = place("xld", [128, 4, 1024], F32, at=AOFF)
    junkA = place("junkA", [128, 1024], F32, at=AOFF + 16384)
    htmp = place("htmp", [128, 1024], BF16, at=AOFF + 20480)
    U0 = place("U0", [128, 8, GRP], F32, at=AOFF)
    U1 = place("U1", [128, 8, GRP], F32, at=AOFF + 8192)
    U2 = place("U2", [128, 8, GRP], F32, at=AOFF + 16384)
    U3 = place("U3", [128, 8, GRP], F32, at=AOFF + 24576)
    ksfm = place("ksfm", [128, 8, GRP], BF16, at=AOFF + 32768)
    kstm = place("kstm", [128, 2, 1024], BF16, at=AOFF + 36864)
    vtmp = place("vtmp", [128, 2, 2048], BF16, at=AOFF)
    silu_t = place("silu_t", [128, 512], F32, at=AOFF + 8192)
    osb = place("osb", [128, 2048], F32, at=AOFF)
    asb = place("asb", [128, 1024], F32, at=AOFF + 2048)
    oft = place("oft", [128, 2048], BF16, at=AOFF + 8192)
    oTs = place("oTs", [128, 16, 128], BF16, at=AOFF + 12288)
    t2x2 = place("t2x2", [128, 1024], F32, at=AOFF + 16384)
    junkC = place("junkC", [128, 512], F32, at=AOFF + 20480)
    h2b = place("h2b", [128, 1024], F32, at=AOFF + 20480)
    rt = place("rt", [128, 1024], F32, at=AOFF + 24576)
    ri = place("ri", [128, 1024], I32, at=AOFF + 28672)
    ra = place("ra", [128, 1024], F32, at=AOFF + 28672)
    rf = place("rf", [128, 1024], F32, at=AOFF + 32768)
    xqt = place("xqt", [128, 1024], BF16, at=AOFF + 36864)
    wch = place("wch", [128, 2, 4096], F32, at=AOFF)
    ternb = place("ternb", [128, 4096], BF16, at=AOFF + 32768)
    silu_f1 = place("silu_f1", [128, 512], F32, at=AOFF + 24576)
    rt1p = [place("rt1a", [128, 1024], F32, at=AOFF),
            place("rt1b", [128, 1024], F32, at=AOFF + 4096)]
    rf1p = [place("rf1a", [128, 1024], F32, at=AOFF + 8192),
            place("rf1b", [128, 1024], F32, at=AOFF + 12288)]
    ri1q = place("ri1q", [128, 1024], I32, at=AOFF + 16384)
    ra1q = place("ra1q", [128, 1024], F32, at=AOFF + 20480)
    guq = place("guq", [128, 4096], BF16, at=AOFF + 26624)
    guqT = place("guqT", [128, 16, 128], BF16, at=AOFF + 34816)
    guqs = place("guqs", [128, 2, 32, 128], BF16, at=AOFF)
    x2t = place("x2t", [128, 2, 1024], F32, at=AOFF + 16384)
    fsb = place("fsb", [128, 1024], F32, at=AOFF + 24576)
    t3b = place("t3b", [128, 1024], F32, at=AOFF + 28672)
    off[0] = AOFF + 40960

    BOFF = off[0]
    qg = place("qg", [128, 8, 2048], BF16, at=BOFF)
    f0t = place("f0t", [128, 4096], F32, at=BOFF)
    f0bn = place("f0bn", [128, 4096], F32, at=BOFF + 16384)
    gu_lo = place("gu_lo", [128, 2, 4096], F32, at=BOFF)
    td_lo = place("td_lo", [128, 16, 1024], BF16, at=BOFF)
    off[0] = BOFF + 32768

    COFF = off[0]
    h_fm = place("h_fm", [128, 8, 2048], BF16, at=COFF)
    xq_fm = place("xq_fm", [128, 8, 2048], BF16, at=COFF)
    ysb = place("ysb", [128, 2, 1024], F32, at=COFF)
    off[0] = COFF + 32768

    DOFF = off[0]
    vtmp0 = place("vtmp0", [128, 2, 2048], BF16, at=DOFF)
    vs = place("vs", [128, 2, 2048], BF16, at=DOFF)
    kss = place("kss", [128, 2, 1024], BF16, at=DOFF + 8192)
    kis = place("kis", [128, 2, 8, 128], BF16, at=DOFF + 12288)
    gss = place("gss", [128, 2, 2048], BF16, at=DOFF + 16384)
    xcs = place("xcs", [128, 2, 1024], F32, at=DOFF + 24576)
    gu_hi = place("gu_hi", [128, 2, 4096], F32, at=DOFF)
    td_hi = place("td_hi", [128, 16, 1024], BF16, at=DOFF)
    kigrp = place("kigrp", [128, 8, GRP], BF16, at=DOFF)
    off[0] = DOFF + 32768

    assert off[0] <= 227328, f"sbuf overflow: {off[0]}"
    nc.sbuf_base = off[0]

    PS = [nc.alloc_psum_tensor(f"ps{i}", [128, 1024], F32) for i in range(4)]

    def psv(bank, lo=0, hi=512):
        return PS[bank // 2][:, 512 * (bank % 2) + lo: 512 * (bank % 2) + hi]

    bufs = {}

    def B(name):
        if name not in bufs:
            bufs[name] = Buf(name)
        return bufs[name]

    PB = [B(f"pb{i}") for i in range(8)]
    bW, bA, bB, bC = B("W"), B("A"), B("BB"), B("BC")
    bS = [B(f"S{d}") for d in range(8)]
    bST = B("stats")
    bCONST = B("const")
    bSC = B("scores")
    bKIG = B("kigrp")
    bT1 = B("t1")
    bTST = B("ternTst")
    bXQ = B("xq")
    bD2 = [B("D0"), B("D1")]
    bTG2 = [B("TG0"), B("TG1")]
    bF2 = [B("F20"), B("F21")]
    bOUT = B("out")
    bSP = {n: B("sp_" + n) for n in
           ("ki", "ks", "v", "gs", "x2", "tg", "tu", "td", "gq")}
    bSPGQ = [B("spgq%d" % _t) for _t in range(NT)]

    def dma(queue, out, in_, reads, writes):
        eng = QISSUER[queue]
        attr = {"Y": "sync", "S": "scalar", "G": "gpsimd"}[eng]

        def bld(nc, out=out, in_=in_, attr=attr):
            return getattr(nc, attr).dma_start(out=out, in_=in_)
        P.emit(eng, bld, reads=reads, writes=writes, queue=queue)

    def mm(out_ap, lhsT, rhs, start, stop, reads, writes):
        def bld(nc, o=out_ap, l=lhsT, r=rhs, st=start, sp=stop):
            return nc.tensor.matmul(o, l, r, start=st, stop=sp)
        P.emit("T", bld, reads=reads, writes=writes)

    _LUT_FUNCS = (AF.Sqrt, AF.Sigmoid, AF.Ln, AF.Exp, AF.Silu)

    def act(out_ap, in_ap, func, reads, writes, scale=None, bias=None,
            accum=None, fd=1024):
        def bld(nc, o=out_ap, i=in_ap, f=func, s=scale, b=bias, a=accum):
            kw = {}
            if s is not None:
                kw["scale"] = s
            if b is not None:
                kw["bias"] = b
            if a is not None:
                kw["accum_out"] = a
            if f in _LUT_FUNCS:
                nc.scalar.preload_act_table(f)
            return nc.scalar.activation(o, i, f, **kw)
        P.emit("S", bld, reads=reads, writes=writes, fd=fd)

    def scopy(out_ap, in_ap, reads, writes, fd=512):
        def bld(nc, o=out_ap, i=in_ap):
            return nc.scalar.copy(o, i)
        P.emit("S", bld, reads=reads, writes=writes, fd=fd)

    def vop(method, args, reads, writes, fd, eng="V"):
        def bld(nc, m=method, a=args, e=eng):
            eo = nc.vector if e == "V" else nc.gpsimd
            return getattr(eo, m)(*a)
        P.emit(eng, bld, reads=reads, writes=writes, fd=fd)

    def tap(label, ap, shape, dt=None, reads=()):
        if label not in _TAPS:
            return
        dt = dt or ap.dtype
        dbg = nc.dram_tensor("dbg_" + label, list(shape), dt,
                             kind="ExternalOutput").ap()
        dma("QO", dbg, ap, list(reads), [B("dbg_" + label)])

    def ln_finish(csum, cssq, cmean, crstd, cscr, n):
        vop("tensor_scalar_mul", (cst[:, cmean:cmean + 1], cst[:, csum:csum + 1],
            1.0 / n), [bST], [bST], fd=1)
        vop("tensor_mul", (cst[:, cscr:cscr + 1], cst[:, cmean:cmean + 1],
            cst[:, cmean:cmean + 1]), [bST], [bST], fd=1)
        vop("scalar_tensor_tensor",
            (cst[:, crstd:crstd + 1], cst[:, cssq:cssq + 1], 1.0 / n,
             cst[:, cscr:cscr + 1], ALU.mult, ALU.subtract), [bST], [bST], fd=1)
        vop("tensor_scalar_add", (cst[:, crstd:crstd + 1], cst[:, crstd:crstd + 1],
            1e-5), [bST], [bST], fd=1)
        act(cst[:, cscr:cscr + 1], cst[:, crstd:crstd + 1], AF.Sqrt,
            [bST], [bST], fd=1)
        vop("reciprocal", (cst[:, crstd:crstd + 1], cst[:, cscr:cscr + 1]),
            [bST], [bST], fd=1)

    dma("QS", cb[:, :], cbD, [], [bCONST])
    dma("QS", ident[:, :], identD, [], [bCONST])
    dma("QS", utri[:, :], utriD, [], [bCONST])
    dma("QS", onescol[:, :], onescolD, [], [bCONST])
    dma("QS", onesrow[:, :], onesrowD, [], [bCONST])
    for d in range(8):
        dma("QS", bgk_sb[:, d:d + 1], bgk2[128 * d:128 * (d + 1)].unsqueeze(1),
            [], [bCONST])
    dma("QS", gnw_sb[:, :], gnw.unsqueeze(0), [], [bCONST])
    dma("QG", wgk1b[:, :, :], Wgk1.rearrange("(a p) f -> p a f", p=128), [], [bCONST])
    dma("QS", allones[:, :], allonesD, [], [bCONST])
    dma("QG", wgk2b[:, :], Wgk2, [], [bCONST])
    mm(psv(4), onesrow[:, :], gnw_sb[:, :], True, True, [bCONST], [PB[4]])
    vop("tensor_copy", (gnwb[:, :], psv(4)), [PB[4], bCONST], [bCONST], fd=512)
    vop("memset", (S_sb[:, :, :], 0.0), [], list(bS), fd=4096)

    def emit_ln_h(xdram, tapit=False):
        for ch in range(4):
            dma("QS", xld[:, :, :],
                xdram[512 * ch:512 * (ch + 1), :].rearrange("(a p) f -> p a f", p=128),
                [], [bA])
            for tt in range(4):
                t = 4 * ch + tt
                vop("reduce_sum", (sums[:, t:t + 1], xld[:, tt, :], AX.X),
                    [bA], [bST], fd=1)
                act(junkA[:, :], xld[:, tt, :], AF.Square, [bA], [bA, bST],
                    accum=sumsq[:, t:t + 1])
        if tapit:
            tap("sumsq", sumsq[:, :], [128, 16], reads=[bST])
        vop("tensor_scalar_mul", (meanb[:, :], sums[:, :], 1.0 / D), [bST], [bST], fd=16)
        vop("tensor_mul", (nta[:, :], meanb[:, :], meanb[:, :]), [bST], [bST], fd=16)
        if tapit:
            tap("nta", nta[:, :], [128, 16], reads=[bST])
        vop("scalar_tensor_tensor",
            (aepsb[:, :], sumsq[:, :], 1.0 / D, nta[:, :], ALU.mult, ALU.subtract),
            [bST], [bST], fd=16)
        vop("tensor_scalar_add", (aepsb[:, :], aepsb[:, :], 1e-5), [bST], [bST], fd=16)
        act(nta[:, :], aepsb[:, :], AF.Sqrt, [bST], [bST], fd=16)
        vop("reciprocal", (rstdb[:, :], nta[:, :]), [bST], [bST], fd=16)
        vop("tensor_mul", (sums[:, :], rstdb[:, :], rstdb[:, :]), [bST], [bST], fd=16)
        vop("tensor_mul", (sums[:, :], sums[:, :], aepsb[:, :]), [bST], [bST], fd=16)
        vop("tensor_scalar", (sums[:, :], sums[:, :], -0.5, 1.5, ALU.mult, ALU.add),
            [bST], [bST], fd=16)
        vop("tensor_mul", (rstdb[:, :], rstdb[:, :], sums[:, :]), [bST], [bST], fd=16)
        if tapit:
            tap("aeps", aepsb[:, :], [128, 16], reads=[bST])
            tap("rstd", rstdb[:, :], [128, 16], reads=[bST])
            tap("meanb", meanb[:, :], [128, 16], reads=[bST])
        for ch in range(4):
            dma("QS", xld[:, :, :],
                xdram[512 * ch:512 * (ch + 1), :].rearrange("(a p) f -> p a f", p=128),
                [], [bA])
            for tt in range(4):
                t = 4 * ch + tt
                vop("tensor_scalar",
                    (htmp[:, :], xld[:, tt, :], meanb[:, t:t + 1], rstdb[:, t:t + 1],
                     ALU.subtract, ALU.mult), [bA, bST], [bA], fd=1024)
                for f in range(8):
                    bank = 4 + (f // 4)
                    mm(psv(bank, 128 * (f % 4), 128 * (f % 4) + 128),
                       htmp[:, 128 * f:128 * (f + 1)], ident[:, :], True, True,
                       [bA, bCONST], [PB[bank]])
                for half in range(2):
                    bank = 4 + half
                    scopy(h_fm[:, 4 * half:4 * half + 4, 128 * t:128 * (t + 1)],
                          psv(bank).rearrange("p (a f) -> p a f", a=4),
                          [PB[bank]], [bC])

    def emit_gk(g, want_q):
        cols = slice(GRP * g, GRP * (g + 1))
        for it in range(NI):
            mm(PS[2][:, 0:GRP], wgk1b[:, it, :], h_fm[:, it, cols],
               it == 0, it == NI - 1, [bCONST, bC], [PB[4]])
        scopy(t1b[:, :], PS[2][0:16, 0:GRP], [PB[4]], [bT1], fd=GRP)
        for d in range(NI):
            bank = 2 + (d % 2)
            mm(psv(bank, 0, GRP), wgk2b[:, 128 * d:128 * (d + 1)], t1b[:, :],
               True, True, [bCONST, bT1], [PB[bank]])
            act(U0[:, d, :], psv(bank, 0, GRP), AF.Sigmoid, [PB[bank], bCONST],
                [bA], bias=bgk_sb[:, d:d + 1], fd=GRP)
        vop("tensor_scalar_max", (U0[:, :, :], U0[:, :, :], 1e-38), [bA], [bA],
            fd=8 * GRP)
        if g == 0 and want_q:
            tap("sg0", U0[:, :, :], [128, 8, GRP], reads=[bA])
        act(U1[:, :, :], U0[:, :, :], AF.Ln, [bA], [bA], fd=8 * GRP)
        for d in range(NI):
            for lc in range(GRP // 128):
                si = U1[:, d, 128 * lc:128 * (lc + 1)]
                so = U0[:, d, 128 * lc:128 * (lc + 1)]
                vop("tensor_tensor_scan", (so, si, si, 0.0, ALU.add, ALU.bypass),
                    [bA], [bA], fd=128)
        for lc in range(GRP // 128):
            vop("tensor_copy", (blastb[:, :, lc:lc + 1], U0[:, :, 128 * lc + 127:128 * lc + 128]),
                [bA], [bST], fd=8)
        if g == 0 and want_q:
            tap("ls0", U1[:, :, :], [128, 8, GRP], reads=[bA])
            tap("b0", U0[:, :, :], [128, 8, GRP], reads=[bA])
            tap("blast0", blastb[:, :, :], [128, 8, 2], reads=[bST])
        act(expbl[:, :, 2 * g:2 * g + 2], blastb[:, :, :], AF.Exp,
            [bST, bCONST], [bST], scale=cb[:, C_P16:C_P16 + 1], fd=16)
        if want_q:
            act(U2[:, :, :], U0[:, :, :], AF.Exp, [bA, bCONST], [bA],
                scale=cb[:, C_P16:C_P16 + 1], bias=cb[:, C_NLN16:C_NLN16 + 1],
                fd=8 * GRP)
            act(U3[:, :, :], U0[:, :, :], AF.Exp, [bA, bCONST], [bA],
                scale=cb[:, C_N16:C_N16 + 1], fd=8 * GRP)
        for d in range(NI):
            for lc in range(GRP // 128):
                vop("tensor_scalar",
                    (U1[:, d, 128 * lc:128 * (lc + 1)],
                     U0[:, d, 128 * lc:128 * (lc + 1)],
                     blastb[:, d, lc:lc + 1], None, ALU.subtract),
                    [bA, bST], [bA], fd=128)
        act(U0[:, :, :], U1[:, :, :], AF.Exp, [bA, bCONST], [bA],
            scale=cb[:, C_N16:C_N16 + 1], fd=8 * GRP)

    def emit_ks_transpose_group():
        for lc in range(2):
            for dh in range(2):
                bank = 4 + dh
                for dd in range(4):
                    d = 4 * dh + dd
                    mm(psv(bank, 128 * dd, 128 * dd + 128),
                       ksfm[:, d, 128 * lc:128 * (lc + 1)], ident[:, :], True, True,
                       [bA, bCONST], [PB[bank]])
                scopy(kstm[:, lc, 512 * dh:512 * (dh + 1)], psv(bank),
                      [PB[bank]], [bA])

    emit_ln_h(x_prev)
    wv_p0 = qg
    dma("QG", wqk[:, 0:8, :], Wk.rearrange("(a p) f -> p a f", p=128), [bW], [bW])
    dma("QG", wv_p0[:, :, :], Wv.rearrange("(a p) f -> p a f", p=128), [bB], [bB])
    for g in range(NG):
        cols = slice(GRP * g, GRP * (g + 1))
        emit_gk(g, want_q=False)
        for d in range(NI):
            bank = 0 + (d % 2)
            for it in range(NI):
                mm(psv(bank, 0, GRP), wqk[:, it, 128 * d:128 * (d + 1)],
                   h_fm[:, it, cols], it == 0, it == NI - 1, [bW, bC], [PB[bank]])
            vop("tensor_mul", (ksfm[:, d, :], psv(bank, 0, GRP), U0[:, d, :]),
                [PB[bank], bA], [bA], fd=GRP)
        emit_ks_transpose_group()
        for tt in range(2):
            for dvg in range(4):
                bank = 2 + (dvg % 2)
                for it in range(NI):
                    mm(psv(bank),
                       h_fm[:, it, GRP * g + 128 * tt: GRP * g + 128 * (tt + 1)],
                       wv_p0[:, it, 512 * dvg:512 * (dvg + 1)],
                       it == 0, it == NI - 1, [bB, bC], [PB[bank]])
                scopy(vtmp0[:, tt, 512 * dvg:512 * (dvg + 1)], psv(bank),
                      [PB[bank]], [bD2[0]])
        for lc in range(2):
            c = 2 * g + lc
            for hh in range(NH):
                for j in range(2):
                    dd = 2 * hh + j
                    bank = 6 + j
                    mm(psv(bank),
                       kstm[:, lc, 256 * hh + 128 * j: 256 * hh + 128 * (j + 1)],
                       vtmp0[:, lc, 512 * hh:512 * (hh + 1)], True, True,
                       [bA, bD2[0], bS[dd]], [PB[bank]])
                    vop("scalar_tensor_tensor",
                        (S_sb[:, dd, :], S_sb[:, dd, :], expbl[:, dd, c:c + 1],
                         psv(bank), ALU.mult, ALU.add),
                        [bS[dd], bST, PB[bank]], [bS[dd]], fd=512)

    emit_ln_h(x_own, tapit=True)
    tap("h", h_fm[:, :, :], [128, 8, 2048], reads=[bC])

    dma("QG", wqk[:, 0:8, :], Wq.rearrange("(a p) f -> p a f", p=128), [bW], [bW])
    dma("QG", wqk[:, 8:16, :], Wk.rearrange("(a p) f -> p a f", p=128), [bW], [bW])
    for g in range(NG):
        cols = slice(GRP * g, GRP * (g + 1))
        emit_gk(g, want_q=True)
        for d in range(NI):
            bank = 0 + (d % 2)
            for it in range(NI):
                mm(psv(bank, 0, GRP), wqk[:, it, 128 * d:128 * (d + 1)],
                   h_fm[:, it, cols], it == 0, it == NI - 1, [bW, bC], [PB[bank]])
            vop("tensor_mul", (qg[:, d, cols], psv(bank, 0, GRP), U2[:, d, :]),
                [PB[bank], bA], [bB], fd=GRP)
            bank2 = 2 + (d % 2)
            for it in range(NI):
                mm(psv(bank2, 0, GRP), wqk[:, 8 + it, 128 * d:128 * (d + 1)],
                   h_fm[:, it, cols], it == 0, it == NI - 1, [bW, bC], [PB[bank2]])
            vop("tensor_mul", (kigrp[:, d, :], psv(bank2, 0, GRP), U3[:, d, :]),
                [PB[bank2], bA], [bKIG, bD2[0]], fd=GRP)
            vop("tensor_mul", (ksfm[:, d, :], psv(bank2, 0, GRP), U0[:, d, :]),
                [PB[bank2], bA], [bA], fd=GRP)

        for lc in range(2):
            def ki_out(nc, g=g, lc=lc):
                return nc.scalar.dma_start(
                    out=SP_KI[2 * g + lc, :, :, :].rearrange("a p t -> p a t"),
                    in_=kigrp[:, :, 128 * lc:128 * (lc + 1)])
            P.emit("S", ki_out, reads=[bKIG, bD2[0]], writes=[bSP["ki"]], queue="QA")
        emit_ks_transpose_group()

        def ks_out(nc, g=g):
            return nc.scalar.dma_start(
                out=SP_KS[2 * g:2 * g + 2, :, :].rearrange("l p d -> p l d"),
                in_=kstm[:, :, :])
        P.emit("S", ks_out, reads=[bA], writes=[bSP["ks"]], queue="QA")

    tap("qg", qg[:, :, :], [128, 8, 2048], reads=[bB])
    tap("expbl", expbl[:, :, :], [128, 8, 16], reads=[bST])
    dma("QG", wvb[:, :, :], Wv.rearrange("(a p) f -> p a f", p=128), [bW], [bW])
    for g in range(NG):
        for tt in range(2):
            for dvg in range(4):
                bank = (dvg % 2)
                for it in range(NI):
                    mm(psv(bank),
                       h_fm[:, it, GRP * g + 128 * tt: GRP * g + 128 * (tt + 1)],
                       wvb[:, it, 512 * dvg:512 * (dvg + 1)],
                       it == 0, it == NI - 1, [bW, bC], [PB[bank]])
                scopy(vtmp[:, tt, 512 * dvg:512 * (dvg + 1)], psv(bank),
                      [PB[bank]], [bA])

        def v_out(nc, g=g):
            return nc.scalar.dma_start(
                out=SP_V[2 * g:2 * g + 2, :, :].rearrange("l p d -> p l d"),
                in_=vtmp[:, :, :])
        P.emit("S", v_out, reads=[bA], writes=[bSP["v"]], queue="QA")

    dma("QG", wgb[:, :, :], Wg.rearrange("(a p) f -> p a f", p=128), [bW], [bW])
    for g in range(NG):
        for tt in range(2):
            for hh in range(4):
                bank = (hh % 2)
                for it in range(NI):
                    mm(psv(bank),
                       h_fm[:, it, GRP * g + 128 * tt: GRP * g + 128 * (tt + 1)],
                       wgb[:, it, 512 * hh:512 * (hh + 1)],
                       it == 0, it == NI - 1, [bW, bC], [PB[bank]])
                act(silu_t[:, :], psv(bank), AF.Silu, [PB[bank]], [bA], fd=512)
                vop("tensor_mul", (vtmp[:, tt, 512 * hh:512 * (hh + 1)],
                                   silu_t[:, :], gnwb[:, :]),
                    [bA, bCONST], [bA], fd=512)

        def gs_out(nc, g=g):
            return nc.scalar.dma_start(
                out=SP_GS[2 * g:2 * g + 2, :, :].rearrange("l p d -> p l d"),
                in_=vtmp[:, :, :])
        P.emit("S", gs_out, reads=[bA], writes=[bSP["gs"]], queue="QA")

    dma("QG", wob[:, :, :], Wo.rearrange("(a p) f -> p a f", p=128), [bW], [bW])
    for dd in range(8):
        tap(f"Sprev{dd}", S_sb[:, dd, :], [128, 512], reads=[bS[dd]])
        scopy(S_bf[:, dd, :], S_sb[:, dd, :], [bS[dd]], [bS[dd]])

    def c_prefetch(c):
        i = c % 2
        dma("QS", vs[:, i, :], SP_V[c, :, :], [bSP["v"]], [bD2[i]])
        dma("QS", kss[:, i, :], SP_KS[c, :, :], [bSP["ks"]], [bD2[i]])
        dma("QS", kis[:, i, :, :], SP_KI[c, :, :, :].rearrange("a p t -> p a t"),
            [bSP["ki"]], [bD2[i]])
        dma("QS", gss[:, i, :], SP_GS[c, :, :], [bSP["gs"]], [bD2[i]])
        dma("QS", xcs[:, i, :], x_own[128 * c:128 * (c + 1), :], [], [bD2[i]])

    c_prefetch(0)
    for c in range(NCH):
        i = c % 2
        if c + 1 < NCH:
            c_prefetch(c + 1)
        ccols = slice(128 * c, 128 * (c + 1))
        for hh in range(NH):
            for j in range(2):
                dd = 2 * hh + j
                mm(PS[0][:, 0:128], kis[:, i, dd, :], qg[:, dd, ccols],
                   j == 0, j == 1, [bD2[i], bB], [PB[0]])
            vop("tensor_mul", (sc_sb[:, :], PS[0][:, 0:128], utri[:, :]),
                [PB[0], bCONST], [bSC], fd=128)
            mm(psv(1), sc_sb[:, :], vs[:, i, 512 * hh:512 * (hh + 1)],
               True, False, [bSC, bD2[i]], [PB[1]])
            for j in range(2):
                dd = 2 * hh + j
                mm(psv(1), qg[:, dd, ccols], S_bf[:, dd, :],
                   False, j == 1, [bB, bS[dd]], [PB[1]])
            vop("tensor_copy", (osb[:, 512 * hh:512 * (hh + 1)], psv(1)),
                [PB[1]], [bA], fd=512)
            for j in range(2):
                dd = 2 * hh + j
                bank = 2 + j
                mm(psv(bank),
                   kss[:, i, 256 * hh + 128 * j: 256 * hh + 128 * (j + 1)],
                   vs[:, i, 512 * hh:512 * (hh + 1)], True, True,
                   [bD2[i], bS[dd]], [PB[bank]])
                vop("scalar_tensor_tensor",
                    (S_sb[:, dd, :], S_sb[:, dd, :], expbl[:, dd, c:c + 1],
                     psv(bank), ALU.mult, ALU.add),
                    [bS[dd], bST, PB[bank]], [bS[dd]], fd=512)
                scopy(S_bf[:, dd, :], S_sb[:, dd, :], [bS[dd]], [bS[dd]])
            act(junkC[:, :], osb[:, 512 * hh:512 * (hh + 1)], AF.Square,
                [bA], [bA, bST], accum=ssq4[:, hh:hh + 1], fd=512)
        if c == 0:
            tap("osb0", osb[:, :], [128, 2048], reads=[bA])
            tap("ssq40", ssq4[:, :], [128, 4], reads=[bST])
        vop("tensor_scalar", (a4[:, :], ssq4[:, :], 1.0 / 512, 1e-5,
            ALU.mult, ALU.add), [bST], [bST], fd=4)
        act(sd4[:, :], a4[:, :], AF.Sqrt, [bST], [bST], fd=4)
        vop("reciprocal", (rs4[:, :], sd4[:, :]), [bST], [bST], fd=4)
        vop("tensor_mul", (n4[:, :], rs4[:, :], rs4[:, :]), [bST], [bST], fd=4)
        vop("tensor_mul", (n4[:, :], n4[:, :], a4[:, :]), [bST], [bST], fd=4)
        vop("tensor_scalar", (n4[:, :], n4[:, :], -0.5, 1.5, ALU.mult, ALU.add),
            [bST], [bST], fd=4)
        vop("tensor_mul", (rs4[:, :], rs4[:, :], n4[:, :]), [bST], [bST], fd=4)
        for hh in range(NH):
            vop("scalar_tensor_tensor",
                (oft[:, 512 * hh:512 * (hh + 1)], osb[:, 512 * hh:512 * (hh + 1)],
                 rs4[:, hh:hh + 1], gss[:, i, 512 * hh:512 * (hh + 1)],
                 ALU.mult, ALU.mult), [bA, bST, bD2[i]], [bA], fd=512)
        for b4 in range(4):
            bank = 4 + (b4 % 2)
            for dd in range(4):
                dvt = 4 * b4 + dd
                mm(psv(bank, 128 * dd, 128 * dd + 128),
                   oft[:, 128 * dvt:128 * (dvt + 1)], ident[:, :], True, True,
                   [bA, bCONST], [PB[bank]])
            scopy(oTs[:, 4 * b4:4 * b4 + 4, :],
                  psv(bank).rearrange("p (a f) -> p a f", a=4), [PB[bank]], [bA])
        for dh in range(2):
            for dvt in range(16):
                mm(PS[3][:, 512 * dh:512 * (dh + 1)], oTs[:, dvt, :],
                   wob[:, dvt, 512 * dh:512 * (dh + 1)],
                   dvt == 0, dvt == 15, [bA, bW], [PB[6 + dh]])
        vop("tensor_copy", (asb[:, :], PS[3][:, :]), [PB[6], PB[7]], [bA], fd=1024)
        if c == 0:
            tap("asb0", asb[:, :], [128, 1024], reads=[bA])
            tap("oft0", oft[:, :], [128, 2048], reads=[bA])
        vop("reduce_sum", (cst[:, 0:1], asb[:, :], AX.X), [bA], [bST], fd=1)
        act(junkC[:, :], asb[:, 0:512], AF.Square, [bA], [bA, bST],
            accum=cst[:, 1:2], fd=512)
        act(junkC[:, :], asb[:, 512:1024], AF.Square, [bA], [bA, bST],
            accum=cst[:, 2:3], fd=512)
        vop("tensor_add", (cst[:, 1:2], cst[:, 1:2], cst[:, 2:3]), [bST], [bST], fd=1)
        ln_finish(0, 1, 3, 4, 5, D)
        vop("tensor_mul", (cst[:, 5:6], cst[:, 3:4], cst[:, 4:5]), [bST], [bST], fd=1)
        vop("tensor_scalar_mul", (cst[:, 5:6], cst[:, 5:6], -1.0), [bST], [bST], fd=1)
        act(t2x2[:, :], asb[:, :], AF.Identity, [bA, bST], [bA],
            scale=cst[:, 4:5], bias=cst[:, 5:6])
        vop("tensor_add", (t2x2[:, :], t2x2[:, :], xcs[:, i, :]),
            [bA, bD2[i]], [bA], fd=1024)

        if c == 0:
            tap("x20", t2x2[:, :], [128, 1024], reads=[bA])
        def x2_out(nc, c=c):
            return nc.scalar.dma_start(out=SP_X2[c, :, :], in_=t2x2[:, :])
        P.emit("S", x2_out, reads=[bA], writes=[bSP["x2"]], queue="QA")
        vop("reduce_sum", (cst[:, 6:7], t2x2[:, :], AX.X), [bA], [bST], fd=1)
        act(junkC[:, :], t2x2[:, 0:512], AF.Square, [bA], [bA, bST],
            accum=cst[:, 7:8], fd=512)
        act(junkC[:, :], t2x2[:, 512:1024], AF.Square, [bA], [bA, bST],
            accum=cst[:, 8:9], fd=512)
        vop("tensor_add", (cst[:, 7:8], cst[:, 7:8], cst[:, 8:9]), [bST], [bST], fd=1)
        ln_finish(6, 7, 9, 10, 11, D)
        vop("tensor_scalar", (h2b[:, :], t2x2[:, :], cst[:, 9:10], cst[:, 10:11],
            ALU.subtract, ALU.mult), [bA, bST], [bA], fd=1024)

        if c == 0:
            tap("h20", h2b[:, :], [128, 1024], reads=[bA])
            tap("cst0", cst[:, 0:11], [128, 11], reads=[bST])

        def redmax(nc):
            return nc.vector.reduce_max(cst[:, 11:12], h2b[:, :], axis=AX.X,
                                        apply_absolute_value=True)
        P.emit("V", redmax, reads=[bA], writes=[bST], fd=1)
        vop("tensor_scalar_max", (cst[:, 11:12], cst[:, 11:12], 1e-5),
            [bST], [bST], fd=1)
        vop("tensor_scalar_mul", (sinv[:, c:c + 1], cst[:, 11:12], 1.0 / 127.0),
            [bST], [bST], fd=1)
        vop("reciprocal", (cst[:, 11:12], sinv[:, c:c + 1]), [bST], [bST], fd=1)
        act(rt[:, :], h2b[:, :], AF.Identity, [bA, bST], [bA], scale=cst[:, 11:12])
        act(ri[:, :], h2b[:, :], AF.Identity, [bA, bST], [bA],
            scale=cst[:, 11:12])
        act(rf[:, :], ri[:, :], AF.Identity, [bA], [bA])
        vop("tensor_sub", (rt[:, :], rt[:, :], rf[:, :]), [bA], [bA], fd=1024)
        vop("tensor_single_scalar", (ra[:, :], rt[:, :], 0.5, ALU.is_gt),
            [bA], [bA], fd=1024)
        for _k in (1.5, 2.5, 3.5):
            vop("scalar_tensor_tensor", (ra[:, :], rt[:, :], _k, ra[:, :],
                ALU.is_gt, ALU.add), [bA], [bA], fd=1024)
        vop("scalar_tensor_tensor", (ra[:, :], rt[:, :], -0.5, ra[:, :],
            ALU.is_lt, ALU.subtract), [bA], [bA], fd=1024)
        for _k in (-1.5, -2.5, -3.5):
            vop("scalar_tensor_tensor", (ra[:, :], rt[:, :], _k, ra[:, :],
                ALU.is_lt, ALU.add), [bA], [bA], fd=1024)
        vop("tensor_sub", (xqt[:, :], rf[:, :], ra[:, :]), [bA], [bA], fd=1024)
        if c == 0:
            tap("xqt0", xqt[:, :], [128, 1024], reads=[bA])
        for half in range(2):
            bank = 4 + half
            for dd in range(4):
                f = 4 * half + dd
                mm(psv(bank, 128 * dd, 128 * dd + 128),
                   xqt[:, 128 * f:128 * (f + 1)], ident[:, :], True, True,
                   [bA, bCONST], [PB[bank]])
            scopy(xq_fm[:, 4 * half:4 * half + 4, 128 * c:128 * (c + 1)],
                  psv(bank).rearrange("p (a f) -> p a f", a=4),
                  [PB[bank]], [bXQ, bC])

    bWC = [B("wc0"), B("wc1")]
    bF0T, bF0B, bTRN = B("f0tb"), B("f0bb"), B("trnb")
    bTST2 = [bTST, B("ternTstB")]
    ternTst2 = [ternTst, ternTstB]
    _f0first = set()

    def f0w(key, bufs, extra):
        if key in _f0first:
            return bufs
        _f0first.add(key)
        return bufs + extra

    def emit_wquant(wdram, O, IW, mcol, spill, spkey):
        rows = 524288 // IW
        nchunks = O // rows
        n_it = IW // 128
        n_ot = rows // 128
        for ch in range(nchunks):
            pi = ch % 2
            src = wdram[rows * ch: rows * (ch + 1), :]
            if n_ot > 1:
                src = src.rearrange("(a p) f -> p a f", p=128)
                dst = wch[:, pi, :].rearrange("p (a f) -> p a f", a=n_ot)
            else:
                dst = wch[:, pi, :]
            dma("QS" if pi == 0 else "QG", dst, src, [],
                f0w("wc%d" % pi, [bWC[pi]], [bA]))

            def red(nc, pi=pi, ch=ch):
                return nc.vector.reduce_sum(part4[:, ch:ch + 1], wch[:, pi, :],
                                            axis=AX.X, apply_absolute_value=True)
            P.emit("V", red, reads=[bWC[pi]], writes=[bST], fd=1)
        vop("reduce_sum", (tot1[:, :], part4[:, 0:nchunks], AX.X), [bST], [bST], fd=1)
        mm(PS[2][:, 0:1], allones[:, :], tot1[:, :], True, True,
           [bCONST, bST], [PB[4]])
        vop("tensor_copy", (mbc[:, mcol:mcol + 1], PS[2][:, 0:1]), [PB[4]], [bST], fd=1)
        vop("tensor_scalar_mul", (mbc[:, mcol:mcol + 1], mbc[:, mcol:mcol + 1],
            1.0 / (O * IW)), [bST], [bST], fd=1)
        vop("tensor_scalar_max", (mbc[:, mcol:mcol + 1], mbc[:, mcol:mcol + 1],
            1e-5), [bST], [bST], fd=1)
        vop("reciprocal", (mibc[:, mcol:mcol + 1], mbc[:, mcol:mcol + 1]),
            [bST], [bST], fd=1)
        for ch in range(nchunks):
            pi = ch % 2
            src = wdram[rows * ch: rows * (ch + 1), :]
            if n_ot > 1:
                src = src.rearrange("(a p) f -> p a f", p=128)
                dst = wch[:, pi, :].rearrange("p (a f) -> p a f", a=n_ot)
            else:
                dst = wch[:, pi, :]
            dma("QS" if pi == 0 else "QG", dst, src, [], [bWC[pi]])
            act(f0t[:, :], wch[:, pi, :], AF.Identity, [bWC[pi], bST],
                f0w("f0t", [bF0T], [bB]), scale=mibc[:, mcol:mcol + 1], fd=4096)
            vop("tensor_single_scalar", (f0bn[:, :], f0t[:, :], -0.5, ALU.is_lt),
                [bF0T], f0w("f0b", [bF0B], [bB]), fd=4096)
            vop("scalar_tensor_tensor", (ternb[:, :], f0t[:, :], 0.5, f0bn[:, :],
                ALU.is_gt, ALU.subtract), [bF0T, bF0B],
                f0w("trn", [bTRN], [bA]), fd=4096)
            for ot in range(n_ot):
                oglob = n_ot * ch + ot
                for itb in range(n_it // 4):
                    bank = 4 + (itb % 2)
                    ti2 = (ot * (n_it // 4) + itb) % 2
                    for dd in range(4):
                        it = 4 * itb + dd
                        mm(psv(bank, 128 * dd, 128 * dd + 128),
                           ternb[:, IW * ot + 128 * it: IW * ot + 128 * (it + 1)],
                           ident[:, :], True, True, [bTRN, bCONST], [PB[bank]])
                    if ti2 == 0:
                        scopy(ternTst2[0][:, :, :],
                              psv(bank).rearrange("p (a f) -> p a f", a=4),
                              [PB[bank]], f0w("tst0", [bTST2[0]], [bCONST, bA]))
                    else:
                        vop("tensor_copy",
                            (ternTst2[1][:, :, :],
                             psv(bank).rearrange("p (a f) -> p a f", a=4)),
                            [PB[bank]], f0w("tst1", [bTST2[1]], [bCONST, bA]),
                            fd=512)

                    def tout(nc, itb=itb, oglob=oglob, spill=spill, ti2=ti2):
                        return nc.scalar.dma_start(
                            out=spill[4 * itb:4 * itb + 4, :,
                                      128 * oglob:128 * (oglob + 1)]
                            .rearrange("a p t -> p a t"),
                            in_=ternTst2[ti2][:, :, :])
                    P.emit("S", tout, reads=[bTST2[ti2]], writes=[bSP[spkey]],
                           queue="QA")

    emit_wquant(wgate, INNER, D, 0, SP_TG, "tg")
    emit_wquant(wup, INNER, D, 1, SP_TU, "tu")
    emit_wquant(wdown, D, INNER, 2, SP_TD, "td")

    tap("mbc", mbc[:, 0:3], [128, 3], reads=[bST])
    tap("sinv", sinv[:, :], [128, 16], reads=[bST])
    vop("tensor_scalar_mul", (dqg[:, :], sinv[:, :], mbc[:, 0:1]), [bST], [bST], fd=16)
    vop("tensor_scalar_mul", (dqu[:, :], sinv[:, :], mbc[:, 1:2]), [bST], [bST], fd=16)

    bGU = [bB, B("GUHI")]
    bRT = [B("rt1a"), B("rt1b")]
    bRF = [B("rf1a"), B("rf1b")]
    bRI, bRA = B("ri1q"), B("ra1q")
    bGUQ, bGUT, bSIL = B("guq1"), B("guT1"), B("silF1")
    _first = set()

    def fwr(key, bufs, extra):
        if key in _first:
            return bufs
        _first.add(key)
        return bufs + extra

    def round_pieces(t):
        reg = (t // 2) % 2
        gu_t = gu_lo if reg == 0 else gu_hi
        gb = bGU[reg]
        sc = 10 + (t % 2)

        def p_rmax():
            def rmx(nc, gu_t=gu_t, ti=t % 2, sc=sc):
                return nc.vector.reduce_max(cst[:, sc:sc + 1], gu_t[:, ti, :],
                                            axis=AX.X, apply_absolute_value=True)
            P.emit("V", rmx, reads=[gb], writes=[bST], fd=1)
            vop("tensor_scalar_max", (cst[:, sc:sc + 1], cst[:, sc:sc + 1], 1e-5),
                [bST], [bST], fd=1)
            vop("tensor_scalar_mul", (s2inv[:, t:t + 1], cst[:, sc:sc + 1],
                1.0 / 127.0), [bST], [bST], fd=1)
            vop("reciprocal", (cst[:, sc:sc + 1], s2inv[:, t:t + 1]),
                [bST], [bST], fd=1)

        def p_acts(q):
            pi = q % 2
            qc = slice(1024 * q, 1024 * (q + 1))
            act(rt1p[pi][:, :], gu_t[:, t % 2, qc], AF.Identity,
                [gb, bST], fwr("rt%d" % pi, [bRT[pi]], [bA]),
                scale=cst[:, sc:sc + 1], bias=cb[:, 8:9], fd=1024)
            act(ri1q[:, :], gu_t[:, t % 2, qc], AF.Identity,
                [gb, bST], fwr("ri", [bRI], [bA]),
                scale=cst[:, sc:sc + 1], fd=1024)
            act(rf1p[pi][:, :], ri1q[:, :], AF.Identity,
                [bRI], fwr("rf%d" % pi, [bRF[pi]], [bA]), fd=1024)

        def p_ladder(q):
            pi = q % 2
            qc = slice(1024 * q, 1024 * (q + 1))
            vop("scalar_tensor_tensor", (ra1q[:, :], rf1p[pi][:, :], -1.0,
                rt1p[pi][:, :], ALU.mult, ALU.add),
                [bRF[pi], bRT[pi]], fwr("ra", [bRA], [bA]), fd=1024)
            for _k in (64.0, 32.0, 16.0, 8.0, 4.0, 2.0, 1.0):
                vop("tensor_single_scalar", (rf1p[pi][:, :], ra1q[:, :], _k,
                    ALU.is_ge), [bRA], [bRF[pi]], fd=1024)
                vop("scalar_tensor_tensor", (ra1q[:, :], rf1p[pi][:, :], -_k,
                    ra1q[:, :], ALU.mult, ALU.add), [bRF[pi]], [bRA], fd=1024)
            vop("scalar_tensor_tensor", (guq[:, qc], ra1q[:, :], -1.0,
                rt1p[pi][:, :], ALU.mult, ALU.add),
                [bRA, bRT[pi]], fwr("guq", [bGUQ], [bA]), fd=1024)
            vop("tensor_single_scalar", (guq[:, qc], guq[:, qc], -64.0, ALU.add),
                [bGUQ], [bGUQ], fd=1024)

        def p_trans():
            tap("guqt%d" % t, guq[:, :], [128, 4096], reads=[bGUQ])
            for half in range(2):
                for b4 in range(4 * half, 4 * half + 4):
                    bank = 4 + (b4 % 2)
                    for dd in range(4):
                        it = 4 * b4 + dd
                        mm(psv(bank, 128 * dd, 128 * dd + 128),
                           guq[:, 128 * it:128 * (it + 1)], ident[:, :],
                           True, True, [bGUQ, bCONST], [PB[bank]])
                    scopy(guqT[:, 4 * (b4 % 4):4 * (b4 % 4) + 4, :],
                          psv(bank).rearrange("p (a f) -> p a f", a=4),
                          [PB[bank]], fwr("guT", [bGUT], [bA]))

                def gq_out(nc, t=t, half=half):
                    return nc.scalar.dma_start(
                        out=SP_GQ[t, 16 * half:16 * (half + 1), :, :]
                        .rearrange("a p u -> p a u"),
                        in_=guqT[:, :, :])
                P.emit("S", gq_out, reads=[bGUT], writes=[bSPGQ[t]],
                       queue="QA")

        return [lambda: (p_rmax(), p_acts(0), p_acts(1)),
                lambda: (p_ladder(0), p_acts(2)),
                lambda: (p_ladder(1), p_acts(3)),
                lambda: p_ladder(2),
                lambda: p_ladder(3),
                p_trans]

    pieces = []
    DRAIN = [3, 3, 2, 2, 1, 1, 0, 0]
    for p in range(8):
        preg = p % 2
        gu_t = gu_lo if preg == 0 else gu_hi
        gkey = "gulo" if preg == 0 else "guhi"
        gext = [] if preg == 0 else [bD2[0], bD2[1]]
        for og in range(8):
            pi = og % 2
            wr_extra = [bW] if (p == 0 and og < 2) else []
            dma("QS", tgs[:, pi, :, :],
                SP_TG[:, :, 512 * og:512 * (og + 1)].rearrange("a p t -> p a t"),
                [bSP["tg"]], [bTG2[pi]] + wr_extra)
            dma("QG", tus[:, pi, :, :],
                SP_TU[:, :, 512 * og:512 * (og + 1)].rearrange("a p t -> p a t"),
                [bSP["tu"]], [bTG2[pi]] + wr_extra)
            for ti in range(2):
                t = 2 * p + ti
                gpb = psv(2 * ti)
                upb = psv(1 + 2 * ti)
                for it in range(NI):
                    mm(gpb, xq_fm[:, it, 128 * t:128 * (t + 1)],
                       tgs[:, pi, it, :], it == 0, it == NI - 1,
                       [bXQ, bTG2[pi]], [PB[2 * ti]])
                act(silu_f1[:, :], gpb, AF.Silu, [PB[2 * ti], bST],
                    fwr("sil", [bSIL], [bA]), scale=dqg[:, t:t + 1], fd=512)
                for it in range(NI):
                    mm(upb, xq_fm[:, it, 128 * t:128 * (t + 1)],
                       tus[:, pi, it, :], it == 0, it == NI - 1,
                       [bXQ, bTG2[pi]], [PB[1 + 2 * ti]])
                vop("scalar_tensor_tensor",
                    (gu_t[:, ti, 512 * og:512 * (og + 1)], upb,
                     dqu[:, t:t + 1], silu_f1[:, :], ALU.mult, ALU.mult),
                    [PB[1 + 2 * ti], bST, bSIL],
                    fwr(gkey, [bGU[preg]], gext), fd=512)
            for _ in range(DRAIN[og]):
                if pieces:
                    pieces.pop(0)()
        if p == 0:
            tap("guA", gu_lo[:, 0, :], [128, 4096], reads=[bGU[0]])
        pieces.extend(round_pieces(2 * p))
        pieces.extend(round_pieces(2 * p + 1))
    while pieces:
        pieces.pop(0)()

    vop("tensor_scalar_mul", (dqd[:, :], s2inv[:, :], mbc[:, 2:3]), [bST], [bST], fd=16)
    tap("dqdf", dqd[:, :], [128, 16], reads=[bST])
    tap("xqf", xq_fm[:, :, :], [128, 8, 2048], reads=[bXQ])
    tap("s2inv", s2inv[:, :], [128, 16], reads=[bST])
    tap("dqgf", dqg[:, :], [128, 16], reads=[bST])
    tap("dquf", dqu[:, :], [128, 16], reads=[bST])

    dma("QS", td_lo[:, :, :], SP_TD[0:16, :, :].rearrange("a p t -> p a t"),
        [bSP["td"]], [bB])
    dma("QS", td_hi[:, :, :], SP_TD[16:32, :, :].rearrange("a p t -> p a t"),
        [bSP["td"]], [bD2[0], bD2[1], bGU[1]])

    def f2_prefetch(t, first=False):
        i = t % 2
        extra = ([bA, bRT[0], bRT[1], bRF[0], bRF[1], bRI, bRA,
                  bGUQ, bGUT, bSIL] if first else [])
        dma("QS", guqs[:, i, :, :], SP_GQ[t, :, :, :].rearrange("a p u -> p a u"),
            [bSPGQ[t]], [bF2[i]] + extra)
        dma("QS", x2t[:, i, :], SP_X2[t, :, :], [bSP["x2"]], [bF2[i]])

    f2_prefetch(0, first=True)
    for t in range(NT):
        i = t % 2
        if t + 1 < NT:
            f2_prefetch(t + 1)
        for dh in range(2):
            bank = 2 * dh
            for it in range(32):
                tdt = td_lo if it < 16 else td_hi
                tdb = bB if it < 16 else B("D0")
                mm(psv(bank), guqs[:, i, it, :],
                   tdt[:, it % 16, 512 * dh:512 * (dh + 1)],
                   it == 0, it == 31, [bF2[i], tdb], [PB[bank]])
            act(fsb[:, 512 * dh:512 * (dh + 1)], psv(bank), AF.Copy,
                [PB[bank], bST], [bA], scale=dqd[:, t:t + 1], fd=512)
        if t == 0:
            tap("fsb0", fsb[:, :], [128, 1024], reads=[bA])
            tap("tdl0", td_lo[:, 0, :], [128, 1024], reads=[bB])
            tap("guqs0", guqs[:, 0, :, :], [128, 32, 128], reads=[bF2[0]])
            tap("x2t0", x2t[:, 0, :], [128, 1024], reads=[bF2[0]])
        vop("reduce_sum", (cst[:, 0:1], fsb[:, :], AX.X), [bA], [bST], fd=1)
        act(t3b[:, 0:512], fsb[:, 0:512], AF.Square, [bA], [bA, bST],
            accum=cst[:, 1:2], fd=512)
        act(t3b[:, 0:512], fsb[:, 512:1024], AF.Square, [bA], [bA, bST],
            accum=cst[:, 2:3], fd=512)
        vop("tensor_add", (cst[:, 1:2], cst[:, 1:2], cst[:, 2:3]), [bST], [bST], fd=1)
        ln_finish(0, 1, 3, 4, 5, D)
        vop("tensor_mul", (cst[:, 5:6], cst[:, 3:4], cst[:, 4:5]), [bST], [bST], fd=1)
        vop("tensor_scalar_mul", (cst[:, 5:6], cst[:, 5:6], -1.0), [bST], [bST], fd=1)
        act(t3b[:, :], fsb[:, :], AF.Identity, [bA, bST], [bA],
            scale=cst[:, 4:5], bias=cst[:, 5:6])
        vop("tensor_add", (ysb[:, i, :], t3b[:, :], x2t[:, i, :]),
            [bA, bF2[i]], fwr("ysb%d" % i, [B("ysb%d" % i)], [bXQ]), fd=1024)

        def y_out(nc, t=t, i=i):
            return nc.gpsimd.dma_start(out=yD[128 * t:128 * (t + 1), :],
                                       in_=ysb[:, i, :])
        P.emit("G", y_out, reads=[B("ysb%d" % i)], writes=[bOUT], queue="QO")

    P.finalize()
    sems = {o: nc.alloc_semaphore(name=f"sem_{o}") for o in OWNERS}

    def val(o2, k2):
        v = P.incno[o2][k2 - 1]
        return v * 16 if o2 in QUEUES else v

    with nc.Block() as blk:
        handles = {"T": blk.tensor, "V": blk.vector, "S": blk.scalar,
                   "G": blk.gpsimd, "Y": blk.sync}
        for e in ENGS:
            if not P.estream[e] and e != "Y":
                continue

            def section(eobj, e=e):
                for (owner, k) in P.estream[e]:
                    op = P.ops[owner][k - 1]
                    for (o2, k2) in op.waits:
                        eobj.wait_ge(sems[o2], val(o2, k2))
                    ins = op.build(nc)
                    if op.marked:
                        ins.then_inc(sems[owner], 16 if owner in QUEUES else 1)
                if e == "Y":
                    for q in QUEUES:
                        tot = P.totals[q]
                        if tot:
                            eobj.wait_ge(sems[q], tot * 16)
            handles[e](section)
    return nc


_NC_CACHE = []


def kernel(**inputs):
    x = np.ascontiguousarray(inputs["L0_attn_hidden_states"], dtype=np.float32)
    Bsz, L, Dm = x.shape
    if not _NC_CACHE:
        _NC_CACHE.append(build_program())
    nc = _NC_CACHE[0]

    cbv = np.zeros((128, 16), np.float32)
    cbv[:, C_HALF] = 0.5
    cbv[:, C_NHALF] = -0.5
    cbv[:, C_P16] = 1.0 / 16
    cbv[:, C_N16] = -1.0 / 16
    cbv[:, C_NLN16] = -LN16
    cbv[:, C_TINY] = 1e-38
    cbv[:, C_ONE] = 1.0
    cbv[:, C_EPS] = 1e-5
    cbv[:, 8] = 64.5
    identv = bf16(np.eye(128, dtype=np.float32))
    utriv = bf16(np.tril(np.ones((128, 128), np.float32)).T)
    wgk1pad = np.zeros((D, 128), np.float32)
    wgk1pad[:, 0:16] = np.asarray(inputs["L0_attn_Wgk1"], np.float32)
    common = {
        "cb": cbv, "ident": identv, "utri": utriv,
        "onescol": np.ones((128, 1), np.float32),
        "onesrow": np.ones((1, 128), np.float32),
        "allones": np.ones((128, 128), np.float32),
        "wgk1pad": wgk1pad,
    }
    for nm in ("L0_attn_Wq", "L0_attn_Wk", "L0_attn_Wv", "L0_attn_Wg",
               "L0_attn_Wgk2", "L0_attn_bgk2",
               "L0_attn_g_norm_w", "L0_attn_Wo", "L0_ffn_w_gate",
               "L0_ffn_w_up", "L0_ffn_w_down"):
        common[nm] = np.ascontiguousarray(inputs[nm], np.float32)

    zeros = np.zeros((T, Dm), np.float32)
    in_maps = []
    for c in range(8):
        b, h = c // 2, c % 2
        m = dict(common)
        m["x_own"] = np.ascontiguousarray(x[b, h * T:(h + 1) * T, :])
        m["x_prev"] = np.ascontiguousarray(x[b, 0:T, :]) if h == 1 else zeros
        in_maps.append(m)

    res = run_bass_kernel_spmd(nc, in_maps, core_ids=list(range(8)))
    out = np.empty((Bsz, L, Dm), np.float32)
    for c in range(8):
        b, h = c // 2, c % 2
        out[b, h * T:(h + 1) * T, :] = res.results[c]["y"]
    return out


# version 17
# speedup vs baseline: 1.0196x; 1.0196x over previous
import numpy as np
import ml_dtypes
import concourse.bass as bass
import concourse.mybir as mybir
from concourse.bass_utils import run_bass_kernel_spmd

F32 = mybir.dt.float32
BF16 = mybir.dt.bfloat16
I32 = mybir.dt.int32
AF = mybir.ActivationFunctionType
ALU = mybir.AluOpType
AX = mybir.AxisListType

T = 2048
NT = 16
NCH = 16
D = 1024
NI = 8
DK = 1024
DV = 2048
NH = 4
INNER = 4096
GRP = 256
NG = T // GRP
LN16 = float(np.log(16.0))

C_HALF, C_NHALF, C_P16, C_N16, C_NLN16, C_TINY, C_ONE, C_EPS = range(8)

_TAPS = {}

ENGS = ("T", "V", "S", "G", "Y")
QUEUES = ("QS", "QA", "QG", "QO")
OWNERS = ENGS + QUEUES
QISSUER = {"QS": "Y", "QA": "S", "QG": "G", "QO": "G"}


class Buf:
    __slots__ = ("name", "writer", "readers")

    def __init__(self, name):
        self.name = name
        self.writer = None
        self.readers = []


class Op:
    __slots__ = ("build", "waits", "marked", "inc", "fd")

    def __init__(self, build, fd):
        self.build = build
        self.fd = fd
        self.waits = []
        self.marked = False
        self.inc = 0


class Plan:
    def __init__(self):
        self.ops = {o: [] for o in OWNERS}
        self.estream = {e: [] for e in ENGS}
        self.synced = {e: {} for e in ENGS}
        self.snaps = {o: [] for o in OWNERS}

    def emit(self, eng, build, reads=(), writes=(), queue=None, fd=1024):
        owner = queue if queue else eng
        k = len(self.ops[owner]) + 1
        op = Op(build, fd)

        deps = set()
        for b in reads:
            w = b.writer
            if w is None:
                continue
            o2, k2 = w
            if o2 == eng and queue is None:
                if eng in ("V", "G") and self.ops[o2][k2 - 1].fd < 512:
                    deps.add(w)
                continue
            deps.add(w)
        for b in writes:
            w = b.writer
            if w is not None:
                o2, k2 = w
                if not (o2 == eng and queue is None):
                    deps.add(w)
                elif eng in ("V", "G") and self.ops[o2][k2 - 1].fd < 512:
                    deps.add(w)
            for r in b.readers:
                o2, k2 = r
                if o2 == eng and queue is None:
                    continue
                deps.add(r)

        syn = self.synced[eng]
        final = []
        for (o2, k2) in sorted(deps):
            if syn.get(o2, 0) >= k2:
                continue
            final.append((o2, k2))
            if syn.get(o2, 0) < k2:
                syn[o2] = k2
            for o3, k3 in self.snaps[o2][k2 - 1].items():
                if syn.get(o3, 0) < k3:
                    syn[o3] = k3
            self.ops[o2][k2 - 1].marked = True

        op.waits = final
        self.ops[owner].append(op)
        self.estream[eng].append((owner, k))
        snap = dict(syn)
        snap[owner] = k
        self.snaps[owner].append(snap)

        for b in writes:
            b.writer = (owner, k)
            b.readers = []
        for b in reads:
            b.readers.append((owner, k))
        return op

    def finalize(self):
        incno = {}
        for o in OWNERS:
            cnt = 0
            lst = []
            for op in self.ops[o]:
                if o in QUEUES:
                    op.marked = True
                if op.marked:
                    cnt += 1
                op.inc = cnt
                lst.append(cnt)
            incno[o] = lst
        self.incno = incno
        self.totals = {o: (incno[o][-1] if incno[o] else 0) for o in OWNERS}


def bf16(x):
    return np.asarray(x, dtype=ml_dtypes.bfloat16)


def build_program():
    nc = bass.Bass("TRN2", target_bir_lowering=False,
                   detect_race_conditions=False)
    P = Plan()

    def din(name, shape, dt=F32):
        return nc.dram_tensor(name, list(shape), dt, kind="ExternalInput").ap()

    x_own = din("x_own", [T, D])
    x_prev = din("x_prev", [T, D])
    Wq = din("L0_attn_Wq", [D, DK])
    Wk = din("L0_attn_Wk", [D, DK])
    Wv = din("L0_attn_Wv", [D, DV])
    Wg = din("L0_attn_Wg", [D, DV])
    Wgk1 = din("wgk1pad", [D, 128])
    Wgk2 = din("L0_attn_Wgk2", [16, DK])
    bgk2 = din("L0_attn_bgk2", [DK])
    gnw = din("L0_attn_g_norm_w", [512])
    Wo = din("L0_attn_Wo", [DV, D])
    wgate = din("L0_ffn_w_gate", [INNER, D])
    wup = din("L0_ffn_w_up", [INNER, D])
    wdown = din("L0_ffn_w_down", [D, INNER])
    cbD = din("cb", [128, 16])
    identD = din("ident", [128, 128], BF16)
    utriD = din("utri", [128, 128], BF16)
    onescolD = din("onescol", [128, 1])
    onesrowD = din("onesrow", [1, 128])
    allonesD = din("allones", [128, 128])
    yD = nc.dram_tensor("y", [T, D], F32, kind="ExternalOutput").ap()

    spk = "ExternalOutput" if "_spills" in _TAPS else "Internal"
    SP_KI = nc.dram_tensor("sp_ki", [NCH, NI, 128, 128], BF16, kind=spk).ap()
    SP_KS = nc.dram_tensor("sp_ks", [NCH, 128, DK], BF16, kind=spk).ap()
    SP_V = nc.dram_tensor("sp_v", [NCH, 128, DV], BF16, kind=spk).ap()
    SP_GS = nc.dram_tensor("sp_gs", [NCH, 128, DV], BF16, kind=spk).ap()
    SP_X2 = nc.dram_tensor("sp_x2", [NCH, 128, D], F32, kind=spk).ap()
    SP_TG = nc.dram_tensor("sp_tg", [NI, 128, INNER], BF16, kind=spk).ap()
    SP_TU = nc.dram_tensor("sp_tu", [NI, 128, INNER], BF16, kind=spk).ap()
    SP_TD = nc.dram_tensor("sp_td", [32, 128, D], BF16, kind=spk).ap()
    SP_GQ = nc.dram_tensor("sp_gq", [NT, 32, 128, 128], BF16, kind=spk).ap()

    BASE = (nc.sbuf_base + 512 + 31) & ~31
    off = [BASE]

    _offs = {}

    def place(name, shape, dt, at=None):
        nbytes = int(np.prod(shape[1:])) * mybir.dt.size(dt)
        if at is None:
            at = off[0]
            off[0] = (at + nbytes + 31) & ~31
        _offs[name] = at
        return nc.alloc_sbuf_tensor_at(name, list(shape), dt, offset=at)

    cb = place("cb", [128, 16], F32)
    bgk_sb = place("bgk", [128, 8], F32)
    gnwb = place("gnwb", [128, 512], F32)
    ident = place("ident", [128, 128], BF16)
    utri = place("utri", [128, 128], BF16)
    onescol = place("onescol", [128, 1], F32)
    onesrow = place("onesrow", [1, 128], F32)
    wgk1b = place("wgk1b", [128, 8, 128], BF16)
    allones = place("allones", [128, 128], F32)
    wgk2b = place("wgk2b", [16, DK], BF16)
    t1b = place("t1b", [16, GRP], BF16)
    gnw_sb = place("gnw_sb", [1, 512], F32)
    S_sb = place("S", [128, 8, 512], F32)
    S_bf = place("S_bf", [128, 8, 512], BF16)
    expbl = place("expbl", [128, 8, 16], F32)
    blastb = place("blastb", [128, 8, 2], F32)
    sums = place("sums", [128, 16], F32)
    sumsq = place("sumsq", [128, 16], F32)
    meanb = place("meanb", [128, 16], F32)
    aepsb = place("aepsb", [128, 16], F32)
    rstdb = place("rstdb", [128, 16], F32)
    nta = place("nta", [128, 16], F32)
    sinv = place("sinv", [128, 16], F32)
    s2inv = place("s2inv", [128, 16], F32)
    dqg = place("dqg", [128, 16], F32)
    dqu = place("dqu", [128, 16], F32)
    dqd = place("dqd", [128, 16], F32)
    mbc = place("mbc", [128, 4], F32)
    mibc = place("mibc", [128, 4], F32)
    part4 = place("part4", [128, 16], F32)
    tot1 = place("tot1", [128, 1], F32)
    t11 = place("t11", [1, 4], F32)
    ssq4 = place("ssq4", [128, 4], F32)
    a4 = place("a4", [128, 4], F32)
    sd4 = place("sd4", [128, 4], F32)
    rs4 = place("rs4", [128, 4], F32)
    n4 = place("n4", [128, 4], F32)
    cst = place("cst", [128, 12], F32)
    sc_sb = place("sc_sb", [128, 128], BF16)
    ternTst = place("ternTst", [128, 4, 128], BF16)
    ternTstB = place("ternTstB", [128, 4, 128], BF16, at=_offs["gnwb"])

    WOFF = off[0]
    wqk = place("wqk", [128, 16, 1024], BF16, at=WOFF)
    wvb = place("wvb", [128, 8, 2048], BF16, at=WOFF)
    wgb = place("wgb", [128, 8, 2048], BF16, at=WOFF)
    wob = place("wob", [128, 16, 1024], BF16, at=WOFF)
    tgs = place("tgs", [128, 2, 8, 512], BF16, at=WOFF)
    tus = place("tus", [128, 2, 8, 512], BF16, at=WOFF + 16384)
    off[0] = WOFF + 32768

    AOFF = off[0]
    xld = place("xld", [128, 4, 1024], F32, at=AOFF)
    xldB = place("xldB", [128, 4, 1024], F32, at=AOFF + 24576)
    junkA = place("junkA", [128, 1024], F32, at=AOFF + 16384)
    htmp = place("htmp", [128, 1024], BF16, at=AOFF + 20480)
    U0 = place("U0", [128, 8, GRP], F32, at=AOFF)
    U1 = place("U1", [128, 8, GRP], F32, at=AOFF + 8192)
    U2 = place("U2", [128, 8, GRP], F32, at=AOFF + 16384)
    U3 = place("U3", [128, 8, GRP], F32, at=AOFF + 24576)
    ksfm = place("ksfm", [128, 8, GRP], BF16, at=AOFF + 32768)
    kstm = place("kstm", [128, 2, 1024], BF16, at=AOFF + 36864)
    vtmp = place("vtmp", [128, 2, 2048], BF16, at=AOFF)
    silu_t = place("silu_t", [128, 512], F32, at=AOFF + 8192)
    osb = place("osb", [128, 2048], F32, at=AOFF)
    asb = place("asb", [128, 1024], F32, at=AOFF + 2048)
    oft = place("oft", [128, 2048], BF16, at=AOFF + 8192)
    oTs = place("oTs", [128, 16, 128], BF16, at=AOFF + 12288)
    t2x2 = place("t2x2", [128, 1024], F32, at=AOFF + 16384)
    junkC = place("junkC", [128, 512], F32, at=AOFF + 20480)
    h2b = place("h2b", [128, 1024], F32, at=AOFF + 20480)
    rt = place("rt", [128, 1024], F32, at=AOFF + 24576)
    ri = place("ri", [128, 1024], I32, at=AOFF + 28672)
    ra = place("ra", [128, 1024], F32, at=AOFF + 28672)
    rf = place("rf", [128, 1024], F32, at=AOFF + 32768)
    xqt = place("xqt", [128, 1024], BF16, at=AOFF + 36864)
    wch = place("wch", [128, 2, 4096], F32, at=AOFF)
    ternb = place("ternb", [128, 4096], BF16, at=AOFF + 32768)
    silu_f1 = place("silu_f1", [128, 512], F32, at=AOFF + 24576)
    rt1p = [place("rt1a", [128, 1024], F32, at=AOFF),
            place("rt1b", [128, 1024], F32, at=AOFF + 4096)]
    rf1p = [place("rf1a", [128, 1024], F32, at=AOFF + 8192),
            place("rf1b", [128, 1024], F32, at=AOFF + 12288)]
    ri1q = place("ri1q", [128, 1024], I32, at=AOFF + 16384)
    ra1q = place("ra1q", [128, 1024], F32, at=AOFF + 20480)
    guq = place("guq", [128, 4096], BF16, at=AOFF + 26624)
    guqT = place("guqT", [128, 16, 128], BF16, at=AOFF + 34816)
    guqs = place("guqs", [128, 2, 32, 128], BF16, at=AOFF)
    x2t = place("x2t", [128, 2, 1024], F32, at=AOFF + 16384)
    fsb = place("fsb", [128, 1024], F32, at=AOFF + 24576)
    t3b = place("t3b", [128, 1024], F32, at=AOFF + 28672)
    off[0] = AOFF + 40960

    BOFF = off[0]
    qg = place("qg", [128, 8, 2048], BF16, at=BOFF)
    f0t = place("f0t", [128, 4096], F32, at=BOFF)
    f0bn = place("f0bn", [128, 4096], F32, at=BOFF + 16384)
    gu_lo = place("gu_lo", [128, 2, 4096], F32, at=BOFF)
    td_lo = place("td_lo", [128, 16, 1024], BF16, at=BOFF)
    off[0] = BOFF + 32768

    COFF = off[0]
    h_fm = place("h_fm", [128, 8, 2048], BF16, at=COFF)
    xq_fm = place("xq_fm", [128, 8, 2048], BF16, at=COFF)
    ysb = place("ysb", [128, 2, 1024], F32, at=COFF)
    off[0] = COFF + 32768

    DOFF = off[0]
    vtmp0 = place("vtmp0", [128, 2, 2048], BF16, at=DOFF)
    vs = place("vs", [128, 2, 2048], BF16, at=DOFF)
    kss = place("kss", [128, 2, 1024], BF16, at=DOFF + 8192)
    kis = place("kis", [128, 2, 8, 128], BF16, at=DOFF + 12288)
    gss = place("gss", [128, 2, 2048], BF16, at=DOFF + 16384)
    xcs = place("xcs", [128, 2, 1024], F32, at=DOFF + 24576)
    gu_hi = place("gu_hi", [128, 2, 4096], F32, at=DOFF)
    td_hi = place("td_hi", [128, 16, 1024], BF16, at=DOFF)
    kigrp = place("kigrp", [128, 8, GRP], BF16, at=DOFF)
    off[0] = DOFF + 32768

    assert off[0] <= 227328, f"sbuf overflow: {off[0]}"
    nc.sbuf_base = off[0]

    PS = [nc.alloc_psum_tensor(f"ps{i}", [128, 1024], F32) for i in range(4)]

    def psv(bank, lo=0, hi=512):
        return PS[bank // 2][:, 512 * (bank % 2) + lo: 512 * (bank % 2) + hi]

    bufs = {}

    def B(name):
        if name not in bufs:
            bufs[name] = Buf(name)
        return bufs[name]

    PB = [B(f"pb{i}") for i in range(8)]
    bW, bA, bB, bC = B("W"), B("A"), B("BB"), B("BC")
    bS = [B(f"S{d}") for d in range(8)]
    bST = B("stats")
    bCONST = B("const")
    bSC = B("scores")
    bKIG = B("kigrp")
    bT1 = B("t1")
    bTST = B("ternTst")
    bXQ = B("xq")
    bD2 = [B("D0"), B("D1")]
    bTG2 = [B("TG0"), B("TG1")]
    bF2 = [B("F20"), B("F21")]
    bOUT = B("out")
    bSP = {n: B("sp_" + n) for n in
           ("ki", "ks", "v", "gs", "x2", "tg", "tu", "td", "gq")}
    bSPGQ = [B("spgq%d" % _t) for _t in range(NT)]

    def dma(queue, out, in_, reads, writes):
        eng = QISSUER[queue]
        attr = {"Y": "sync", "S": "scalar", "G": "gpsimd"}[eng]

        def bld(nc, out=out, in_=in_, attr=attr):
            return getattr(nc, attr).dma_start(out=out, in_=in_)
        P.emit(eng, bld, reads=reads, writes=writes, queue=queue)

    def mm(out_ap, lhsT, rhs, start, stop, reads, writes):
        def bld(nc, o=out_ap, l=lhsT, r=rhs, st=start, sp=stop):
            return nc.tensor.matmul(o, l, r, start=st, stop=sp)
        P.emit("T", bld, reads=reads, writes=writes)

    _LUT_FUNCS = (AF.Sqrt, AF.Sigmoid, AF.Ln, AF.Exp, AF.Silu)

    def act(out_ap, in_ap, func, reads, writes, scale=None, bias=None,
            accum=None, fd=1024):
        def bld(nc, o=out_ap, i=in_ap, f=func, s=scale, b=bias, a=accum):
            kw = {}
            if s is not None:
                kw["scale"] = s
            if b is not None:
                kw["bias"] = b
            if a is not None:
                kw["accum_out"] = a
            if f in _LUT_FUNCS:
                nc.scalar.preload_act_table(f)
            return nc.scalar.activation(o, i, f, **kw)
        P.emit("S", bld, reads=reads, writes=writes, fd=fd)

    def scopy(out_ap, in_ap, reads, writes, fd=512):
        def bld(nc, o=out_ap, i=in_ap):
            return nc.scalar.copy(o, i)
        P.emit("S", bld, reads=reads, writes=writes, fd=fd)

    def vop(method, args, reads, writes, fd, eng="V"):
        def bld(nc, m=method, a=args, e=eng):
            eo = nc.vector if e == "V" else nc.gpsimd
            return getattr(eo, m)(*a)
        P.emit(eng, bld, reads=reads, writes=writes, fd=fd)

    def tap(label, ap, shape, dt=None, reads=()):
        if label not in _TAPS:
            return
        dt = dt or ap.dtype
        dbg = nc.dram_tensor("dbg_" + label, list(shape), dt,
                             kind="ExternalOutput").ap()
        dma("QO", dbg, ap, list(reads), [B("dbg_" + label)])

    def ln_finish(csum, cssq, cmean, crstd, cscr, n):
        vop("tensor_scalar_mul", (cst[:, cmean:cmean + 1], cst[:, csum:csum + 1],
            1.0 / n), [bST], [bST], fd=1)
        vop("tensor_mul", (cst[:, cscr:cscr + 1], cst[:, cmean:cmean + 1],
            cst[:, cmean:cmean + 1]), [bST], [bST], fd=1)
        vop("scalar_tensor_tensor",
            (cst[:, crstd:crstd + 1], cst[:, cssq:cssq + 1], 1.0 / n,
             cst[:, cscr:cscr + 1], ALU.mult, ALU.subtract), [bST], [bST], fd=1)
        vop("tensor_scalar_add", (cst[:, crstd:crstd + 1], cst[:, crstd:crstd + 1],
            1e-5), [bST], [bST], fd=1)
        act(cst[:, cscr:cscr + 1], cst[:, crstd:crstd + 1], AF.Sqrt,
            [bST], [bST], fd=1)
        vop("reciprocal", (cst[:, crstd:crstd + 1], cst[:, cscr:cscr + 1]),
            [bST], [bST], fd=1)

    dma("QS", cb[:, :], cbD, [], [bCONST])
    dma("QS", ident[:, :], identD, [], [bCONST])
    dma("QS", utri[:, :], utriD, [], [bCONST])
    dma("QS", onescol[:, :], onescolD, [], [bCONST])
    dma("QS", onesrow[:, :], onesrowD, [], [bCONST])
    for d in range(8):
        dma("QS", bgk_sb[:, d:d + 1], bgk2[128 * d:128 * (d + 1)].unsqueeze(1),
            [], [bCONST])
    dma("QS", gnw_sb[:, :], gnw.unsqueeze(0), [], [bCONST])
    dma("QG", wgk1b[:, :, :], Wgk1.rearrange("(a p) f -> p a f", p=128), [], [bCONST])
    dma("QS", allones[:, :], allonesD, [], [bCONST])
    dma("QG", wgk2b[:, :], Wgk2, [], [bCONST])
    mm(psv(4), onesrow[:, :], gnw_sb[:, :], True, True, [bCONST], [PB[4]])
    vop("tensor_copy", (gnwb[:, :], psv(4)), [PB[4], bCONST], [bCONST], fd=512)
    vop("memset", (S_sb[:, :, :], 0.0), [], list(bS), fd=4096)

    bXLD = [B("xldA"), B("xldB")]

    def emit_ln_h(xdram, tapit=False):
        for ch in range(4):
            pi = ch % 2
            xt = xld if pi == 0 else xldB
            dma("QS", xt[:, :, :],
                xdram[512 * ch:512 * (ch + 1), :].rearrange("(a p) f -> p a f", p=128),
                [], [bXLD[pi], bA] if ch < 2 else [bXLD[pi]])
            for tt in range(4):
                t = 4 * ch + tt
                vop("reduce_sum", (sums[:, t:t + 1], xt[:, tt, :], AX.X),
                    [bXLD[pi]], [bST], fd=1)
                act(junkA[:, :], xt[:, tt, :], AF.Square, [bXLD[pi]], [bA, bST],
                    accum=sumsq[:, t:t + 1])
        if tapit:
            tap("sumsq", sumsq[:, :], [128, 16], reads=[bST])
        vop("tensor_scalar_mul", (meanb[:, :], sums[:, :], 1.0 / D), [bST], [bST], fd=16)
        vop("tensor_mul", (nta[:, :], meanb[:, :], meanb[:, :]), [bST], [bST], fd=16)
        if tapit:
            tap("nta", nta[:, :], [128, 16], reads=[bST])
        vop("scalar_tensor_tensor",
            (aepsb[:, :], sumsq[:, :], 1.0 / D, nta[:, :], ALU.mult, ALU.subtract),
            [bST], [bST], fd=16)
        vop("tensor_scalar_add", (aepsb[:, :], aepsb[:, :], 1e-5), [bST], [bST], fd=16)
        act(nta[:, :], aepsb[:, :], AF.Sqrt, [bST], [bST], fd=16)
        vop("reciprocal", (rstdb[:, :], nta[:, :]), [bST], [bST], fd=16)
        vop("tensor_mul", (sums[:, :], rstdb[:, :], rstdb[:, :]), [bST], [bST], fd=16)
        vop("tensor_mul", (sums[:, :], sums[:, :], aepsb[:, :]), [bST], [bST], fd=16)
        vop("tensor_scalar", (sums[:, :], sums[:, :], -0.5, 1.5, ALU.mult, ALU.add),
            [bST], [bST], fd=16)
        vop("tensor_mul", (rstdb[:, :], rstdb[:, :], sums[:, :]), [bST], [bST], fd=16)
        if tapit:
            tap("aeps", aepsb[:, :], [128, 16], reads=[bST])
            tap("rstd", rstdb[:, :], [128, 16], reads=[bST])
            tap("meanb", meanb[:, :], [128, 16], reads=[bST])
        for ch in range(4):
            dma("QS", xld[:, :, :],
                xdram[512 * ch:512 * (ch + 1), :].rearrange("(a p) f -> p a f", p=128),
                [], [bA, bXLD[0], bXLD[1]] if ch == 0 else [bA])
            for tt in range(4):
                t = 4 * ch + tt
                vop("tensor_scalar",
                    (htmp[:, :], xld[:, tt, :], meanb[:, t:t + 1], rstdb[:, t:t + 1],
                     ALU.subtract, ALU.mult), [bA, bST], [bA], fd=1024)
                for f in range(8):
                    bank = 4 + (f // 4)
                    mm(psv(bank, 128 * (f % 4), 128 * (f % 4) + 128),
                       htmp[:, 128 * f:128 * (f + 1)], ident[:, :], True, True,
                       [bA, bCONST], [PB[bank]])
                for half in range(2):
                    bank = 4 + half
                    scopy(h_fm[:, 4 * half:4 * half + 4, 128 * t:128 * (t + 1)],
                          psv(bank).rearrange("p (a f) -> p a f", a=4),
                          [PB[bank]], [bC])

    def emit_gk(g, want_q):
        cols = slice(GRP * g, GRP * (g + 1))
        for it in range(NI):
            mm(PS[2][:, 0:GRP], wgk1b[:, it, :], h_fm[:, it, cols],
               it == 0, it == NI - 1, [bCONST, bC], [PB[4]])
        scopy(t1b[:, :], PS[2][0:16, 0:GRP], [PB[4]], [bT1], fd=GRP)
        for d in range(NI):
            bank = 2 + (d % 2)
            mm(psv(bank, 0, GRP), wgk2b[:, 128 * d:128 * (d + 1)], t1b[:, :],
               True, True, [bCONST, bT1], [PB[bank]])
            act(U0[:, d, :], psv(bank, 0, GRP), AF.Sigmoid, [PB[bank], bCONST],
                [bA], bias=bgk_sb[:, d:d + 1], fd=GRP)
        vop("tensor_scalar_max", (U0[:, :, :], U0[:, :, :], 1e-38), [bA], [bA],
            fd=8 * GRP)
        if g == 0 and want_q:
            tap("sg0", U0[:, :, :], [128, 8, GRP], reads=[bA])
        act(U1[:, :, :], U0[:, :, :], AF.Ln, [bA], [bA], fd=8 * GRP)
        for d in range(NI):
            for lc in range(GRP // 128):
                si = U1[:, d, 128 * lc:128 * (lc + 1)]
                so = U0[:, d, 128 * lc:128 * (lc + 1)]
                vop("tensor_tensor_scan", (so, si, si, 0.0, ALU.add, ALU.bypass),
                    [bA], [bA], fd=128)
        for lc in range(GRP // 128):
            vop("tensor_copy", (blastb[:, :, lc:lc + 1], U0[:, :, 128 * lc + 127:128 * lc + 128]),
                [bA], [bST], fd=8)
        if g == 0 and want_q:
            tap("ls0", U1[:, :, :], [128, 8, GRP], reads=[bA])
            tap("b0", U0[:, :, :], [128, 8, GRP], reads=[bA])
            tap("blast0", blastb[:, :, :], [128, 8, 2], reads=[bST])
        act(expbl[:, :, 2 * g:2 * g + 2], blastb[:, :, :], AF.Exp,
            [bST, bCONST], [bST], scale=cb[:, C_P16:C_P16 + 1], fd=16)
        if want_q:
            act(U2[:, :, :], U0[:, :, :], AF.Exp, [bA, bCONST], [bA],
                scale=cb[:, C_P16:C_P16 + 1], bias=cb[:, C_NLN16:C_NLN16 + 1],
                fd=8 * GRP)
            act(U3[:, :, :], U0[:, :, :], AF.Exp, [bA, bCONST], [bA],
                scale=cb[:, C_N16:C_N16 + 1], fd=8 * GRP)
        for d in range(NI):
            for lc in range(GRP // 128):
                vop("tensor_scalar",
                    (U1[:, d, 128 * lc:128 * (lc + 1)],
                     U0[:, d, 128 * lc:128 * (lc + 1)],
                     blastb[:, d, lc:lc + 1], None, ALU.subtract),
                    [bA, bST], [bA], fd=128)
        act(U0[:, :, :], U1[:, :, :], AF.Exp, [bA, bCONST], [bA],
            scale=cb[:, C_N16:C_N16 + 1], fd=8 * GRP)

    def emit_ks_transpose_group():
        for lc in range(2):
            for dh in range(2):
                bank = 4 + dh
                for dd in range(4):
                    d = 4 * dh + dd
                    mm(psv(bank, 128 * dd, 128 * dd + 128),
                       ksfm[:, d, 128 * lc:128 * (lc + 1)], ident[:, :], True, True,
                       [bA, bCONST], [PB[bank]])
                scopy(kstm[:, lc, 512 * dh:512 * (dh + 1)], psv(bank),
                      [PB[bank]], [bA])

    emit_ln_h(x_prev)
    wv_p0 = qg
    dma("QG", wqk[:, 0:8, :], Wk.rearrange("(a p) f -> p a f", p=128), [bW], [bW])
    dma("QG", wv_p0[:, :, :], Wv.rearrange("(a p) f -> p a f", p=128), [bB], [bB])
    for g in range(NG):
        cols = slice(GRP * g, GRP * (g + 1))
        emit_gk(g, want_q=False)
        for d in range(NI):
            bank = 0 + (d % 2)
            for it in range(NI):
                mm(psv(bank, 0, GRP), wqk[:, it, 128 * d:128 * (d + 1)],
                   h_fm[:, it, cols], it == 0, it == NI - 1, [bW, bC], [PB[bank]])
            vop("tensor_mul", (ksfm[:, d, :], psv(bank, 0, GRP), U0[:, d, :]),
                [PB[bank], bA], [bA], fd=GRP)
        emit_ks_transpose_group()
        for tt in range(2):
            for dvg in range(4):
                bank = 2 + (dvg % 2)
                for it in range(NI):
                    mm(psv(bank),
                       h_fm[:, it, GRP * g + 128 * tt: GRP * g + 128 * (tt + 1)],
                       wv_p0[:, it, 512 * dvg:512 * (dvg + 1)],
                       it == 0, it == NI - 1, [bB, bC], [PB[bank]])
                scopy(vtmp0[:, tt, 512 * dvg:512 * (dvg + 1)], psv(bank),
                      [PB[bank]], [bD2[0]])
        for lc in range(2):
            c = 2 * g + lc
            for hh in range(NH):
                for j in range(2):
                    dd = 2 * hh + j
                    bank = 6 + j
                    mm(psv(bank),
                       kstm[:, lc, 256 * hh + 128 * j: 256 * hh + 128 * (j + 1)],
                       vtmp0[:, lc, 512 * hh:512 * (hh + 1)], True, True,
                       [bA, bD2[0], bS[dd]], [PB[bank]])
                    vop("scalar_tensor_tensor",
                        (S_sb[:, dd, :], S_sb[:, dd, :], expbl[:, dd, c:c + 1],
                         psv(bank), ALU.mult, ALU.add),
                        [bS[dd], bST, PB[bank]], [bS[dd]], fd=512)

    emit_ln_h(x_own, tapit=True)
    tap("h", h_fm[:, :, :], [128, 8, 2048], reads=[bC])

    dma("QG", wqk[:, 0:8, :], Wq.rearrange("(a p) f -> p a f", p=128), [bW], [bW])
    dma("QG", wqk[:, 8:16, :], Wk.rearrange("(a p) f -> p a f", p=128), [bW], [bW])
    for g in range(NG):
        cols = slice(GRP * g, GRP * (g + 1))
        emit_gk(g, want_q=True)
        for d in range(NI):
            bank = 0 + (d % 2)
            for it in range(NI):
                mm(psv(bank, 0, GRP), wqk[:, it, 128 * d:128 * (d + 1)],
                   h_fm[:, it, cols], it == 0, it == NI - 1, [bW, bC], [PB[bank]])
            vop("tensor_mul", (qg[:, d, cols], psv(bank, 0, GRP), U2[:, d, :]),
                [PB[bank], bA], [bB], fd=GRP)
            bank2 = 2 + (d % 2)
            for it in range(NI):
                mm(psv(bank2, 0, GRP), wqk[:, 8 + it, 128 * d:128 * (d + 1)],
                   h_fm[:, it, cols], it == 0, it == NI - 1, [bW, bC], [PB[bank2]])
            vop("tensor_mul", (kigrp[:, d, :], psv(bank2, 0, GRP), U3[:, d, :]),
                [PB[bank2], bA], [bKIG, bD2[0]], fd=GRP)
            vop("tensor_mul", (ksfm[:, d, :], psv(bank2, 0, GRP), U0[:, d, :]),
                [PB[bank2], bA], [bA], fd=GRP)

        for lc in range(2):
            def ki_out(nc, g=g, lc=lc):
                return nc.scalar.dma_start(
                    out=SP_KI[2 * g + lc, :, :, :].rearrange("a p t -> p a t"),
                    in_=kigrp[:, :, 128 * lc:128 * (lc + 1)])
            P.emit("S", ki_out, reads=[bKIG, bD2[0]], writes=[bSP["ki"]], queue="QA")
        emit_ks_transpose_group()

        def ks_out(nc, g=g):
            return nc.scalar.dma_start(
                out=SP_KS[2 * g:2 * g + 2, :, :].rearrange("l p d -> p l d"),
                in_=kstm[:, :, :])
        P.emit("S", ks_out, reads=[bA], writes=[bSP["ks"]], queue="QA")

    tap("qg", qg[:, :, :], [128, 8, 2048], reads=[bB])
    tap("expbl", expbl[:, :, :], [128, 8, 16], reads=[bST])
    dma("QG", wvb[:, :, :], Wv.rearrange("(a p) f -> p a f", p=128), [bW], [bW])
    for g in range(NG):
        for tt in range(2):
            for dvg in range(4):
                bank = (dvg % 2)
                for it in range(NI):
                    mm(psv(bank),
                       h_fm[:, it, GRP * g + 128 * tt: GRP * g + 128 * (tt + 1)],
                       wvb[:, it, 512 * dvg:512 * (dvg + 1)],
                       it == 0, it == NI - 1, [bW, bC], [PB[bank]])
                scopy(vtmp[:, tt, 512 * dvg:512 * (dvg + 1)], psv(bank),
                      [PB[bank]], [bA])

        def v_out(nc, g=g):
            return nc.scalar.dma_start(
                out=SP_V[2 * g:2 * g + 2, :, :].rearrange("l p d -> p l d"),
                in_=vtmp[:, :, :])
        P.emit("S", v_out, reads=[bA], writes=[bSP["v"]], queue="QA")

    dma("QG", wgb[:, :, :], Wg.rearrange("(a p) f -> p a f", p=128), [bW], [bW])
    for g in range(NG):
        for tt in range(2):
            for hh in range(4):
                bank = (hh % 2)
                for it in range(NI):
                    mm(psv(bank),
                       h_fm[:, it, GRP * g + 128 * tt: GRP * g + 128 * (tt + 1)],
                       wgb[:, it, 512 * hh:512 * (hh + 1)],
                       it == 0, it == NI - 1, [bW, bC], [PB[bank]])
                act(silu_t[:, :], psv(bank), AF.Silu, [PB[bank]], [bA], fd=512)
                vop("tensor_mul", (vtmp[:, tt, 512 * hh:512 * (hh + 1)],
                                   silu_t[:, :], gnwb[:, :]),
                    [bA, bCONST], [bA], fd=512)

        def gs_out(nc, g=g):
            return nc.scalar.dma_start(
                out=SP_GS[2 * g:2 * g + 2, :, :].rearrange("l p d -> p l d"),
                in_=vtmp[:, :, :])
        P.emit("S", gs_out, reads=[bA], writes=[bSP["gs"]], queue="QA")

    dma("QG", wob[:, :, :], Wo.rearrange("(a p) f -> p a f", p=128), [bW], [bW])
    for dd in range(8):
        tap(f"Sprev{dd}", S_sb[:, dd, :], [128, 512], reads=[bS[dd]])
        scopy(S_bf[:, dd, :], S_sb[:, dd, :], [bS[dd]], [bS[dd]])

    def c_prefetch(c):
        i = c % 2
        dma("QS", vs[:, i, :], SP_V[c, :, :], [bSP["v"]], [bD2[i]])
        dma("QS", kss[:, i, :], SP_KS[c, :, :], [bSP["ks"]], [bD2[i]])
        dma("QS", kis[:, i, :, :], SP_KI[c, :, :, :].rearrange("a p t -> p a t"),
            [bSP["ki"]], [bD2[i]])
        dma("QS", gss[:, i, :], SP_GS[c, :, :], [bSP["gs"]], [bD2[i]])
        dma("QS", xcs[:, i, :], x_own[128 * c:128 * (c + 1), :], [], [bD2[i]])

    c_prefetch(0)
    for c in range(NCH):
        i = c % 2
        if c + 1 < NCH:
            c_prefetch(c + 1)
        ccols = slice(128 * c, 128 * (c + 1))
        for hh in range(NH):
            for j in range(2):
                dd = 2 * hh + j
                mm(PS[0][:, 0:128], kis[:, i, dd, :], qg[:, dd, ccols],
                   j == 0, j == 1, [bD2[i], bB], [PB[0]])
            vop("tensor_mul", (sc_sb[:, :], PS[0][:, 0:128], utri[:, :]),
                [PB[0], bCONST], [bSC], fd=128)
            mm(psv(1), sc_sb[:, :], vs[:, i, 512 * hh:512 * (hh + 1)],
               True, False, [bSC, bD2[i]], [PB[1]])
            for j in range(2):
                dd = 2 * hh + j
                mm(psv(1), qg[:, dd, ccols], S_bf[:, dd, :],
                   False, j == 1, [bB, bS[dd]], [PB[1]])
            vop("tensor_copy", (osb[:, 512 * hh:512 * (hh + 1)], psv(1)),
                [PB[1]], [bA], fd=512)
            for j in range(2):
                dd = 2 * hh + j
                bank = 2 + j
                mm(psv(bank),
                   kss[:, i, 256 * hh + 128 * j: 256 * hh + 128 * (j + 1)],
                   vs[:, i, 512 * hh:512 * (hh + 1)], True, True,
                   [bD2[i], bS[dd]], [PB[bank]])
                vop("scalar_tensor_tensor",
                    (S_sb[:, dd, :], S_sb[:, dd, :], expbl[:, dd, c:c + 1],
                     psv(bank), ALU.mult, ALU.add),
                    [bS[dd], bST, PB[bank]], [bS[dd]], fd=512)
                scopy(S_bf[:, dd, :], S_sb[:, dd, :], [bS[dd]], [bS[dd]])
            act(junkC[:, :], osb[:, 512 * hh:512 * (hh + 1)], AF.Square,
                [bA], [bA, bST], accum=ssq4[:, hh:hh + 1], fd=512)
        if c == 0:
            tap("osb0", osb[:, :], [128, 2048], reads=[bA])
            tap("ssq40", ssq4[:, :], [128, 4], reads=[bST])
        vop("tensor_scalar", (a4[:, :], ssq4[:, :], 1.0 / 512, 1e-5,
            ALU.mult, ALU.add), [bST], [bST], fd=4)
        act(sd4[:, :], a4[:, :], AF.Sqrt, [bST], [bST], fd=4)
        vop("reciprocal", (rs4[:, :], sd4[:, :]), [bST], [bST], fd=4)
        vop("tensor_mul", (n4[:, :], rs4[:, :], rs4[:, :]), [bST], [bST], fd=4)
        vop("tensor_mul", (n4[:, :], n4[:, :], a4[:, :]), [bST], [bST], fd=4)
        vop("tensor_scalar", (n4[:, :], n4[:, :], -0.5, 1.5, ALU.mult, ALU.add),
            [bST], [bST], fd=4)
        vop("tensor_mul", (rs4[:, :], rs4[:, :], n4[:, :]), [bST], [bST], fd=4)
        for hh in range(NH):
            vop("scalar_tensor_tensor",
                (oft[:, 512 * hh:512 * (hh + 1)], osb[:, 512 * hh:512 * (hh + 1)],
                 rs4[:, hh:hh + 1], gss[:, i, 512 * hh:512 * (hh + 1)],
                 ALU.mult, ALU.mult), [bA, bST, bD2[i]], [bA], fd=512)
        for b4 in range(4):
            bank = 4 + (b4 % 2)
            for dd in range(4):
                dvt = 4 * b4 + dd
                mm(psv(bank, 128 * dd, 128 * dd + 128),
                   oft[:, 128 * dvt:128 * (dvt + 1)], ident[:, :], True, True,
                   [bA, bCONST], [PB[bank]])
            scopy(oTs[:, 4 * b4:4 * b4 + 4, :],
                  psv(bank).rearrange("p (a f) -> p a f", a=4), [PB[bank]], [bA])
        for dh in range(2):
            for dvt in range(16):
                mm(PS[3][:, 512 * dh:512 * (dh + 1)], oTs[:, dvt, :],
                   wob[:, dvt, 512 * dh:512 * (dh + 1)],
                   dvt == 0, dvt == 15, [bA, bW], [PB[6 + dh]])
        vop("tensor_copy", (asb[:, :], PS[3][:, :]), [PB[6], PB[7]], [bA], fd=1024)
        if c == 0:
            tap("asb0", asb[:, :], [128, 1024], reads=[bA])
            tap("oft0", oft[:, :], [128, 2048], reads=[bA])
        vop("reduce_sum", (cst[:, 0:1], asb[:, :], AX.X), [bA], [bST], fd=1)
        act(junkC[:, :], asb[:, 0:512], AF.Square, [bA], [bA, bST],
            accum=cst[:, 1:2], fd=512)
        act(junkC[:, :], asb[:, 512:1024], AF.Square, [bA], [bA, bST],
            accum=cst[:, 2:3], fd=512)
        vop("tensor_add", (cst[:, 1:2], cst[:, 1:2], cst[:, 2:3]), [bST], [bST], fd=1)
        ln_finish(0, 1, 3, 4, 5, D)
        vop("tensor_mul", (cst[:, 5:6], cst[:, 3:4], cst[:, 4:5]), [bST], [bST], fd=1)
        vop("tensor_scalar_mul", (cst[:, 5:6], cst[:, 5:6], -1.0), [bST], [bST], fd=1)
        act(t2x2[:, :], asb[:, :], AF.Identity, [bA, bST], [bA],
            scale=cst[:, 4:5], bias=cst[:, 5:6])
        vop("tensor_add", (t2x2[:, :], t2x2[:, :], xcs[:, i, :]),
            [bA, bD2[i]], [bA], fd=1024)

        if c == 0:
            tap("x20", t2x2[:, :], [128, 1024], reads=[bA])
        def x2_out(nc, c=c):
            return nc.scalar.dma_start(out=SP_X2[c, :, :], in_=t2x2[:, :])
        P.emit("S", x2_out, reads=[bA], writes=[bSP["x2"]], queue="QA")
        vop("reduce_sum", (cst[:, 6:7], t2x2[:, :], AX.X), [bA], [bST], fd=1)
        act(junkC[:, :], t2x2[:, 0:512], AF.Square, [bA], [bA, bST],
            accum=cst[:, 7:8], fd=512)
        act(junkC[:, :], t2x2[:, 512:1024], AF.Square, [bA], [bA, bST],
            accum=cst[:, 8:9], fd=512)
        vop("tensor_add", (cst[:, 7:8], cst[:, 7:8], cst[:, 8:9]), [bST], [bST], fd=1)
        ln_finish(6, 7, 9, 10, 11, D)
        vop("tensor_scalar", (h2b[:, :], t2x2[:, :], cst[:, 9:10], cst[:, 10:11],
            ALU.subtract, ALU.mult), [bA, bST], [bA], fd=1024)

        if c == 0:
            tap("h20", h2b[:, :], [128, 1024], reads=[bA])
            tap("cst0", cst[:, 0:11], [128, 11], reads=[bST])

        def redmax(nc):
            return nc.vector.reduce_max(cst[:, 11:12], h2b[:, :], axis=AX.X,
                                        apply_absolute_value=True)
        P.emit("V", redmax, reads=[bA], writes=[bST], fd=1)
        vop("tensor_scalar_max", (cst[:, 11:12], cst[:, 11:12], 1e-5),
            [bST], [bST], fd=1)
        vop("tensor_scalar_mul", (sinv[:, c:c + 1], cst[:, 11:12], 1.0 / 127.0),
            [bST], [bST], fd=1)
        vop("reciprocal", (cst[:, 11:12], sinv[:, c:c + 1]), [bST], [bST], fd=1)
        act(rt[:, :], h2b[:, :], AF.Identity, [bA, bST], [bA], scale=cst[:, 11:12])
        act(ri[:, :], h2b[:, :], AF.Identity, [bA, bST], [bA],
            scale=cst[:, 11:12])
        act(rf[:, :], ri[:, :], AF.Identity, [bA], [bA])
        vop("tensor_sub", (rt[:, :], rt[:, :], rf[:, :]), [bA], [bA], fd=1024)
        vop("tensor_single_scalar", (ra[:, :], rt[:, :], 0.5, ALU.is_gt),
            [bA], [bA], fd=1024)
        for _k in (1.5, 2.5, 3.5):
            vop("scalar_tensor_tensor", (ra[:, :], rt[:, :], _k, ra[:, :],
                ALU.is_gt, ALU.add), [bA], [bA], fd=1024)
        vop("scalar_tensor_tensor", (ra[:, :], rt[:, :], -0.5, ra[:, :],
            ALU.is_lt, ALU.subtract), [bA], [bA], fd=1024)
        for _k in (-1.5, -2.5, -3.5):
            vop("scalar_tensor_tensor", (ra[:, :], rt[:, :], _k, ra[:, :],
                ALU.is_lt, ALU.add), [bA], [bA], fd=1024)
        vop("tensor_sub", (xqt[:, :], rf[:, :], ra[:, :]), [bA], [bA], fd=1024)
        if c == 0:
            tap("xqt0", xqt[:, :], [128, 1024], reads=[bA])
        for half in range(2):
            bank = 4 + half
            for dd in range(4):
                f = 4 * half + dd
                mm(psv(bank, 128 * dd, 128 * dd + 128),
                   xqt[:, 128 * f:128 * (f + 1)], ident[:, :], True, True,
                   [bA, bCONST], [PB[bank]])
            scopy(xq_fm[:, 4 * half:4 * half + 4, 128 * c:128 * (c + 1)],
                  psv(bank).rearrange("p (a f) -> p a f", a=4),
                  [PB[bank]], [bXQ, bC])

    bWC = [B("wc0"), B("wc1")]
    bF0T, bF0B, bTRN = B("f0tb"), B("f0bb"), B("trnb")
    bTST2 = [bTST, B("ternTstB")]
    ternTst2 = [ternTst, ternTstB]
    _f0first = set()

    def f0w(key, bufs, extra):
        if key in _f0first:
            return bufs
        _f0first.add(key)
        return bufs + extra

    def emit_wquant(wdram, O, IW, mcol, spill, spkey):
        rows = 524288 // IW
        nchunks = O // rows
        n_it = IW // 128
        n_ot = rows // 128
        for ch in range(nchunks):
            pi = ch % 2
            src = wdram[rows * ch: rows * (ch + 1), :]
            if n_ot > 1:
                src = src.rearrange("(a p) f -> p a f", p=128)
                dst = wch[:, pi, :].rearrange("p (a f) -> p a f", a=n_ot)
            else:
                dst = wch[:, pi, :]
            dma("QS" if pi == 0 else "QG", dst, src, [],
                f0w("wc%d" % pi, [bWC[pi]], [bA]))

            def red(nc, pi=pi, ch=ch):
                return nc.vector.reduce_sum(part4[:, ch:ch + 1], wch[:, pi, :],
                                            axis=AX.X, apply_absolute_value=True)
            P.emit("V", red, reads=[bWC[pi]], writes=[bST], fd=1)
        vop("reduce_sum", (tot1[:, :], part4[:, 0:nchunks], AX.X), [bST], [bST], fd=1)
        mm(PS[2][:, 0:1], allones[:, :], tot1[:, :], True, True,
           [bCONST, bST], [PB[4]])
        vop("tensor_copy", (mbc[:, mcol:mcol + 1], PS[2][:, 0:1]), [PB[4]], [bST], fd=1)
        vop("tensor_scalar_mul", (mbc[:, mcol:mcol + 1], mbc[:, mcol:mcol + 1],
            1.0 / (O * IW)), [bST], [bST], fd=1)
        vop("tensor_scalar_max", (mbc[:, mcol:mcol + 1], mbc[:, mcol:mcol + 1],
            1e-5), [bST], [bST], fd=1)
        vop("reciprocal", (mibc[:, mcol:mcol + 1], mbc[:, mcol:mcol + 1]),
            [bST], [bST], fd=1)
        for ch in range(nchunks):
            pi = ch % 2
            src = wdram[rows * ch: rows * (ch + 1), :]
            if n_ot > 1:
                src = src.rearrange("(a p) f -> p a f", p=128)
                dst = wch[:, pi, :].rearrange("p (a f) -> p a f", a=n_ot)
            else:
                dst = wch[:, pi, :]
            dma("QS" if pi == 0 else "QG", dst, src, [], [bWC[pi]])
            act(f0t[:, :], wch[:, pi, :], AF.Identity, [bWC[pi], bST],
                f0w("f0t", [bF0T], [bB]), scale=mibc[:, mcol:mcol + 1], fd=4096)
            vop("tensor_single_scalar", (f0bn[:, :], f0t[:, :], -0.5, ALU.is_lt),
                [bF0T], f0w("f0b", [bF0B], [bB]), fd=4096)
            vop("scalar_tensor_tensor", (ternb[:, :], f0t[:, :], 0.5, f0bn[:, :],
                ALU.is_gt, ALU.subtract), [bF0T, bF0B],
                f0w("trn", [bTRN], [bA]), fd=4096)
            for ot in range(n_ot):
                oglob = n_ot * ch + ot
                for itb in range(n_it // 4):
                    bank = 4 + (itb % 2)
                    ti2 = (ot * (n_it // 4) + itb) % 2
                    for dd in range(4):
                        it = 4 * itb + dd
                        mm(psv(bank, 128 * dd, 128 * dd + 128),
                           ternb[:, IW * ot + 128 * it: IW * ot + 128 * (it + 1)],
                           ident[:, :], True, True, [bTRN, bCONST], [PB[bank]])
                    if ti2 == 0:
                        scopy(ternTst2[0][:, :, :],
                              psv(bank).rearrange("p (a f) -> p a f", a=4),
                              [PB[bank]], f0w("tst0", [bTST2[0]], [bCONST, bA]))
                    else:
                        vop("tensor_copy",
                            (ternTst2[1][:, :, :],
                             psv(bank).rearrange("p (a f) -> p a f", a=4)),
                            [PB[bank]], f0w("tst1", [bTST2[1]], [bCONST, bA]),
                            fd=512)

                    def tout(nc, itb=itb, oglob=oglob, spill=spill, ti2=ti2):
                        return nc.scalar.dma_start(
                            out=spill[4 * itb:4 * itb + 4, :,
                                      128 * oglob:128 * (oglob + 1)]
                            .rearrange("a p t -> p a t"),
                            in_=ternTst2[ti2][:, :, :])
                    P.emit("S", tout, reads=[bTST2[ti2]], writes=[bSP[spkey]],
                           queue="QA")

    emit_wquant(wgate, INNER, D, 0, SP_TG, "tg")
    emit_wquant(wup, INNER, D, 1, SP_TU, "tu")
    emit_wquant(wdown, D, INNER, 2, SP_TD, "td")

    tap("mbc", mbc[:, 0:3], [128, 3], reads=[bST])
    tap("sinv", sinv[:, :], [128, 16], reads=[bST])
    vop("tensor_scalar_mul", (dqg[:, :], sinv[:, :], mbc[:, 0:1]), [bST], [bST], fd=16)
    vop("tensor_scalar_mul", (dqu[:, :], sinv[:, :], mbc[:, 1:2]), [bST], [bST], fd=16)

    bGU = [bB, B("GUHI")]
    bRT = [B("rt1a"), B("rt1b")]
    bRF = [B("rf1a"), B("rf1b")]
    bRI, bRA = B("ri1q"), B("ra1q")
    bGUQ, bGUT, bSIL = B("guq1"), B("guT1"), B("silF1")
    _first = set()

    def fwr(key, bufs, extra):
        if key in _first:
            return bufs
        _first.add(key)
        return bufs + extra

    def round_pieces(t):
        reg = (t // 2) % 2
        gu_t = gu_lo if reg == 0 else gu_hi
        gb = bGU[reg]
        sc = 10 + (t % 2)

        def p_rmax():
            def rmx(nc, gu_t=gu_t, ti=t % 2, sc=sc):
                return nc.vector.reduce_max(cst[:, sc:sc + 1], gu_t[:, ti, :],
                                            axis=AX.X, apply_absolute_value=True)
            P.emit("V", rmx, reads=[gb], writes=[bST], fd=1)
            vop("tensor_scalar_max", (cst[:, sc:sc + 1], cst[:, sc:sc + 1], 1e-5),
                [bST], [bST], fd=1)
            vop("tensor_scalar_mul", (s2inv[:, t:t + 1], cst[:, sc:sc + 1],
                1.0 / 127.0), [bST], [bST], fd=1)
            vop("reciprocal", (cst[:, sc:sc + 1], s2inv[:, t:t + 1]),
                [bST], [bST], fd=1)

        def p_acts(q):
            pi = q % 2
            qc = slice(1024 * q, 1024 * (q + 1))
            act(rt1p[pi][:, :], gu_t[:, t % 2, qc], AF.Identity,
                [gb, bST], fwr("rt%d" % pi, [bRT[pi]], [bA]),
                scale=cst[:, sc:sc + 1], bias=cb[:, 8:9], fd=1024)
            act(ri1q[:, :], gu_t[:, t % 2, qc], AF.Identity,
                [gb, bST], fwr("ri", [bRI], [bA]),
                scale=cst[:, sc:sc + 1], fd=1024)
            act(rf1p[pi][:, :], ri1q[:, :], AF.Identity,
                [bRI], fwr("rf%d" % pi, [bRF[pi]], [bA]), fd=1024)

        def p_ladder(q):
            pi = q % 2
            qc = slice(1024 * q, 1024 * (q + 1))
            vop("scalar_tensor_tensor", (ra1q[:, :], rf1p[pi][:, :], -1.0,
                rt1p[pi][:, :], ALU.mult, ALU.add),
                [bRF[pi], bRT[pi]], fwr("ra", [bRA], [bA]), fd=1024)
            for _k in (64.0, 32.0, 16.0, 8.0, 4.0, 2.0, 1.0):
                vop("tensor_single_scalar", (rf1p[pi][:, :], ra1q[:, :], _k,
                    ALU.is_ge), [bRA], [bRF[pi]], fd=1024)
                vop("scalar_tensor_tensor", (ra1q[:, :], rf1p[pi][:, :], -_k,
                    ra1q[:, :], ALU.mult, ALU.add), [bRF[pi]], [bRA], fd=1024)
            vop("scalar_tensor_tensor", (guq[:, qc], ra1q[:, :], -1.0,
                rt1p[pi][:, :], ALU.mult, ALU.add),
                [bRA, bRT[pi]], fwr("guq", [bGUQ], [bA]), fd=1024)
            vop("tensor_single_scalar", (guq[:, qc], guq[:, qc], -64.0, ALU.add),
                [bGUQ], [bGUQ], fd=1024)

        def p_trans():
            tap("guqt%d" % t, guq[:, :], [128, 4096], reads=[bGUQ])
            for half in range(2):
                for b4 in range(4 * half, 4 * half + 4):
                    bank = 4 + (b4 % 2)
                    for dd in range(4):
                        it = 4 * b4 + dd
                        mm(psv(bank, 128 * dd, 128 * dd + 128),
                           guq[:, 128 * it:128 * (it + 1)], ident[:, :],
                           True, True, [bGUQ, bCONST], [PB[bank]])
                    scopy(guqT[:, 4 * (b4 % 4):4 * (b4 % 4) + 4, :],
                          psv(bank).rearrange("p (a f) -> p a f", a=4),
                          [PB[bank]], fwr("guT", [bGUT], [bA]))

                def gq_out(nc, t=t, half=half):
                    return nc.scalar.dma_start(
                        out=SP_GQ[t, 16 * half:16 * (half + 1), :, :]
                        .rearrange("a p u -> p a u"),
                        in_=guqT[:, :, :])
                P.emit("S", gq_out, reads=[bGUT], writes=[bSPGQ[t]],
                       queue="QA")

        return [lambda: (p_rmax(), p_acts(0), p_acts(1)),
                lambda: (p_ladder(0), p_acts(2)),
                lambda: (p_ladder(1), p_acts(3)),
                lambda: p_ladder(2),
                lambda: p_ladder(3),
                p_trans]

    pieces = []
    DRAIN = [2, 2, 2, 2, 1, 1, 1, 1]
    for p in range(8):
        preg = p % 2
        gu_t = gu_lo if preg == 0 else gu_hi
        gkey = "gulo" if preg == 0 else "guhi"
        gext = [] if preg == 0 else [bD2[0], bD2[1]]
        for og in range(8):
            pi = og % 2
            wr_extra = [bW] if (p == 0 and og < 2) else []
            dma("QS", tgs[:, pi, :, :],
                SP_TG[:, :, 512 * og:512 * (og + 1)].rearrange("a p t -> p a t"),
                [bSP["tg"]], [bTG2[pi]] + wr_extra)
            dma("QG", tus[:, pi, :, :],
                SP_TU[:, :, 512 * og:512 * (og + 1)].rearrange("a p t -> p a t"),
                [bSP["tu"]], [bTG2[pi]] + wr_extra)
            for ti in range(2):
                t = 2 * p + ti
                gpb = psv(2 * ti)
                upb = psv(1 + 2 * ti)
                for it in range(NI):
                    mm(gpb, xq_fm[:, it, 128 * t:128 * (t + 1)],
                       tgs[:, pi, it, :], it == 0, it == NI - 1,
                       [bXQ, bTG2[pi]], [PB[2 * ti]])
                act(silu_f1[:, :], gpb, AF.Silu, [PB[2 * ti], bST],
                    fwr("sil", [bSIL], [bA]), scale=dqg[:, t:t + 1], fd=512)
                for it in range(NI):
                    mm(upb, xq_fm[:, it, 128 * t:128 * (t + 1)],
                       tus[:, pi, it, :], it == 0, it == NI - 1,
                       [bXQ, bTG2[pi]], [PB[1 + 2 * ti]])
                vop("scalar_tensor_tensor",
                    (gu_t[:, ti, 512 * og:512 * (og + 1)], upb,
                     dqu[:, t:t + 1], silu_f1[:, :], ALU.mult, ALU.mult),
                    [PB[1 + 2 * ti], bST, bSIL],
                    fwr(gkey, [bGU[preg]], gext), fd=512)
            for _ in range(DRAIN[og]):
                if pieces:
                    pieces.pop(0)()
        if p == 0:
            tap("guA", gu_lo[:, 0, :], [128, 4096], reads=[bGU[0]])
        pieces.extend(round_pieces(2 * p))
        pieces.extend(round_pieces(2 * p + 1))
    while pieces:
        pieces.pop(0)()

    vop("tensor_scalar_mul", (dqd[:, :], s2inv[:, :], mbc[:, 2:3]), [bST], [bST], fd=16)
    tap("dqdf", dqd[:, :], [128, 16], reads=[bST])
    tap("xqf", xq_fm[:, :, :], [128, 8, 2048], reads=[bXQ])
    tap("s2inv", s2inv[:, :], [128, 16], reads=[bST])
    tap("dqgf", dqg[:, :], [128, 16], reads=[bST])
    tap("dquf", dqu[:, :], [128, 16], reads=[bST])

    dma("QS", td_lo[:, :, :], SP_TD[0:16, :, :].rearrange("a p t -> p a t"),
        [bSP["td"]], [bB])
    dma("QS", td_hi[:, :, :], SP_TD[16:32, :, :].rearrange("a p t -> p a t"),
        [bSP["td"]], [bD2[0], bD2[1], bGU[1]])

    def f2_prefetch(t, first=False):
        i = t % 2
        extra = ([bA, bRT[0], bRT[1], bRF[0], bRF[1], bRI, bRA,
                  bGUQ, bGUT, bSIL] if first else [])
        dma("QS", guqs[:, i, :, :], SP_GQ[t, :, :, :].rearrange("a p u -> p a u"),
            [bSPGQ[t]], [bF2[i]] + extra)
        dma("QS", x2t[:, i, :], SP_X2[t, :, :], [bSP["x2"]], [bF2[i]])

    f2_prefetch(0, first=True)
    for t in range(NT):
        i = t % 2
        if t + 1 < NT:
            f2_prefetch(t + 1)
        for dh in range(2):
            bank = 2 * dh
            for it in range(32):
                tdt = td_lo if it < 16 else td_hi
                tdb = bB if it < 16 else B("D0")
                mm(psv(bank), guqs[:, i, it, :],
                   tdt[:, it % 16, 512 * dh:512 * (dh + 1)],
                   it == 0, it == 31, [bF2[i], tdb], [PB[bank]])
            act(fsb[:, 512 * dh:512 * (dh + 1)], psv(bank), AF.Copy,
                [PB[bank], bST], [bA], scale=dqd[:, t:t + 1], fd=512)
        if t == 0:
            tap("fsb0", fsb[:, :], [128, 1024], reads=[bA])
            tap("tdl0", td_lo[:, 0, :], [128, 1024], reads=[bB])
            tap("guqs0", guqs[:, 0, :, :], [128, 32, 128], reads=[bF2[0]])
            tap("x2t0", x2t[:, 0, :], [128, 1024], reads=[bF2[0]])
        vop("reduce_sum", (cst[:, 0:1], fsb[:, :], AX.X), [bA], [bST], fd=1)
        act(t3b[:, 0:512], fsb[:, 0:512], AF.Square, [bA], [bA, bST],
            accum=cst[:, 1:2], fd=512)
        act(t3b[:, 0:512], fsb[:, 512:1024], AF.Square, [bA], [bA, bST],
            accum=cst[:, 2:3], fd=512)
        vop("tensor_add", (cst[:, 1:2], cst[:, 1:2], cst[:, 2:3]), [bST], [bST], fd=1)
        ln_finish(0, 1, 3, 4, 5, D)
        vop("tensor_mul", (cst[:, 5:6], cst[:, 3:4], cst[:, 4:5]), [bST], [bST], fd=1)
        vop("tensor_scalar_mul", (cst[:, 5:6], cst[:, 5:6], -1.0), [bST], [bST], fd=1)
        act(t3b[:, :], fsb[:, :], AF.Identity, [bA, bST], [bA],
            scale=cst[:, 4:5], bias=cst[:, 5:6])
        vop("tensor_add", (ysb[:, i, :], t3b[:, :], x2t[:, i, :]),
            [bA, bF2[i]], fwr("ysb%d" % i, [B("ysb%d" % i)], [bXQ]), fd=1024)

        def y_out(nc, t=t, i=i):
            return nc.gpsimd.dma_start(out=yD[128 * t:128 * (t + 1), :],
                                       in_=ysb[:, i, :])
        P.emit("G", y_out, reads=[B("ysb%d" % i)], writes=[bOUT], queue="QO")

    P.finalize()
    sems = {o: nc.alloc_semaphore(name=f"sem_{o}") for o in OWNERS}

    def val(o2, k2):
        v = P.incno[o2][k2 - 1]
        return v * 16 if o2 in QUEUES else v

    with nc.Block() as blk:
        handles = {"T": blk.tensor, "V": blk.vector, "S": blk.scalar,
                   "G": blk.gpsimd, "Y": blk.sync}
        for e in ENGS:
            if not P.estream[e] and e != "Y":
                continue

            def section(eobj, e=e):
                for (owner, k) in P.estream[e]:
                    op = P.ops[owner][k - 1]
                    for (o2, k2) in op.waits:
                        eobj.wait_ge(sems[o2], val(o2, k2))
                    ins = op.build(nc)
                    if op.marked:
                        ins.then_inc(sems[owner], 16 if owner in QUEUES else 1)
                if e == "Y":
                    for q in QUEUES:
                        tot = P.totals[q]
                        if tot:
                            eobj.wait_ge(sems[q], tot * 16)
            handles[e](section)
    return nc


_NC_CACHE = []


def kernel(**inputs):
    x = np.ascontiguousarray(inputs["L0_attn_hidden_states"], dtype=np.float32)
    Bsz, L, Dm = x.shape
    if not _NC_CACHE:
        _NC_CACHE.append(build_program())
    nc = _NC_CACHE[0]

    cbv = np.zeros((128, 16), np.float32)
    cbv[:, C_HALF] = 0.5
    cbv[:, C_NHALF] = -0.5
    cbv[:, C_P16] = 1.0 / 16
    cbv[:, C_N16] = -1.0 / 16
    cbv[:, C_NLN16] = -LN16
    cbv[:, C_TINY] = 1e-38
    cbv[:, C_ONE] = 1.0
    cbv[:, C_EPS] = 1e-5
    cbv[:, 8] = 64.5
    identv = bf16(np.eye(128, dtype=np.float32))
    utriv = bf16(np.tril(np.ones((128, 128), np.float32)).T)
    wgk1pad = np.zeros((D, 128), np.float32)
    wgk1pad[:, 0:16] = np.asarray(inputs["L0_attn_Wgk1"], np.float32)
    common = {
        "cb": cbv, "ident": identv, "utri": utriv,
        "onescol": np.ones((128, 1), np.float32),
        "onesrow": np.ones((1, 128), np.float32),
        "allones": np.ones((128, 128), np.float32),
        "wgk1pad": wgk1pad,
    }
    for nm in ("L0_attn_Wq", "L0_attn_Wk", "L0_attn_Wv", "L0_attn_Wg",
               "L0_attn_Wgk2", "L0_attn_bgk2",
               "L0_attn_g_norm_w", "L0_attn_Wo", "L0_ffn_w_gate",
               "L0_ffn_w_up", "L0_ffn_w_down"):
        common[nm] = np.ascontiguousarray(inputs[nm], np.float32)

    zeros = np.zeros((T, Dm), np.float32)
    in_maps = []
    for c in range(8):
        b, h = c // 2, c % 2
        m = dict(common)
        m["x_own"] = np.ascontiguousarray(x[b, h * T:(h + 1) * T, :])
        m["x_prev"] = np.ascontiguousarray(x[b, 0:T, :]) if h == 1 else zeros
        in_maps.append(m)

    res = run_bass_kernel_spmd(nc, in_maps, core_ids=list(range(8)))
    out = np.empty((Bsz, L, Dm), np.float32)
    for c in range(8):
        b, h = c // 2, c % 2
        out[b, h * T:(h + 1) * T, :] = res.results[c]["y"]
    return out


# version 18
# speedup vs baseline: 1.0311x; 1.0113x over previous
import numpy as np
import ml_dtypes
import concourse.bass as bass
import concourse.mybir as mybir
from concourse.bass_utils import run_bass_kernel_spmd

F32 = mybir.dt.float32
BF16 = mybir.dt.bfloat16
I32 = mybir.dt.int32
AF = mybir.ActivationFunctionType
ALU = mybir.AluOpType
AX = mybir.AxisListType

T = 2048
NT = 16
NCH = 16
D = 1024
NI = 8
DK = 1024
DV = 2048
NH = 4
INNER = 4096
GRP = 256
NG = T // GRP
LN16 = float(np.log(16.0))

C_HALF, C_NHALF, C_P16, C_N16, C_NLN16, C_TINY, C_ONE, C_EPS = range(8)

_TAPS = {}

ENGS = ("T", "V", "S", "G", "Y")
QUEUES = ("QS", "QA", "QG", "QO")
OWNERS = ENGS + QUEUES
QISSUER = {"QS": "Y", "QA": "S", "QG": "G", "QO": "G"}


class Buf:
    __slots__ = ("name", "writer", "readers")

    def __init__(self, name):
        self.name = name
        self.writer = None
        self.readers = []


class Op:
    __slots__ = ("build", "waits", "marked", "inc", "fd")

    def __init__(self, build, fd):
        self.build = build
        self.fd = fd
        self.waits = []
        self.marked = False
        self.inc = 0


class Plan:
    def __init__(self):
        self.ops = {o: [] for o in OWNERS}
        self.estream = {e: [] for e in ENGS}
        self.synced = {e: {} for e in ENGS}
        self.snaps = {o: [] for o in OWNERS}

    def emit(self, eng, build, reads=(), writes=(), queue=None, fd=1024):
        owner = queue if queue else eng
        k = len(self.ops[owner]) + 1
        op = Op(build, fd)

        deps = set()
        for b in reads:
            w = b.writer
            if w is None:
                continue
            o2, k2 = w
            if o2 == eng and queue is None:
                if eng in ("V", "G") and self.ops[o2][k2 - 1].fd < 512:
                    deps.add(w)
                continue
            deps.add(w)
        for b in writes:
            w = b.writer
            if w is not None:
                o2, k2 = w
                if not (o2 == eng and queue is None):
                    deps.add(w)
                elif eng in ("V", "G") and self.ops[o2][k2 - 1].fd < 512:
                    deps.add(w)
            for r in b.readers:
                o2, k2 = r
                if o2 == eng and queue is None:
                    continue
                deps.add(r)

        syn = self.synced[eng]
        final = []
        for (o2, k2) in sorted(deps):
            if syn.get(o2, 0) >= k2:
                continue
            final.append((o2, k2))
            if syn.get(o2, 0) < k2:
                syn[o2] = k2
            for o3, k3 in self.snaps[o2][k2 - 1].items():
                if syn.get(o3, 0) < k3:
                    syn[o3] = k3
            self.ops[o2][k2 - 1].marked = True

        op.waits = final
        self.ops[owner].append(op)
        self.estream[eng].append((owner, k))
        snap = dict(syn)
        snap[owner] = k
        self.snaps[owner].append(snap)

        for b in writes:
            b.writer = (owner, k)
            b.readers = []
        for b in reads:
            b.readers.append((owner, k))
        return op

    def finalize(self):
        incno = {}
        for o in OWNERS:
            cnt = 0
            lst = []
            for op in self.ops[o]:
                if o in QUEUES:
                    op.marked = True
                if op.marked:
                    cnt += 1
                op.inc = cnt
                lst.append(cnt)
            incno[o] = lst
        self.incno = incno
        self.totals = {o: (incno[o][-1] if incno[o] else 0) for o in OWNERS}


def bf16(x):
    return np.asarray(x, dtype=ml_dtypes.bfloat16)


def build_program():
    nc = bass.Bass("TRN2", target_bir_lowering=False,
                   detect_race_conditions=False)
    P = Plan()

    def din(name, shape, dt=F32):
        return nc.dram_tensor(name, list(shape), dt, kind="ExternalInput").ap()

    x_own = din("x_own", [T, D])
    x_prev = din("x_prev", [T, D])
    Wq = din("L0_attn_Wq", [D, DK])
    Wk = din("L0_attn_Wk", [D, DK])
    Wv = din("L0_attn_Wv", [D, DV])
    Wg = din("L0_attn_Wg", [D, DV])
    Wgk1 = din("wgk1pad", [D, 128])
    Wgk2 = din("L0_attn_Wgk2", [16, DK])
    bgk2 = din("L0_attn_bgk2", [DK])
    gnw = din("L0_attn_g_norm_w", [512])
    Wo = din("L0_attn_Wo", [DV, D])
    wgate = din("L0_ffn_w_gate", [INNER, D])
    wup = din("L0_ffn_w_up", [INNER, D])
    wdown = din("L0_ffn_w_down", [D, INNER])
    cbD = din("cb", [128, 16])
    identD = din("ident", [128, 128], BF16)
    utriD = din("utri", [128, 128], BF16)
    onescolD = din("onescol", [128, 1])
    onesrowD = din("onesrow", [1, 128])
    allonesD = din("allones", [128, 128])
    yD = nc.dram_tensor("y", [T, D], F32, kind="ExternalOutput").ap()

    spk = "ExternalOutput" if "_spills" in _TAPS else "Internal"
    SP_KI = nc.dram_tensor("sp_ki", [NCH, NI, 128, 128], BF16, kind=spk).ap()
    SP_KS = nc.dram_tensor("sp_ks", [NCH, 128, DK], BF16, kind=spk).ap()
    SP_V = nc.dram_tensor("sp_v", [NCH, 128, DV], BF16, kind=spk).ap()
    SP_GS = nc.dram_tensor("sp_gs", [NCH, 128, DV], BF16, kind=spk).ap()
    SP_X2 = nc.dram_tensor("sp_x2", [NCH, 128, D], F32, kind=spk).ap()
    SP_TG = nc.dram_tensor("sp_tg", [NI, 128, INNER], BF16, kind=spk).ap()
    SP_TU = nc.dram_tensor("sp_tu", [NI, 128, INNER], BF16, kind=spk).ap()
    SP_TD = nc.dram_tensor("sp_td", [32, 128, D], BF16, kind=spk).ap()
    SP_GQ = nc.dram_tensor("sp_gq", [NT, 32, 128, 128], BF16, kind=spk).ap()

    BASE = (nc.sbuf_base + 512 + 31) & ~31
    off = [BASE]

    _offs = {}

    def place(name, shape, dt, at=None):
        nbytes = int(np.prod(shape[1:])) * mybir.dt.size(dt)
        if at is None:
            at = off[0]
            off[0] = (at + nbytes + 31) & ~31
        _offs[name] = at
        return nc.alloc_sbuf_tensor_at(name, list(shape), dt, offset=at)

    cb = place("cb", [128, 16], F32)
    bgk_sb = place("bgk", [128, 8], F32)
    gnwb = place("gnwb", [128, 512], F32)
    ident = place("ident", [128, 128], BF16)
    utri = place("utri", [128, 128], BF16)
    onescol = place("onescol", [128, 1], F32)
    onesrow = place("onesrow", [1, 128], F32)
    wgk1b = place("wgk1b", [128, 8, 128], BF16)
    allones = place("allones", [128, 128], F32)
    wgk2b = place("wgk2b", [16, DK], BF16)
    t1b = place("t1b", [16, GRP], BF16)
    gnw_sb = place("gnw_sb", [1, 512], F32)
    S_sb = place("S", [128, 8, 512], F32)
    S_bf = place("S_bf", [128, 8, 512], BF16)
    expbl = place("expbl", [128, 8, 16], F32)
    blastb = place("blastb", [128, 8, 2], F32)
    sums = place("sums", [128, 16], F32)
    sumsq = place("sumsq", [128, 16], F32)
    meanb = place("meanb", [128, 16], F32)
    aepsb = place("aepsb", [128, 16], F32)
    rstdb = place("rstdb", [128, 16], F32)
    nta = place("nta", [128, 16], F32)
    sinv = place("sinv", [128, 16], F32)
    s2inv = place("s2inv", [128, 16], F32)
    dqg = place("dqg", [128, 16], F32)
    dqu = place("dqu", [128, 16], F32)
    dqd = place("dqd", [128, 16], F32)
    mbc = place("mbc", [128, 4], F32)
    mibc = place("mibc", [128, 4], F32)
    part4 = place("part4", [128, 16], F32)
    tot1 = place("tot1", [128, 1], F32)
    t11 = place("t11", [1, 4], F32)
    ssq4 = place("ssq4", [128, 4], F32)
    a4 = place("a4", [128, 4], F32)
    sd4 = place("sd4", [128, 4], F32)
    rs4 = place("rs4", [128, 4], F32)
    n4 = place("n4", [128, 4], F32)
    cst = place("cst", [128, 12], F32)
    sc_sb = place("sc_sb", [128, 128], BF16)
    ternTst = place("ternTst", [128, 4, 128], BF16)
    ternTstB = place("ternTstB", [128, 4, 128], BF16, at=_offs["gnwb"])

    WOFF = off[0]
    wqk = place("wqk", [128, 16, 1024], BF16, at=WOFF)
    wvb = place("wvb", [128, 8, 2048], BF16, at=WOFF)
    wgb = place("wgb", [128, 8, 2048], BF16, at=WOFF)
    wob = place("wob", [128, 16, 1024], BF16, at=WOFF)
    tgs = place("tgs", [128, 2, 8, 512], BF16, at=WOFF)
    tus = place("tus", [128, 2, 8, 512], BF16, at=WOFF + 16384)
    off[0] = WOFF + 32768

    AOFF = off[0]
    xld = place("xld", [128, 4, 1024], F32, at=AOFF)
    xldB = place("xldB", [128, 4, 1024], F32, at=AOFF + 24576)
    htmpB = place("htmpB", [128, 1024], BF16, at=AOFF + 22528)
    junkA = place("junkA", [128, 1024], F32, at=AOFF + 16384)
    htmp = place("htmp", [128, 1024], BF16, at=AOFF + 20480)
    U0 = place("U0", [128, 8, GRP], F32, at=AOFF)
    U1 = place("U1", [128, 8, GRP], F32, at=AOFF + 8192)
    U2 = place("U2", [128, 8, GRP], F32, at=AOFF + 16384)
    U3 = place("U3", [128, 8, GRP], F32, at=AOFF + 24576)
    ksfm = place("ksfm", [128, 8, GRP], BF16, at=AOFF + 32768)
    kstm = place("kstm", [128, 2, 1024], BF16, at=AOFF + 36864)
    vtmp = place("vtmp", [128, 2, 2048], BF16, at=AOFF)
    silu_t = place("silu_t", [128, 512], F32, at=AOFF + 8192)
    osb = place("osb", [128, 2048], F32, at=AOFF)
    asb = place("asb", [128, 1024], F32, at=AOFF + 2048)
    oft = place("oft", [128, 2048], BF16, at=AOFF + 8192)
    oTs = place("oTs", [128, 16, 128], BF16, at=AOFF + 12288)
    t2x2 = place("t2x2", [128, 1024], F32, at=AOFF + 16384)
    junkC = place("junkC", [128, 512], F32, at=AOFF + 20480)
    h2b = place("h2b", [128, 1024], F32, at=AOFF + 20480)
    rt = place("rt", [128, 1024], F32, at=AOFF + 24576)
    ri = place("ri", [128, 1024], I32, at=AOFF + 28672)
    ra = place("ra", [128, 1024], F32, at=AOFF + 28672)
    rf = place("rf", [128, 1024], F32, at=AOFF + 32768)
    xqt = place("xqt", [128, 1024], BF16, at=AOFF + 36864)
    wch = place("wch", [128, 2, 4096], F32, at=AOFF)
    ternb = place("ternb", [128, 4096], BF16, at=AOFF + 32768)
    silu_f1 = place("silu_f1", [128, 512], F32, at=AOFF + 24576)
    rt1p = [place("rt1a", [128, 1024], F32, at=AOFF),
            place("rt1b", [128, 1024], F32, at=AOFF + 4096)]
    rf1p = [place("rf1a", [128, 1024], F32, at=AOFF + 8192),
            place("rf1b", [128, 1024], F32, at=AOFF + 12288)]
    ri1q = place("ri1q", [128, 1024], I32, at=AOFF + 16384)
    ra1q = place("ra1q", [128, 1024], F32, at=AOFF + 20480)
    guq = place("guq", [128, 4096], BF16, at=AOFF + 26624)
    guqT = place("guqT", [128, 16, 128], BF16, at=AOFF + 34816)
    guqs = place("guqs", [128, 2, 32, 128], BF16, at=AOFF)
    x2t = place("x2t", [128, 2, 1024], F32, at=AOFF + 16384)
    fsb = place("fsb", [128, 1024], F32, at=AOFF + 24576)
    t3b = place("t3b", [128, 1024], F32, at=AOFF + 28672)
    off[0] = AOFF + 40960

    BOFF = off[0]
    qg = place("qg", [128, 8, 2048], BF16, at=BOFF)
    f0t = place("f0t", [128, 4096], F32, at=BOFF)
    f0bn = place("f0bn", [128, 4096], F32, at=BOFF + 16384)
    gu_lo = place("gu_lo", [128, 2, 4096], F32, at=BOFF)
    td_lo = place("td_lo", [128, 16, 1024], BF16, at=BOFF)
    off[0] = BOFF + 32768

    COFF = off[0]
    h_fm = place("h_fm", [128, 8, 2048], BF16, at=COFF)
    xq_fm = place("xq_fm", [128, 8, 2048], BF16, at=COFF)
    ysb = place("ysb", [128, 2, 1024], F32, at=COFF)
    off[0] = COFF + 32768

    DOFF = off[0]
    vtmp0 = place("vtmp0", [128, 2, 2048], BF16, at=DOFF)
    vs = place("vs", [128, 2, 2048], BF16, at=DOFF)
    kss = place("kss", [128, 2, 1024], BF16, at=DOFF + 8192)
    kis = place("kis", [128, 2, 8, 128], BF16, at=DOFF + 12288)
    gss = place("gss", [128, 2, 2048], BF16, at=DOFF + 16384)
    xcs = place("xcs", [128, 2, 1024], F32, at=DOFF + 24576)
    gu_hi = place("gu_hi", [128, 2, 4096], F32, at=DOFF)
    td_hi = place("td_hi", [128, 16, 1024], BF16, at=DOFF)
    kigrp = place("kigrp", [128, 8, GRP], BF16, at=DOFF)
    off[0] = DOFF + 32768

    assert off[0] <= 227328, f"sbuf overflow: {off[0]}"
    nc.sbuf_base = off[0]

    PS = [nc.alloc_psum_tensor(f"ps{i}", [128, 1024], F32) for i in range(4)]

    def psv(bank, lo=0, hi=512):
        return PS[bank // 2][:, 512 * (bank % 2) + lo: 512 * (bank % 2) + hi]

    bufs = {}

    def B(name):
        if name not in bufs:
            bufs[name] = Buf(name)
        return bufs[name]

    PB = [B(f"pb{i}") for i in range(8)]
    bW, bA, bB, bC = B("W"), B("A"), B("BB"), B("BC")
    bS = [B(f"S{d}") for d in range(8)]
    bST = B("stats")
    bCONST = B("const")
    bSC = B("scores")
    bKIG = B("kigrp")
    bT1 = B("t1")
    bTST = B("ternTst")
    bXQ = B("xq")
    bD2 = [B("D0"), B("D1")]
    bTG2 = [B("TG0"), B("TG1")]
    bF2 = [B("F20"), B("F21")]
    bOUT = B("out")
    bSP = {n: B("sp_" + n) for n in
           ("ki", "ks", "v", "gs", "x2", "tg", "tu", "td", "gq")}
    bSPGQ = [B("spgq%d" % _t) for _t in range(NT)]

    def dma(queue, out, in_, reads, writes):
        eng = QISSUER[queue]
        attr = {"Y": "sync", "S": "scalar", "G": "gpsimd"}[eng]

        def bld(nc, out=out, in_=in_, attr=attr):
            return getattr(nc, attr).dma_start(out=out, in_=in_)
        P.emit(eng, bld, reads=reads, writes=writes, queue=queue)

    def mm(out_ap, lhsT, rhs, start, stop, reads, writes):
        def bld(nc, o=out_ap, l=lhsT, r=rhs, st=start, sp=stop):
            return nc.tensor.matmul(o, l, r, start=st, stop=sp)
        P.emit("T", bld, reads=reads, writes=writes)

    _LUT_FUNCS = (AF.Sqrt, AF.Sigmoid, AF.Ln, AF.Exp, AF.Silu)

    def act(out_ap, in_ap, func, reads, writes, scale=None, bias=None,
            accum=None, fd=1024):
        def bld(nc, o=out_ap, i=in_ap, f=func, s=scale, b=bias, a=accum):
            kw = {}
            if s is not None:
                kw["scale"] = s
            if b is not None:
                kw["bias"] = b
            if a is not None:
                kw["accum_out"] = a
            if f in _LUT_FUNCS:
                nc.scalar.preload_act_table(f)
            return nc.scalar.activation(o, i, f, **kw)
        P.emit("S", bld, reads=reads, writes=writes, fd=fd)

    def scopy(out_ap, in_ap, reads, writes, fd=512):
        def bld(nc, o=out_ap, i=in_ap):
            return nc.scalar.copy(o, i)
        P.emit("S", bld, reads=reads, writes=writes, fd=fd)

    def vop(method, args, reads, writes, fd, eng="V"):
        def bld(nc, m=method, a=args, e=eng):
            eo = nc.vector if e == "V" else nc.gpsimd
            return getattr(eo, m)(*a)
        P.emit(eng, bld, reads=reads, writes=writes, fd=fd)

    def tap(label, ap, shape, dt=None, reads=()):
        if label not in _TAPS:
            return
        dt = dt or ap.dtype
        dbg = nc.dram_tensor("dbg_" + label, list(shape), dt,
                             kind="ExternalOutput").ap()
        dma("QO", dbg, ap, list(reads), [B("dbg_" + label)])

    def ln_finish(csum, cssq, cmean, crstd, cscr, n):
        vop("tensor_scalar_mul", (cst[:, cmean:cmean + 1], cst[:, csum:csum + 1],
            1.0 / n), [bST], [bST], fd=1)
        vop("tensor_mul", (cst[:, cscr:cscr + 1], cst[:, cmean:cmean + 1],
            cst[:, cmean:cmean + 1]), [bST], [bST], fd=1)
        vop("scalar_tensor_tensor",
            (cst[:, crstd:crstd + 1], cst[:, cssq:cssq + 1], 1.0 / n,
             cst[:, cscr:cscr + 1], ALU.mult, ALU.subtract), [bST], [bST], fd=1)
        vop("tensor_scalar_add", (cst[:, crstd:crstd + 1], cst[:, crstd:crstd + 1],
            1e-5), [bST], [bST], fd=1)
        act(cst[:, cscr:cscr + 1], cst[:, crstd:crstd + 1], AF.Sqrt,
            [bST], [bST], fd=1)
        vop("reciprocal", (cst[:, crstd:crstd + 1], cst[:, cscr:cscr + 1]),
            [bST], [bST], fd=1)

    dma("QS", cb[:, :], cbD, [], [bCONST])
    dma("QS", ident[:, :], identD, [], [bCONST])
    dma("QS", utri[:, :], utriD, [], [bCONST])
    dma("QS", onescol[:, :], onescolD, [], [bCONST])
    dma("QS", onesrow[:, :], onesrowD, [], [bCONST])
    for d in range(8):
        dma("QS", bgk_sb[:, d:d + 1], bgk2[128 * d:128 * (d + 1)].unsqueeze(1),
            [], [bCONST])
    dma("QS", gnw_sb[:, :], gnw.unsqueeze(0), [], [bCONST])
    dma("QG", wgk1b[:, :, :], Wgk1.rearrange("(a p) f -> p a f", p=128), [], [bCONST])
    dma("QS", allones[:, :], allonesD, [], [bCONST])
    dma("QG", wgk2b[:, :], Wgk2, [], [bCONST])
    mm(psv(4), onesrow[:, :], gnw_sb[:, :], True, True, [bCONST], [PB[4]])
    vop("tensor_copy", (gnwb[:, :], psv(4)), [PB[4], bCONST], [bCONST], fd=512)
    vop("memset", (S_sb[:, :, :], 0.0), [], list(bS), fd=4096)

    bXLD = [B("xldA"), B("xldB")]
    bHT = [B("htA"), B("htB")]

    def emit_ln_h(xdram, tapit=False):
        for ch in range(4):
            pi = ch % 2
            xt = xld if pi == 0 else xldB
            dma("QS", xt[:, :, :],
                xdram[512 * ch:512 * (ch + 1), :].rearrange("(a p) f -> p a f", p=128),
                [], [bXLD[pi], bA] if ch < 2 else [bXLD[pi]])
            for tt in range(4):
                t = 4 * ch + tt
                vop("reduce_sum", (sums[:, t:t + 1], xt[:, tt, :], AX.X),
                    [bXLD[pi]], [bST], fd=1)
                act(junkA[:, :], xt[:, tt, :], AF.Square, [bXLD[pi]], [bA, bST],
                    accum=sumsq[:, t:t + 1])
        if tapit:
            tap("sumsq", sumsq[:, :], [128, 16], reads=[bST])
        vop("tensor_scalar_mul", (meanb[:, :], sums[:, :], 1.0 / D), [bST], [bST], fd=16)
        vop("tensor_mul", (nta[:, :], meanb[:, :], meanb[:, :]), [bST], [bST], fd=16)
        if tapit:
            tap("nta", nta[:, :], [128, 16], reads=[bST])
        vop("scalar_tensor_tensor",
            (aepsb[:, :], sumsq[:, :], 1.0 / D, nta[:, :], ALU.mult, ALU.subtract),
            [bST], [bST], fd=16)
        vop("tensor_scalar_add", (aepsb[:, :], aepsb[:, :], 1e-5), [bST], [bST], fd=16)
        act(nta[:, :], aepsb[:, :], AF.Sqrt, [bST], [bST], fd=16)
        vop("reciprocal", (rstdb[:, :], nta[:, :]), [bST], [bST], fd=16)
        vop("tensor_mul", (sums[:, :], rstdb[:, :], rstdb[:, :]), [bST], [bST], fd=16)
        vop("tensor_mul", (sums[:, :], sums[:, :], aepsb[:, :]), [bST], [bST], fd=16)
        vop("tensor_scalar", (sums[:, :], sums[:, :], -0.5, 1.5, ALU.mult, ALU.add),
            [bST], [bST], fd=16)
        vop("tensor_mul", (rstdb[:, :], rstdb[:, :], sums[:, :]), [bST], [bST], fd=16)
        if tapit:
            tap("aeps", aepsb[:, :], [128, 16], reads=[bST])
            tap("rstd", rstdb[:, :], [128, 16], reads=[bST])
            tap("meanb", meanb[:, :], [128, 16], reads=[bST])
        for ch in range(4):
            pi = ch % 2
            xt = xld if pi == 0 else xldB
            dma("QS", xt[:, :, :],
                xdram[512 * ch:512 * (ch + 1), :].rearrange("(a p) f -> p a f", p=128),
                [], [bXLD[pi]])
            for tt in range(4):
                t = 4 * ch + tt
                hh = tt % 2
                ht = htmp if hh == 0 else htmpB
                hw = [bHT[hh], bA] if (ch == 0 and tt < 2) else [bHT[hh]]
                vop("tensor_scalar",
                    (ht[:, :], xt[:, tt, :], meanb[:, t:t + 1], rstdb[:, t:t + 1],
                     ALU.subtract, ALU.mult), [bXLD[pi], bST], hw, fd=1024)
                for f in range(8):
                    bank = 4 + (f // 4)
                    mm(psv(bank, 128 * (f % 4), 128 * (f % 4) + 128),
                       ht[:, 128 * f:128 * (f + 1)], ident[:, :], True, True,
                       [bHT[hh], bCONST], [PB[bank]])
                for half in range(2):
                    bank = 4 + half
                    scopy(h_fm[:, 4 * half:4 * half + 4, 128 * t:128 * (t + 1)],
                          psv(bank).rearrange("p (a f) -> p a f", a=4),
                          [PB[bank]], [bC])
        vop("tensor_scalar_mul", (cst[:, 9:10], cst[:, 9:10], 1.0),
            [], [bA, bXLD[0], bXLD[1], bHT[0], bHT[1]], fd=1)

    def emit_gk(g, want_q):
        cols = slice(GRP * g, GRP * (g + 1))
        for it in range(NI):
            mm(PS[2][:, 0:GRP], wgk1b[:, it, :], h_fm[:, it, cols],
               it == 0, it == NI - 1, [bCONST, bC], [PB[4]])
        scopy(t1b[:, :], PS[2][0:16, 0:GRP], [PB[4]], [bT1], fd=GRP)
        for d in range(NI):
            bank = 2 + (d % 2)
            mm(psv(bank, 0, GRP), wgk2b[:, 128 * d:128 * (d + 1)], t1b[:, :],
               True, True, [bCONST, bT1], [PB[bank]])
            act(U0[:, d, :], psv(bank, 0, GRP), AF.Sigmoid, [PB[bank], bCONST],
                [bA], bias=bgk_sb[:, d:d + 1], fd=GRP)
        vop("tensor_scalar_max", (U0[:, :, :], U0[:, :, :], 1e-38), [bA], [bA],
            fd=8 * GRP)
        if g == 0 and want_q:
            tap("sg0", U0[:, :, :], [128, 8, GRP], reads=[bA])
        act(U1[:, :, :], U0[:, :, :], AF.Ln, [bA], [bA], fd=8 * GRP)
        for d in range(NI):
            for lc in range(GRP // 128):
                si = U1[:, d, 128 * lc:128 * (lc + 1)]
                so = U0[:, d, 128 * lc:128 * (lc + 1)]
                vop("tensor_tensor_scan", (so, si, si, 0.0, ALU.add, ALU.bypass),
                    [bA], [bA], fd=128)
        for lc in range(GRP // 128):
            vop("tensor_copy", (blastb[:, :, lc:lc + 1], U0[:, :, 128 * lc + 127:128 * lc + 128]),
                [bA], [bST], fd=8)
        if g == 0 and want_q:
            tap("ls0", U1[:, :, :], [128, 8, GRP], reads=[bA])
            tap("b0", U0[:, :, :], [128, 8, GRP], reads=[bA])
            tap("blast0", blastb[:, :, :], [128, 8, 2], reads=[bST])
        act(expbl[:, :, 2 * g:2 * g + 2], blastb[:, :, :], AF.Exp,
            [bST, bCONST], [bST], scale=cb[:, C_P16:C_P16 + 1], fd=16)
        if want_q:
            act(U2[:, :, :], U0[:, :, :], AF.Exp, [bA, bCONST], [bA],
                scale=cb[:, C_P16:C_P16 + 1], bias=cb[:, C_NLN16:C_NLN16 + 1],
                fd=8 * GRP)
            act(U3[:, :, :], U0[:, :, :], AF.Exp, [bA, bCONST], [bA],
                scale=cb[:, C_N16:C_N16 + 1], fd=8 * GRP)
        for d in range(NI):
            for lc in range(GRP // 128):
                vop("tensor_scalar",
                    (U1[:, d, 128 * lc:128 * (lc + 1)],
                     U0[:, d, 128 * lc:128 * (lc + 1)],
                     blastb[:, d, lc:lc + 1], None, ALU.subtract),
                    [bA, bST], [bA], fd=128)
        act(U0[:, :, :], U1[:, :, :], AF.Exp, [bA, bCONST], [bA],
            scale=cb[:, C_N16:C_N16 + 1], fd=8 * GRP)

    def emit_ks_transpose_group():
        for lc in range(2):
            for dh in range(2):
                bank = 4 + dh
                for dd in range(4):
                    d = 4 * dh + dd
                    mm(psv(bank, 128 * dd, 128 * dd + 128),
                       ksfm[:, d, 128 * lc:128 * (lc + 1)], ident[:, :], True, True,
                       [bA, bCONST], [PB[bank]])
                scopy(kstm[:, lc, 512 * dh:512 * (dh + 1)], psv(bank),
                      [PB[bank]], [bA])

    emit_ln_h(x_prev)
    wv_p0 = qg
    dma("QG", wqk[:, 0:8, :], Wk.rearrange("(a p) f -> p a f", p=128), [bW], [bW])
    dma("QG", wv_p0[:, :, :], Wv.rearrange("(a p) f -> p a f", p=128), [bB], [bB])
    for g in range(NG):
        cols = slice(GRP * g, GRP * (g + 1))
        emit_gk(g, want_q=False)
        for d in range(NI):
            bank = 0 + (d % 2)
            for it in range(NI):
                mm(psv(bank, 0, GRP), wqk[:, it, 128 * d:128 * (d + 1)],
                   h_fm[:, it, cols], it == 0, it == NI - 1, [bW, bC], [PB[bank]])
            vop("tensor_mul", (ksfm[:, d, :], psv(bank, 0, GRP), U0[:, d, :]),
                [PB[bank], bA], [bA], fd=GRP)
        emit_ks_transpose_group()
        for tt in range(2):
            for dvg in range(4):
                bank = 2 + (dvg % 2)
                for it in range(NI):
                    mm(psv(bank),
                       h_fm[:, it, GRP * g + 128 * tt: GRP * g + 128 * (tt + 1)],
                       wv_p0[:, it, 512 * dvg:512 * (dvg + 1)],
                       it == 0, it == NI - 1, [bB, bC], [PB[bank]])
                scopy(vtmp0[:, tt, 512 * dvg:512 * (dvg + 1)], psv(bank),
                      [PB[bank]], [bD2[0]])
        for lc in range(2):
            c = 2 * g + lc
            for hh in range(NH):
                for j in range(2):
                    dd = 2 * hh + j
                    bank = 6 + j
                    mm(psv(bank),
                       kstm[:, lc, 256 * hh + 128 * j: 256 * hh + 128 * (j + 1)],
                       vtmp0[:, lc, 512 * hh:512 * (hh + 1)], True, True,
                       [bA, bD2[0], bS[dd]], [PB[bank]])
                    vop("scalar_tensor_tensor",
                        (S_sb[:, dd, :], S_sb[:, dd, :], expbl[:, dd, c:c + 1],
                         psv(bank), ALU.mult, ALU.add),
                        [bS[dd], bST, PB[bank]], [bS[dd]], fd=512)

    emit_ln_h(x_own, tapit=True)
    tap("h", h_fm[:, :, :], [128, 8, 2048], reads=[bC])

    dma("QG", wqk[:, 0:8, :], Wq.rearrange("(a p) f -> p a f", p=128), [bW], [bW])
    dma("QG", wqk[:, 8:16, :], Wk.rearrange("(a p) f -> p a f", p=128), [bW], [bW])
    for g in range(NG):
        cols = slice(GRP * g, GRP * (g + 1))
        emit_gk(g, want_q=True)
        for d in range(NI):
            bank = 0 + (d % 2)
            for it in range(NI):
                mm(psv(bank, 0, GRP), wqk[:, it, 128 * d:128 * (d + 1)],
                   h_fm[:, it, cols], it == 0, it == NI - 1, [bW, bC], [PB[bank]])
            vop("tensor_mul", (qg[:, d, cols], psv(bank, 0, GRP), U2[:, d, :]),
                [PB[bank], bA], [bB], fd=GRP)
            bank2 = 2 + (d % 2)
            for it in range(NI):
                mm(psv(bank2, 0, GRP), wqk[:, 8 + it, 128 * d:128 * (d + 1)],
                   h_fm[:, it, cols], it == 0, it == NI - 1, [bW, bC], [PB[bank2]])
            vop("tensor_mul", (kigrp[:, d, :], psv(bank2, 0, GRP), U3[:, d, :]),
                [PB[bank2], bA], [bKIG, bD2[0]], fd=GRP)
            vop("tensor_mul", (ksfm[:, d, :], psv(bank2, 0, GRP), U0[:, d, :]),
                [PB[bank2], bA], [bA], fd=GRP)

        for lc in range(2):
            def ki_out(nc, g=g, lc=lc):
                return nc.scalar.dma_start(
                    out=SP_KI[2 * g + lc, :, :, :].rearrange("a p t -> p a t"),
                    in_=kigrp[:, :, 128 * lc:128 * (lc + 1)])
            P.emit("S", ki_out, reads=[bKIG, bD2[0]], writes=[bSP["ki"]], queue="QA")
        emit_ks_transpose_group()

        def ks_out(nc, g=g):
            return nc.scalar.dma_start(
                out=SP_KS[2 * g:2 * g + 2, :, :].rearrange("l p d -> p l d"),
                in_=kstm[:, :, :])
        P.emit("S", ks_out, reads=[bA], writes=[bSP["ks"]], queue="QA")

    tap("qg", qg[:, :, :], [128, 8, 2048], reads=[bB])
    tap("expbl", expbl[:, :, :], [128, 8, 16], reads=[bST])
    dma("QG", wvb[:, :, :], Wv.rearrange("(a p) f -> p a f", p=128), [bW], [bW])
    for g in range(NG):
        for tt in range(2):
            for dvg in range(4):
                bank = (dvg % 2)
                for it in range(NI):
                    mm(psv(bank),
                       h_fm[:, it, GRP * g + 128 * tt: GRP * g + 128 * (tt + 1)],
                       wvb[:, it, 512 * dvg:512 * (dvg + 1)],
                       it == 0, it == NI - 1, [bW, bC], [PB[bank]])
                scopy(vtmp[:, tt, 512 * dvg:512 * (dvg + 1)], psv(bank),
                      [PB[bank]], [bA])

        def v_out(nc, g=g):
            return nc.scalar.dma_start(
                out=SP_V[2 * g:2 * g + 2, :, :].rearrange("l p d -> p l d"),
                in_=vtmp[:, :, :])
        P.emit("S", v_out, reads=[bA], writes=[bSP["v"]], queue="QA")

    dma("QG", wgb[:, :, :], Wg.rearrange("(a p) f -> p a f", p=128), [bW], [bW])
    for g in range(NG):
        for tt in range(2):
            for hh in range(4):
                bank = (hh % 2)
                for it in range(NI):
                    mm(psv(bank),
                       h_fm[:, it, GRP * g + 128 * tt: GRP * g + 128 * (tt + 1)],
                       wgb[:, it, 512 * hh:512 * (hh + 1)],
                       it == 0, it == NI - 1, [bW, bC], [PB[bank]])
                act(silu_t[:, :], psv(bank), AF.Silu, [PB[bank]], [bA], fd=512)
                vop("tensor_mul", (vtmp[:, tt, 512 * hh:512 * (hh + 1)],
                                   silu_t[:, :], gnwb[:, :]),
                    [bA, bCONST], [bA], fd=512)

        def gs_out(nc, g=g):
            return nc.scalar.dma_start(
                out=SP_GS[2 * g:2 * g + 2, :, :].rearrange("l p d -> p l d"),
                in_=vtmp[:, :, :])
        P.emit("S", gs_out, reads=[bA], writes=[bSP["gs"]], queue="QA")

    dma("QG", wob[:, :, :], Wo.rearrange("(a p) f -> p a f", p=128), [bW], [bW])
    for dd in range(8):
        tap(f"Sprev{dd}", S_sb[:, dd, :], [128, 512], reads=[bS[dd]])
        scopy(S_bf[:, dd, :], S_sb[:, dd, :], [bS[dd]], [bS[dd]])

    def c_prefetch(c):
        i = c % 2
        dma("QS", vs[:, i, :], SP_V[c, :, :], [bSP["v"]], [bD2[i]])
        dma("QS", kss[:, i, :], SP_KS[c, :, :], [bSP["ks"]], [bD2[i]])
        dma("QS", kis[:, i, :, :], SP_KI[c, :, :, :].rearrange("a p t -> p a t"),
            [bSP["ki"]], [bD2[i]])
        dma("QS", gss[:, i, :], SP_GS[c, :, :], [bSP["gs"]], [bD2[i]])
        dma("QS", xcs[:, i, :], x_own[128 * c:128 * (c + 1), :], [], [bD2[i]])

    c_prefetch(0)
    for c in range(NCH):
        i = c % 2
        if c + 1 < NCH:
            c_prefetch(c + 1)
        ccols = slice(128 * c, 128 * (c + 1))
        for hh in range(NH):
            for j in range(2):
                dd = 2 * hh + j
                mm(PS[0][:, 0:128], kis[:, i, dd, :], qg[:, dd, ccols],
                   j == 0, j == 1, [bD2[i], bB], [PB[0]])
            vop("tensor_mul", (sc_sb[:, :], PS[0][:, 0:128], utri[:, :]),
                [PB[0], bCONST], [bSC], fd=128)
            mm(psv(1), sc_sb[:, :], vs[:, i, 512 * hh:512 * (hh + 1)],
               True, False, [bSC, bD2[i]], [PB[1]])
            for j in range(2):
                dd = 2 * hh + j
                mm(psv(1), qg[:, dd, ccols], S_bf[:, dd, :],
                   False, j == 1, [bB, bS[dd]], [PB[1]])
            vop("tensor_copy", (osb[:, 512 * hh:512 * (hh + 1)], psv(1)),
                [PB[1]], [bA], fd=512)
            for j in range(2):
                dd = 2 * hh + j
                bank = 2 + j
                mm(psv(bank),
                   kss[:, i, 256 * hh + 128 * j: 256 * hh + 128 * (j + 1)],
                   vs[:, i, 512 * hh:512 * (hh + 1)], True, True,
                   [bD2[i], bS[dd]], [PB[bank]])
                vop("scalar_tensor_tensor",
                    (S_sb[:, dd, :], S_sb[:, dd, :], expbl[:, dd, c:c + 1],
                     psv(bank), ALU.mult, ALU.add),
                    [bS[dd], bST, PB[bank]], [bS[dd]], fd=512)
                scopy(S_bf[:, dd, :], S_sb[:, dd, :], [bS[dd]], [bS[dd]])
            act(junkC[:, :], osb[:, 512 * hh:512 * (hh + 1)], AF.Square,
                [bA], [bA, bST], accum=ssq4[:, hh:hh + 1], fd=512)
        if c == 0:
            tap("osb0", osb[:, :], [128, 2048], reads=[bA])
            tap("ssq40", ssq4[:, :], [128, 4], reads=[bST])
        vop("tensor_scalar", (a4[:, :], ssq4[:, :], 1.0 / 512, 1e-5,
            ALU.mult, ALU.add), [bST], [bST], fd=4)
        act(sd4[:, :], a4[:, :], AF.Sqrt, [bST], [bST], fd=4)
        vop("reciprocal", (rs4[:, :], sd4[:, :]), [bST], [bST], fd=4)
        vop("tensor_mul", (n4[:, :], rs4[:, :], rs4[:, :]), [bST], [bST], fd=4)
        vop("tensor_mul", (n4[:, :], n4[:, :], a4[:, :]), [bST], [bST], fd=4)
        vop("tensor_scalar", (n4[:, :], n4[:, :], -0.5, 1.5, ALU.mult, ALU.add),
            [bST], [bST], fd=4)
        vop("tensor_mul", (rs4[:, :], rs4[:, :], n4[:, :]), [bST], [bST], fd=4)
        for hh in range(NH):
            vop("scalar_tensor_tensor",
                (oft[:, 512 * hh:512 * (hh + 1)], osb[:, 512 * hh:512 * (hh + 1)],
                 rs4[:, hh:hh + 1], gss[:, i, 512 * hh:512 * (hh + 1)],
                 ALU.mult, ALU.mult), [bA, bST, bD2[i]], [bA], fd=512)
        for b4 in range(4):
            bank = 4 + (b4 % 2)
            for dd in range(4):
                dvt = 4 * b4 + dd
                mm(psv(bank, 128 * dd, 128 * dd + 128),
                   oft[:, 128 * dvt:128 * (dvt + 1)], ident[:, :], True, True,
                   [bA, bCONST], [PB[bank]])
            scopy(oTs[:, 4 * b4:4 * b4 + 4, :],
                  psv(bank).rearrange("p (a f) -> p a f", a=4), [PB[bank]], [bA])
        for dh in range(2):
            for dvt in range(16):
                mm(PS[3][:, 512 * dh:512 * (dh + 1)], oTs[:, dvt, :],
                   wob[:, dvt, 512 * dh:512 * (dh + 1)],
                   dvt == 0, dvt == 15, [bA, bW], [PB[6 + dh]])
        vop("tensor_copy", (asb[:, :], PS[3][:, :]), [PB[6], PB[7]], [bA], fd=1024)
        if c == 0:
            tap("asb0", asb[:, :], [128, 1024], reads=[bA])
            tap("oft0", oft[:, :], [128, 2048], reads=[bA])
        vop("reduce_sum", (cst[:, 0:1], asb[:, :], AX.X), [bA], [bST], fd=1)
        act(junkC[:, :], asb[:, 0:512], AF.Square, [bA], [bA, bST],
            accum=cst[:, 1:2], fd=512)
        act(junkC[:, :], asb[:, 512:1024], AF.Square, [bA], [bA, bST],
            accum=cst[:, 2:3], fd=512)
        vop("tensor_add", (cst[:, 1:2], cst[:, 1:2], cst[:, 2:3]), [bST], [bST], fd=1)
        ln_finish(0, 1, 3, 4, 5, D)
        vop("tensor_mul", (cst[:, 5:6], cst[:, 3:4], cst[:, 4:5]), [bST], [bST], fd=1)
        vop("tensor_scalar_mul", (cst[:, 5:6], cst[:, 5:6], -1.0), [bST], [bST], fd=1)
        act(t2x2[:, :], asb[:, :], AF.Identity, [bA, bST], [bA],
            scale=cst[:, 4:5], bias=cst[:, 5:6])
        vop("tensor_add", (t2x2[:, :], t2x2[:, :], xcs[:, i, :]),
            [bA, bD2[i]], [bA], fd=1024)

        if c == 0:
            tap("x20", t2x2[:, :], [128, 1024], reads=[bA])
        def x2_out(nc, c=c):
            return nc.scalar.dma_start(out=SP_X2[c, :, :], in_=t2x2[:, :])
        P.emit("S", x2_out, reads=[bA], writes=[bSP["x2"]], queue="QA")
        vop("reduce_sum", (cst[:, 6:7], t2x2[:, :], AX.X), [bA], [bST], fd=1)
        act(junkC[:, :], t2x2[:, 0:512], AF.Square, [bA], [bA, bST],
            accum=cst[:, 7:8], fd=512)
        act(junkC[:, :], t2x2[:, 512:1024], AF.Square, [bA], [bA, bST],
            accum=cst[:, 8:9], fd=512)
        vop("tensor_add", (cst[:, 7:8], cst[:, 7:8], cst[:, 8:9]), [bST], [bST], fd=1)
        ln_finish(6, 7, 9, 10, 11, D)
        vop("tensor_scalar", (h2b[:, :], t2x2[:, :], cst[:, 9:10], cst[:, 10:11],
            ALU.subtract, ALU.mult), [bA, bST], [bA], fd=1024)

        if c == 0:
            tap("h20", h2b[:, :], [128, 1024], reads=[bA])
            tap("cst0", cst[:, 0:11], [128, 11], reads=[bST])

        def redmax(nc):
            return nc.vector.reduce_max(cst[:, 11:12], h2b[:, :], axis=AX.X,
                                        apply_absolute_value=True)
        P.emit("V", redmax, reads=[bA], writes=[bST], fd=1)
        vop("tensor_scalar_max", (cst[:, 11:12], cst[:, 11:12], 1e-5),
            [bST], [bST], fd=1)
        vop("tensor_scalar_mul", (sinv[:, c:c + 1], cst[:, 11:12], 1.0 / 127.0),
            [bST], [bST], fd=1)
        vop("reciprocal", (cst[:, 11:12], sinv[:, c:c + 1]), [bST], [bST], fd=1)
        act(rt[:, :], h2b[:, :], AF.Identity, [bA, bST], [bA], scale=cst[:, 11:12])
        act(ri[:, :], h2b[:, :], AF.Identity, [bA, bST], [bA],
            scale=cst[:, 11:12])
        act(rf[:, :], ri[:, :], AF.Identity, [bA], [bA])
        vop("tensor_sub", (rt[:, :], rt[:, :], rf[:, :]), [bA], [bA], fd=1024)
        vop("tensor_single_scalar", (ra[:, :], rt[:, :], 0.5, ALU.is_gt),
            [bA], [bA], fd=1024)
        for _k in (1.5, 2.5, 3.5):
            vop("scalar_tensor_tensor", (ra[:, :], rt[:, :], _k, ra[:, :],
                ALU.is_gt, ALU.add), [bA], [bA], fd=1024)
        vop("scalar_tensor_tensor", (ra[:, :], rt[:, :], -0.5, ra[:, :],
            ALU.is_lt, ALU.subtract), [bA], [bA], fd=1024)
        for _k in (-1.5, -2.5, -3.5):
            vop("scalar_tensor_tensor", (ra[:, :], rt[:, :], _k, ra[:, :],
                ALU.is_lt, ALU.add), [bA], [bA], fd=1024)
        vop("tensor_sub", (xqt[:, :], rf[:, :], ra[:, :]), [bA], [bA], fd=1024)
        if c == 0:
            tap("xqt0", xqt[:, :], [128, 1024], reads=[bA])
        for half in range(2):
            bank = 4 + half
            for dd in range(4):
                f = 4 * half + dd
                mm(psv(bank, 128 * dd, 128 * dd + 128),
                   xqt[:, 128 * f:128 * (f + 1)], ident[:, :], True, True,
                   [bA, bCONST], [PB[bank]])
            scopy(xq_fm[:, 4 * half:4 * half + 4, 128 * c:128 * (c + 1)],
                  psv(bank).rearrange("p (a f) -> p a f", a=4),
                  [PB[bank]], [bXQ, bC])

    bWC = [B("wc0"), B("wc1")]
    bF0T, bF0B, bTRN = B("f0tb"), B("f0bb"), B("trnb")
    bTST2 = [bTST, B("ternTstB")]
    ternTst2 = [ternTst, ternTstB]
    _f0first = set()

    def f0w(key, bufs, extra):
        if key in _f0first:
            return bufs
        _f0first.add(key)
        return bufs + extra

    def emit_wquant(wdram, O, IW, mcol, spill, spkey):
        rows = 524288 // IW
        nchunks = O // rows
        n_it = IW // 128
        n_ot = rows // 128
        for ch in range(nchunks):
            pi = ch % 2
            src = wdram[rows * ch: rows * (ch + 1), :]
            if n_ot > 1:
                src = src.rearrange("(a p) f -> p a f", p=128)
                dst = wch[:, pi, :].rearrange("p (a f) -> p a f", a=n_ot)
            else:
                dst = wch[:, pi, :]
            dma("QS" if pi == 0 else "QG", dst, src, [],
                f0w("wc%d" % pi, [bWC[pi]], [bA]))

            def red(nc, pi=pi, ch=ch):
                return nc.vector.reduce_sum(part4[:, ch:ch + 1], wch[:, pi, :],
                                            axis=AX.X, apply_absolute_value=True)
            P.emit("V", red, reads=[bWC[pi]], writes=[bST], fd=1)
        vop("reduce_sum", (tot1[:, :], part4[:, 0:nchunks], AX.X), [bST], [bST], fd=1)
        mm(PS[2][:, 0:1], allones[:, :], tot1[:, :], True, True,
           [bCONST, bST], [PB[4]])
        vop("tensor_copy", (mbc[:, mcol:mcol + 1], PS[2][:, 0:1]), [PB[4]], [bST], fd=1)
        vop("tensor_scalar_mul", (mbc[:, mcol:mcol + 1], mbc[:, mcol:mcol + 1],
            1.0 / (O * IW)), [bST], [bST], fd=1)
        vop("tensor_scalar_max", (mbc[:, mcol:mcol + 1], mbc[:, mcol:mcol + 1],
            1e-5), [bST], [bST], fd=1)
        vop("reciprocal", (mibc[:, mcol:mcol + 1], mbc[:, mcol:mcol + 1]),
            [bST], [bST], fd=1)
        for ch in range(nchunks):
            pi = ch % 2
            src = wdram[rows * ch: rows * (ch + 1), :]
            if n_ot > 1:
                src = src.rearrange("(a p) f -> p a f", p=128)
                dst = wch[:, pi, :].rearrange("p (a f) -> p a f", a=n_ot)
            else:
                dst = wch[:, pi, :]
            dma("QS" if pi == 0 else "QG", dst, src, [], [bWC[pi]])
            act(f0t[:, :], wch[:, pi, :], AF.Identity, [bWC[pi], bST],
                f0w("f0t", [bF0T], [bB]), scale=mibc[:, mcol:mcol + 1], fd=4096)
            vop("tensor_single_scalar", (f0bn[:, :], f0t[:, :], -0.5, ALU.is_lt),
                [bF0T], f0w("f0b", [bF0B], [bB]), fd=4096)
            vop("scalar_tensor_tensor", (ternb[:, :], f0t[:, :], 0.5, f0bn[:, :],
                ALU.is_gt, ALU.subtract), [bF0T, bF0B],
                f0w("trn", [bTRN], [bA]), fd=4096)
            for ot in range(n_ot):
                oglob = n_ot * ch + ot
                for itb in range(n_it // 4):
                    bank = 4 + (itb % 2)
                    ti2 = (ot * (n_it // 4) + itb) % 2
                    for dd in range(4):
                        it = 4 * itb + dd
                        mm(psv(bank, 128 * dd, 128 * dd + 128),
                           ternb[:, IW * ot + 128 * it: IW * ot + 128 * (it + 1)],
                           ident[:, :], True, True, [bTRN, bCONST], [PB[bank]])
                    if ti2 == 0:
                        scopy(ternTst2[0][:, :, :],
                              psv(bank).rearrange("p (a f) -> p a f", a=4),
                              [PB[bank]], f0w("tst0", [bTST2[0]], [bCONST, bA]))
                    else:
                        vop("tensor_copy",
                            (ternTst2[1][:, :, :],
                             psv(bank).rearrange("p (a f) -> p a f", a=4)),
                            [PB[bank]], f0w("tst1", [bTST2[1]], [bCONST, bA]),
                            fd=512)

                    def tout(nc, itb=itb, oglob=oglob, spill=spill, ti2=ti2):
                        return nc.scalar.dma_start(
                            out=spill[4 * itb:4 * itb + 4, :,
                                      128 * oglob:128 * (oglob + 1)]
                            .rearrange("a p t -> p a t"),
                            in_=ternTst2[ti2][:, :, :])
                    P.emit("S", tout, reads=[bTST2[ti2]], writes=[bSP[spkey]],
                           queue="QA")

    emit_wquant(wgate, INNER, D, 0, SP_TG, "tg")
    emit_wquant(wup, INNER, D, 1, SP_TU, "tu")
    emit_wquant(wdown, D, INNER, 2, SP_TD, "td")

    tap("mbc", mbc[:, 0:3], [128, 3], reads=[bST])
    tap("sinv", sinv[:, :], [128, 16], reads=[bST])
    vop("tensor_scalar_mul", (dqg[:, :], sinv[:, :], mbc[:, 0:1]), [bST], [bST], fd=16)
    vop("tensor_scalar_mul", (dqu[:, :], sinv[:, :], mbc[:, 1:2]), [bST], [bST], fd=16)

    bGU = [bB, B("GUHI")]
    bRT = [B("rt1a"), B("rt1b")]
    bRF = [B("rf1a"), B("rf1b")]
    bRI, bRA = B("ri1q"), B("ra1q")
    bGUQ, bGUT, bSIL = B("guq1"), B("guT1"), B("silF1")
    _first = set()

    def fwr(key, bufs, extra):
        if key in _first:
            return bufs
        _first.add(key)
        return bufs + extra

    def round_pieces(t):
        reg = (t // 2) % 2
        gu_t = gu_lo if reg == 0 else gu_hi
        gb = bGU[reg]
        sc = 10 + (t % 2)

        def p_rmax():
            def rmx(nc, gu_t=gu_t, ti=t % 2, sc=sc):
                return nc.vector.reduce_max(cst[:, sc:sc + 1], gu_t[:, ti, :],
                                            axis=AX.X, apply_absolute_value=True)
            P.emit("V", rmx, reads=[gb], writes=[bST], fd=1)
            vop("tensor_scalar_max", (cst[:, sc:sc + 1], cst[:, sc:sc + 1], 1e-5),
                [bST], [bST], fd=1)
            vop("tensor_scalar_mul", (s2inv[:, t:t + 1], cst[:, sc:sc + 1],
                1.0 / 127.0), [bST], [bST], fd=1)
            vop("reciprocal", (cst[:, sc:sc + 1], s2inv[:, t:t + 1]),
                [bST], [bST], fd=1)

        def p_acts(q):
            pi = q % 2
            qc = slice(1024 * q, 1024 * (q + 1))
            act(rt1p[pi][:, :], gu_t[:, t % 2, qc], AF.Identity,
                [gb, bST], fwr("rt%d" % pi, [bRT[pi]], [bA]),
                scale=cst[:, sc:sc + 1], bias=cb[:, 8:9], fd=1024)
            act(ri1q[:, :], gu_t[:, t % 2, qc], AF.Identity,
                [gb, bST], fwr("ri", [bRI], [bA]),
                scale=cst[:, sc:sc + 1], fd=1024)
            act(rf1p[pi][:, :], ri1q[:, :], AF.Identity,
                [bRI], fwr("rf%d" % pi, [bRF[pi]], [bA]), fd=1024)

        def p_ladder(q):
            pi = q % 2
            qc = slice(1024 * q, 1024 * (q + 1))
            vop("scalar_tensor_tensor", (ra1q[:, :], rf1p[pi][:, :], -1.0,
                rt1p[pi][:, :], ALU.mult, ALU.add),
                [bRF[pi], bRT[pi]], fwr("ra", [bRA], [bA]), fd=1024)
            for _k in (64.0, 32.0, 16.0, 8.0, 4.0, 2.0, 1.0):
                vop("tensor_single_scalar", (rf1p[pi][:, :], ra1q[:, :], _k,
                    ALU.is_ge), [bRA], [bRF[pi]], fd=1024)
                vop("scalar_tensor_tensor", (ra1q[:, :], rf1p[pi][:, :], -_k,
                    ra1q[:, :], ALU.mult, ALU.add), [bRF[pi]], [bRA], fd=1024)
            vop("scalar_tensor_tensor", (guq[:, qc], ra1q[:, :], -1.0,
                rt1p[pi][:, :], ALU.mult, ALU.add),
                [bRA, bRT[pi]], fwr("guq", [bGUQ], [bA]), fd=1024)
            vop("tensor_single_scalar", (guq[:, qc], guq[:, qc], -64.0, ALU.add),
                [bGUQ], [bGUQ], fd=1024)

        def p_trans():
            tap("guqt%d" % t, guq[:, :], [128, 4096], reads=[bGUQ])
            for half in range(2):
                for b4 in range(4 * half, 4 * half + 4):
                    bank = 4 + (b4 % 2)
                    for dd in range(4):
                        it = 4 * b4 + dd
                        mm(psv(bank, 128 * dd, 128 * dd + 128),
                           guq[:, 128 * it:128 * (it + 1)], ident[:, :],
                           True, True, [bGUQ, bCONST], [PB[bank]])
                    scopy(guqT[:, 4 * (b4 % 4):4 * (b4 % 4) + 4, :],
                          psv(bank).rearrange("p (a f) -> p a f", a=4),
                          [PB[bank]], fwr("guT", [bGUT], [bA]))

                def gq_out(nc, t=t, half=half):
                    return nc.scalar.dma_start(
                        out=SP_GQ[t, 16 * half:16 * (half + 1), :, :]
                        .rearrange("a p u -> p a u"),
                        in_=guqT[:, :, :])
                P.emit("S", gq_out, reads=[bGUT], writes=[bSPGQ[t]],
                       queue="QA")

        return [lambda: (p_rmax(), p_acts(0), p_acts(1)),
                lambda: (p_ladder(0), p_acts(2)),
                lambda: (p_ladder(1), p_acts(3)),
                lambda: p_ladder(2),
                lambda: p_ladder(3),
                p_trans]

    pieces = []
    DRAIN = [2, 2, 2, 2, 1, 1, 1, 1]
    for p in range(8):
        preg = p % 2
        gu_t = gu_lo if preg == 0 else gu_hi
        gkey = "gulo" if preg == 0 else "guhi"
        gext = [] if preg == 0 else [bD2[0], bD2[1]]
        for og in range(8):
            pi = og % 2
            wr_extra = [bW] if (p == 0 and og < 2) else []
            dma("QS", tgs[:, pi, :, :],
                SP_TG[:, :, 512 * og:512 * (og + 1)].rearrange("a p t -> p a t"),
                [bSP["tg"]], [bTG2[pi]] + wr_extra)
            dma("QG", tus[:, pi, :, :],
                SP_TU[:, :, 512 * og:512 * (og + 1)].rearrange("a p t -> p a t"),
                [bSP["tu"]], [bTG2[pi]] + wr_extra)
            for ti in range(2):
                t = 2 * p + ti
                gpb = psv(2 * ti)
                upb = psv(1 + 2 * ti)
                for it in range(NI):
                    mm(gpb, xq_fm[:, it, 128 * t:128 * (t + 1)],
                       tgs[:, pi, it, :], it == 0, it == NI - 1,
                       [bXQ, bTG2[pi]], [PB[2 * ti]])
                act(silu_f1[:, :], gpb, AF.Silu, [PB[2 * ti], bST],
                    fwr("sil", [bSIL], [bA]), scale=dqg[:, t:t + 1], fd=512)
                for it in range(NI):
                    mm(upb, xq_fm[:, it, 128 * t:128 * (t + 1)],
                       tus[:, pi, it, :], it == 0, it == NI - 1,
                       [bXQ, bTG2[pi]], [PB[1 + 2 * ti]])
                vop("scalar_tensor_tensor",
                    (gu_t[:, ti, 512 * og:512 * (og + 1)], upb,
                     dqu[:, t:t + 1], silu_f1[:, :], ALU.mult, ALU.mult),
                    [PB[1 + 2 * ti], bST, bSIL],
                    fwr(gkey, [bGU[preg]], gext), fd=512)
            for _ in range(DRAIN[og]):
                if pieces:
                    pieces.pop(0)()
        if p == 0:
            tap("guA", gu_lo[:, 0, :], [128, 4096], reads=[bGU[0]])
        pieces.extend(round_pieces(2 * p))
        pieces.extend(round_pieces(2 * p + 1))
    while pieces:
        pieces.pop(0)()

    vop("tensor_scalar_mul", (dqd[:, :], s2inv[:, :], mbc[:, 2:3]), [bST], [bST], fd=16)
    tap("dqdf", dqd[:, :], [128, 16], reads=[bST])
    tap("xqf", xq_fm[:, :, :], [128, 8, 2048], reads=[bXQ])
    tap("s2inv", s2inv[:, :], [128, 16], reads=[bST])
    tap("dqgf", dqg[:, :], [128, 16], reads=[bST])
    tap("dquf", dqu[:, :], [128, 16], reads=[bST])

    dma("QS", td_lo[:, :, :], SP_TD[0:16, :, :].rearrange("a p t -> p a t"),
        [bSP["td"]], [bB])
    dma("QS", td_hi[:, :, :], SP_TD[16:32, :, :].rearrange("a p t -> p a t"),
        [bSP["td"]], [bD2[0], bD2[1], bGU[1]])

    def f2_prefetch(t, first=False):
        i = t % 2
        extra = ([bA, bRT[0], bRT[1], bRF[0], bRF[1], bRI, bRA,
                  bGUQ, bGUT, bSIL] if first else [])
        dma("QS", guqs[:, i, :, :], SP_GQ[t, :, :, :].rearrange("a p u -> p a u"),
            [bSPGQ[t]], [bF2[i]] + extra)
        dma("QS", x2t[:, i, :], SP_X2[t, :, :], [bSP["x2"]], [bF2[i]])

    f2_prefetch(0, first=True)
    for t in range(NT):
        i = t % 2
        if t + 1 < NT:
            f2_prefetch(t + 1)
        for dh in range(2):
            bank = 2 * dh
            for it in range(32):
                tdt = td_lo if it < 16 else td_hi
                tdb = bB if it < 16 else B("D0")
                mm(psv(bank), guqs[:, i, it, :],
                   tdt[:, it % 16, 512 * dh:512 * (dh + 1)],
                   it == 0, it == 31, [bF2[i], tdb], [PB[bank]])
            act(fsb[:, 512 * dh:512 * (dh + 1)], psv(bank), AF.Copy,
                [PB[bank], bST], [bA], scale=dqd[:, t:t + 1], fd=512)
        if t == 0:
            tap("fsb0", fsb[:, :], [128, 1024], reads=[bA])
            tap("tdl0", td_lo[:, 0, :], [128, 1024], reads=[bB])
            tap("guqs0", guqs[:, 0, :, :], [128, 32, 128], reads=[bF2[0]])
            tap("x2t0", x2t[:, 0, :], [128, 1024], reads=[bF2[0]])
        vop("reduce_sum", (cst[:, 0:1], fsb[:, :], AX.X), [bA], [bST], fd=1)
        act(t3b[:, 0:512], fsb[:, 0:512], AF.Square, [bA], [bA, bST],
            accum=cst[:, 1:2], fd=512)
        act(t3b[:, 0:512], fsb[:, 512:1024], AF.Square, [bA], [bA, bST],
            accum=cst[:, 2:3], fd=512)
        vop("tensor_add", (cst[:, 1:2], cst[:, 1:2], cst[:, 2:3]), [bST], [bST], fd=1)
        ln_finish(0, 1, 3, 4, 5, D)
        vop("tensor_mul", (cst[:, 5:6], cst[:, 3:4], cst[:, 4:5]), [bST], [bST], fd=1)
        vop("tensor_scalar_mul", (cst[:, 5:6], cst[:, 5:6], -1.0), [bST], [bST], fd=1)
        act(t3b[:, :], fsb[:, :], AF.Identity, [bA, bST], [bA],
            scale=cst[:, 4:5], bias=cst[:, 5:6])
        vop("tensor_add", (ysb[:, i, :], t3b[:, :], x2t[:, i, :]),
            [bA, bF2[i]], fwr("ysb%d" % i, [B("ysb%d" % i)], [bXQ]), fd=1024)

        def y_out(nc, t=t, i=i):
            return nc.gpsimd.dma_start(out=yD[128 * t:128 * (t + 1), :],
                                       in_=ysb[:, i, :])
        P.emit("G", y_out, reads=[B("ysb%d" % i)], writes=[bOUT], queue="QO")

    P.finalize()
    sems = {o: nc.alloc_semaphore(name=f"sem_{o}") for o in OWNERS}

    def val(o2, k2):
        v = P.incno[o2][k2 - 1]
        return v * 16 if o2 in QUEUES else v

    with nc.Block() as blk:
        handles = {"T": blk.tensor, "V": blk.vector, "S": blk.scalar,
                   "G": blk.gpsimd, "Y": blk.sync}
        for e in ENGS:
            if not P.estream[e] and e != "Y":
                continue

            def section(eobj, e=e):
                for (owner, k) in P.estream[e]:
                    op = P.ops[owner][k - 1]
                    for (o2, k2) in op.waits:
                        eobj.wait_ge(sems[o2], val(o2, k2))
                    ins = op.build(nc)
                    if op.marked:
                        ins.then_inc(sems[owner], 16 if owner in QUEUES else 1)
                if e == "Y":
                    for q in QUEUES:
                        tot = P.totals[q]
                        if tot:
                            eobj.wait_ge(sems[q], tot * 16)
            handles[e](section)
    return nc


_NC_CACHE = []


def kernel(**inputs):
    x = np.ascontiguousarray(inputs["L0_attn_hidden_states"], dtype=np.float32)
    Bsz, L, Dm = x.shape
    if not _NC_CACHE:
        _NC_CACHE.append(build_program())
    nc = _NC_CACHE[0]

    cbv = np.zeros((128, 16), np.float32)
    cbv[:, C_HALF] = 0.5
    cbv[:, C_NHALF] = -0.5
    cbv[:, C_P16] = 1.0 / 16
    cbv[:, C_N16] = -1.0 / 16
    cbv[:, C_NLN16] = -LN16
    cbv[:, C_TINY] = 1e-38
    cbv[:, C_ONE] = 1.0
    cbv[:, C_EPS] = 1e-5
    cbv[:, 8] = 64.5
    identv = bf16(np.eye(128, dtype=np.float32))
    utriv = bf16(np.tril(np.ones((128, 128), np.float32)).T)
    wgk1pad = np.zeros((D, 128), np.float32)
    wgk1pad[:, 0:16] = np.asarray(inputs["L0_attn_Wgk1"], np.float32)
    common = {
        "cb": cbv, "ident": identv, "utri": utriv,
        "onescol": np.ones((128, 1), np.float32),
        "onesrow": np.ones((1, 128), np.float32),
        "allones": np.ones((128, 128), np.float32),
        "wgk1pad": wgk1pad,
    }
    for nm in ("L0_attn_Wq", "L0_attn_Wk", "L0_attn_Wv", "L0_attn_Wg",
               "L0_attn_Wgk2", "L0_attn_bgk2",
               "L0_attn_g_norm_w", "L0_attn_Wo", "L0_ffn_w_gate",
               "L0_ffn_w_up", "L0_ffn_w_down"):
        common[nm] = np.ascontiguousarray(inputs[nm], np.float32)

    zeros = np.zeros((T, Dm), np.float32)
    in_maps = []
    for c in range(8):
        b, h = c // 2, c % 2
        m = dict(common)
        m["x_own"] = np.ascontiguousarray(x[b, h * T:(h + 1) * T, :])
        m["x_prev"] = np.ascontiguousarray(x[b, 0:T, :]) if h == 1 else zeros
        in_maps.append(m)

    res = run_bass_kernel_spmd(nc, in_maps, core_ids=list(range(8)))
    out = np.empty((Bsz, L, Dm), np.float32)
    for c in range(8):
        b, h = c // 2, c % 2
        out[b, h * T:(h + 1) * T, :] = res.results[c]["y"]
    return out


# version 19
# speedup vs baseline: 1.0400x; 1.0087x over previous
import numpy as np
import ml_dtypes
import concourse.bass as bass
import concourse.mybir as mybir
from concourse.bass_utils import run_bass_kernel_spmd

F32 = mybir.dt.float32
BF16 = mybir.dt.bfloat16
I32 = mybir.dt.int32
AF = mybir.ActivationFunctionType
ALU = mybir.AluOpType
AX = mybir.AxisListType

T = 2048
NT = 16
NCH = 16
D = 1024
NI = 8
DK = 1024
DV = 2048
NH = 4
INNER = 4096
GRP = 256
NG = T // GRP
LN16 = float(np.log(16.0))

C_HALF, C_NHALF, C_P16, C_N16, C_NLN16, C_TINY, C_ONE, C_EPS = range(8)

_TAPS = {}

ENGS = ("T", "V", "S", "G", "Y")
QUEUES = ("QS", "QA", "QG", "QO")
OWNERS = ENGS + QUEUES
QISSUER = {"QS": "Y", "QA": "S", "QG": "G", "QO": "G"}


class Buf:
    __slots__ = ("name", "writer", "readers")

    def __init__(self, name):
        self.name = name
        self.writer = None
        self.readers = []


class Op:
    __slots__ = ("build", "waits", "marked", "inc", "fd")

    def __init__(self, build, fd):
        self.build = build
        self.fd = fd
        self.waits = []
        self.marked = False
        self.inc = 0


class Plan:
    def __init__(self):
        self.ops = {o: [] for o in OWNERS}
        self.estream = {e: [] for e in ENGS}
        self.synced = {e: {} for e in ENGS}
        self.snaps = {o: [] for o in OWNERS}

    def emit(self, eng, build, reads=(), writes=(), queue=None, fd=1024):
        owner = queue if queue else eng
        k = len(self.ops[owner]) + 1
        op = Op(build, fd)

        deps = set()
        for b in reads:
            w = b.writer
            if w is None:
                continue
            o2, k2 = w
            if o2 == eng and queue is None:
                if eng in ("V", "G") and self.ops[o2][k2 - 1].fd < 512:
                    deps.add(w)
                continue
            deps.add(w)
        for b in writes:
            w = b.writer
            if w is not None:
                o2, k2 = w
                if not (o2 == eng and queue is None):
                    deps.add(w)
                elif eng in ("V", "G") and self.ops[o2][k2 - 1].fd < 512:
                    deps.add(w)
            for r in b.readers:
                o2, k2 = r
                if o2 == eng and queue is None:
                    continue
                deps.add(r)

        syn = self.synced[eng]
        final = []
        for (o2, k2) in sorted(deps):
            if syn.get(o2, 0) >= k2:
                continue
            final.append((o2, k2))
            if syn.get(o2, 0) < k2:
                syn[o2] = k2
            for o3, k3 in self.snaps[o2][k2 - 1].items():
                if syn.get(o3, 0) < k3:
                    syn[o3] = k3
            self.ops[o2][k2 - 1].marked = True

        op.waits = final
        self.ops[owner].append(op)
        self.estream[eng].append((owner, k))
        snap = dict(syn)
        snap[owner] = k
        self.snaps[owner].append(snap)

        for b in writes:
            b.writer = (owner, k)
            b.readers = []
        for b in reads:
            b.readers.append((owner, k))
        return op

    def finalize(self):
        incno = {}
        for o in OWNERS:
            cnt = 0
            lst = []
            for op in self.ops[o]:
                if o in QUEUES:
                    op.marked = True
                if op.marked:
                    cnt += 1
                op.inc = cnt
                lst.append(cnt)
            incno[o] = lst
        self.incno = incno
        self.totals = {o: (incno[o][-1] if incno[o] else 0) for o in OWNERS}


def bf16(x):
    return np.asarray(x, dtype=ml_dtypes.bfloat16)


def build_program():
    nc = bass.Bass("TRN2", target_bir_lowering=False,
                   detect_race_conditions=False)
    P = Plan()

    def din(name, shape, dt=F32):
        return nc.dram_tensor(name, list(shape), dt, kind="ExternalInput").ap()

    x_own = din("x_own", [T, D])
    x_prev = din("x_prev", [T, D])
    Wq = din("L0_attn_Wq", [D, DK])
    Wk = din("L0_attn_Wk", [D, DK])
    Wv = din("L0_attn_Wv", [D, DV])
    Wg = din("L0_attn_Wg", [D, DV])
    Wgk1 = din("wgk1pad", [D, 128])
    Wgk2 = din("L0_attn_Wgk2", [16, DK])
    bgk2 = din("L0_attn_bgk2", [DK])
    gnw = din("L0_attn_g_norm_w", [512])
    Wo = din("L0_attn_Wo", [DV, D])
    wgate = din("L0_ffn_w_gate", [INNER, D])
    wup = din("L0_ffn_w_up", [INNER, D])
    wdown = din("L0_ffn_w_down", [D, INNER])
    cbD = din("cb", [128, 16])
    identD = din("ident", [128, 128], BF16)
    utriD = din("utri", [128, 128], BF16)
    onescolD = din("onescol", [128, 1])
    onesrowD = din("onesrow", [1, 128])
    allonesD = din("allones", [128, 128])
    yD = nc.dram_tensor("y", [T, D], F32, kind="ExternalOutput").ap()

    spk = "ExternalOutput" if "_spills" in _TAPS else "Internal"
    SP_KI = nc.dram_tensor("sp_ki", [NCH, NI, 128, 128], BF16, kind=spk).ap()
    SP_KS = nc.dram_tensor("sp_ks", [NCH, 128, DK], BF16, kind=spk).ap()
    SP_V = nc.dram_tensor("sp_v", [NCH, 128, DV], BF16, kind=spk).ap()
    SP_GS = nc.dram_tensor("sp_gs", [NCH, 128, DV], BF16, kind=spk).ap()
    SP_X2 = nc.dram_tensor("sp_x2", [NCH, 128, D], F32, kind=spk).ap()
    SP_TG = nc.dram_tensor("sp_tg", [NI, 128, INNER], BF16, kind=spk).ap()
    SP_TU = nc.dram_tensor("sp_tu", [NI, 128, INNER], BF16, kind=spk).ap()
    SP_TD = nc.dram_tensor("sp_td", [32, 128, D], BF16, kind=spk).ap()
    SP_GQ = nc.dram_tensor("sp_gq", [NT, 32, 128, 128], BF16, kind=spk).ap()

    BASE = (nc.sbuf_base + 512 + 31) & ~31
    off = [BASE]

    _offs = {}

    def place(name, shape, dt, at=None):
        nbytes = int(np.prod(shape[1:])) * mybir.dt.size(dt)
        if at is None:
            at = off[0]
            off[0] = (at + nbytes + 31) & ~31
        _offs[name] = at
        return nc.alloc_sbuf_tensor_at(name, list(shape), dt, offset=at)

    cb = place("cb", [128, 16], F32)
    bgk_sb = place("bgk", [128, 8], F32)
    gnwb = place("gnwb", [128, 512], F32)
    ident = place("ident", [128, 128], BF16)
    utri = place("utri", [128, 128], BF16)
    onescol = place("onescol", [128, 1], F32)
    onesrow = place("onesrow", [1, 128], F32)
    wgk1b = place("wgk1b", [128, 8, 128], BF16)
    allones = place("allones", [128, 128], F32)
    wgk2b = place("wgk2b", [16, DK], BF16)
    t1b = place("t1b", [16, GRP], BF16)
    gnw_sb = place("gnw_sb", [1, 512], F32)
    S_sb = place("S", [128, 8, 512], F32)
    S_bf = place("S_bf", [128, 8, 512], BF16)
    expbl = place("expbl", [128, 8, 16], F32)
    blastb = place("blastb", [128, 8, 2], F32)
    sums = place("sums", [128, 16], F32)
    sumsq = place("sumsq", [128, 16], F32)
    meanb = place("meanb", [128, 16], F32)
    aepsb = place("aepsb", [128, 16], F32)
    rstdb = place("rstdb", [128, 16], F32)
    nta = place("nta", [128, 16], F32)
    sinv = place("sinv", [128, 16], F32)
    s2inv = place("s2inv", [128, 16], F32)
    dqg = place("dqg", [128, 16], F32)
    dqu = place("dqu", [128, 16], F32)
    dqd = place("dqd", [128, 16], F32)
    mbc = place("mbc", [128, 4], F32)
    mibc = place("mibc", [128, 4], F32)
    part4 = place("part4", [128, 16], F32)
    tot1 = place("tot1", [128, 1], F32)
    t11 = place("t11", [1, 4], F32)
    ssq4 = place("ssq4", [128, 4], F32)
    a4 = place("a4", [128, 4], F32)
    sd4 = place("sd4", [128, 4], F32)
    rs4 = place("rs4", [128, 4], F32)
    n4 = place("n4", [128, 4], F32)
    cst = place("cst", [128, 12], F32)
    sc_sb = place("sc_sb", [128, 128], BF16)
    ternTst = place("ternTst", [128, 4, 128], BF16)
    ternTstB = place("ternTstB", [128, 4, 128], BF16, at=_offs["gnwb"])

    WOFF = off[0]
    wqk = place("wqk", [128, 16, 1024], BF16, at=WOFF)
    wvb = place("wvb", [128, 8, 2048], BF16, at=WOFF)
    wgb = place("wgb", [128, 8, 2048], BF16, at=WOFF)
    wob = place("wob", [128, 16, 1024], BF16, at=WOFF)
    tgs = place("tgs", [128, 2, 8, 512], BF16, at=WOFF)
    tus = place("tus", [128, 2, 8, 512], BF16, at=WOFF + 16384)
    off[0] = WOFF + 32768

    AOFF = off[0]
    xld = place("xld", [128, 4, 1024], F32, at=AOFF)
    xldB = place("xldB", [128, 4, 1024], F32, at=AOFF + 24576)
    htmpB = place("htmpB", [128, 1024], BF16, at=AOFF + 22528)
    junkA = place("junkA", [128, 1024], F32, at=AOFF + 16384)
    htmp = place("htmp", [128, 1024], BF16, at=AOFF + 20480)
    U0 = place("U0", [128, 8, GRP], F32, at=AOFF)
    U1 = place("U1", [128, 8, GRP], F32, at=AOFF + 8192)
    U2 = place("U2", [128, 8, GRP], F32, at=AOFF + 16384)
    U3 = place("U3", [128, 8, GRP], F32, at=AOFF + 24576)
    ksfm = place("ksfm", [128, 8, GRP], BF16, at=AOFF + 32768)
    kstm = place("kstm", [128, 2, 1024], BF16, at=AOFF + 36864)
    vtmp = place("vtmp", [128, 2, 2048], BF16, at=AOFF)
    silu_t = place("silu_t", [128, 512], F32, at=AOFF + 8192)
    osb = place("osb", [128, 2048], F32, at=AOFF)
    asb = place("asb", [128, 1024], F32, at=AOFF + 2048)
    oft = place("oft", [128, 2048], BF16, at=AOFF + 8192)
    oTs = place("oTs", [128, 16, 128], BF16, at=AOFF + 12288)
    t2x2 = place("t2x2", [128, 1024], F32, at=AOFF + 16384)
    junkC = place("junkC", [128, 512], F32, at=AOFF + 20480)
    h2b = place("h2b", [128, 1024], F32, at=AOFF + 20480)
    rt = place("rt", [128, 1024], F32, at=AOFF + 24576)
    ri = place("ri", [128, 1024], I32, at=AOFF + 28672)
    ra = place("ra", [128, 1024], F32, at=AOFF + 28672)
    rf = place("rf", [128, 1024], F32, at=AOFF + 32768)
    xqt = place("xqt", [128, 1024], BF16, at=AOFF + 36864)
    wch = place("wch", [128, 2, 4096], F32, at=AOFF)
    ternb = place("ternb", [128, 4096], BF16, at=AOFF + 32768)
    silu_f1 = place("silu_f1", [128, 512], F32, at=AOFF + 24576)
    rt1p = [place("rt1a", [128, 1024], F32, at=AOFF),
            place("rt1b", [128, 1024], F32, at=AOFF + 4096)]
    rf1p = [place("rf1a", [128, 1024], F32, at=AOFF + 8192),
            place("rf1b", [128, 1024], F32, at=AOFF + 12288)]
    ri1q = place("ri1q", [128, 1024], I32, at=AOFF + 16384)
    ra1q = place("ra1q", [128, 1024], F32, at=AOFF + 20480)
    guq = place("guq", [128, 4096], BF16, at=AOFF + 26624)
    guqT = place("guqT", [128, 16, 128], BF16, at=AOFF + 34816)
    guqs = place("guqs", [128, 2, 32, 128], BF16, at=AOFF)
    x2t = place("x2t", [128, 2, 1024], F32, at=AOFF + 16384)
    fsb = place("fsb", [128, 1024], F32, at=AOFF + 24576)
    t3b = place("t3b", [128, 1024], F32, at=AOFF + 28672)
    off[0] = AOFF + 40960

    BOFF = off[0]
    qg = place("qg", [128, 8, 2048], BF16, at=BOFF)
    f0t = place("f0t", [128, 4096], F32, at=BOFF)
    f0bn = place("f0bn", [128, 4096], F32, at=BOFF + 16384)
    gu_lo = place("gu_lo", [128, 2, 4096], F32, at=BOFF)
    td_lo = place("td_lo", [128, 16, 1024], BF16, at=BOFF)
    off[0] = BOFF + 32768

    COFF = off[0]
    h_fm = place("h_fm", [128, 8, 2048], BF16, at=COFF)
    xq_fm = place("xq_fm", [128, 8, 2048], BF16, at=COFF)
    ysb = place("ysb", [128, 2, 1024], F32, at=COFF)
    off[0] = COFF + 32768

    DOFF = off[0]
    vtmp0 = place("vtmp0", [128, 2, 2048], BF16, at=DOFF)
    vs = place("vs", [128, 2, 2048], BF16, at=DOFF)
    kss = place("kss", [128, 2, 1024], BF16, at=DOFF + 8192)
    kis = place("kis", [128, 2, 8, 128], BF16, at=DOFF + 12288)
    gss = place("gss", [128, 2, 2048], BF16, at=DOFF + 16384)
    xcs = place("xcs", [128, 2, 1024], F32, at=DOFF + 24576)
    gu_hi = place("gu_hi", [128, 2, 4096], F32, at=DOFF)
    td_hi = place("td_hi", [128, 16, 1024], BF16, at=DOFF)
    kigrp = place("kigrp", [128, 8, GRP], BF16, at=DOFF)
    off[0] = DOFF + 32768

    assert off[0] <= 227328, f"sbuf overflow: {off[0]}"
    nc.sbuf_base = off[0]

    PS = [nc.alloc_psum_tensor(f"ps{i}", [128, 1024], F32) for i in range(4)]

    def psv(bank, lo=0, hi=512):
        return PS[bank // 2][:, 512 * (bank % 2) + lo: 512 * (bank % 2) + hi]

    bufs = {}

    def B(name):
        if name not in bufs:
            bufs[name] = Buf(name)
        return bufs[name]

    PB = [B(f"pb{i}") for i in range(8)]
    bW, bA, bB, bC = B("W"), B("A"), B("BB"), B("BC")
    bS = [B(f"S{d}") for d in range(8)]
    bST = B("stats")
    bCONST = B("const")
    bSC = B("scores")
    bKIG = B("kigrp")
    bT1 = B("t1")
    bTST = B("ternTst")
    bXQ = B("xq")
    bD2 = [B("D0"), B("D1")]
    bTG2 = [B("TG0"), B("TG1")]
    bF2 = [B("F20"), B("F21")]
    bOUT = B("out")
    bSP = {n: B("sp_" + n) for n in
           ("ki", "ks", "v", "gs", "x2", "tg", "tu", "td", "gq")}
    bSPGQ = [B("spgq%d" % _t) for _t in range(NT)]

    def dma(queue, out, in_, reads, writes):
        eng = QISSUER[queue]
        attr = {"Y": "sync", "S": "scalar", "G": "gpsimd"}[eng]

        def bld(nc, out=out, in_=in_, attr=attr):
            return getattr(nc, attr).dma_start(out=out, in_=in_)
        P.emit(eng, bld, reads=reads, writes=writes, queue=queue)

    def mm(out_ap, lhsT, rhs, start, stop, reads, writes):
        def bld(nc, o=out_ap, l=lhsT, r=rhs, st=start, sp=stop):
            return nc.tensor.matmul(o, l, r, start=st, stop=sp)
        P.emit("T", bld, reads=reads, writes=writes)

    _LUT_FUNCS = (AF.Sqrt, AF.Sigmoid, AF.Ln, AF.Exp, AF.Silu)

    def act(out_ap, in_ap, func, reads, writes, scale=None, bias=None,
            accum=None, fd=1024):
        def bld(nc, o=out_ap, i=in_ap, f=func, s=scale, b=bias, a=accum):
            kw = {}
            if s is not None:
                kw["scale"] = s
            if b is not None:
                kw["bias"] = b
            if a is not None:
                kw["accum_out"] = a
            if f in _LUT_FUNCS:
                nc.scalar.preload_act_table(f)
            return nc.scalar.activation(o, i, f, **kw)
        P.emit("S", bld, reads=reads, writes=writes, fd=fd)

    def scopy(out_ap, in_ap, reads, writes, fd=512):
        def bld(nc, o=out_ap, i=in_ap):
            return nc.scalar.copy(o, i)
        P.emit("S", bld, reads=reads, writes=writes, fd=fd)

    def vop(method, args, reads, writes, fd, eng="V"):
        def bld(nc, m=method, a=args, e=eng):
            eo = nc.vector if e == "V" else nc.gpsimd
            return getattr(eo, m)(*a)
        P.emit(eng, bld, reads=reads, writes=writes, fd=fd)

    def tap(label, ap, shape, dt=None, reads=()):
        if label not in _TAPS:
            return
        dt = dt or ap.dtype
        dbg = nc.dram_tensor("dbg_" + label, list(shape), dt,
                             kind="ExternalOutput").ap()
        dma("QO", dbg, ap, list(reads), [B("dbg_" + label)])

    def ln_finish(csum, cssq, cmean, crstd, cscr, n):
        vop("tensor_scalar_mul", (cst[:, cmean:cmean + 1], cst[:, csum:csum + 1],
            1.0 / n), [bST], [bST], fd=1)
        vop("tensor_mul", (cst[:, cscr:cscr + 1], cst[:, cmean:cmean + 1],
            cst[:, cmean:cmean + 1]), [bST], [bST], fd=1)
        vop("scalar_tensor_tensor",
            (cst[:, crstd:crstd + 1], cst[:, cssq:cssq + 1], 1.0 / n,
             cst[:, cscr:cscr + 1], ALU.mult, ALU.subtract), [bST], [bST], fd=1)
        vop("tensor_scalar_add", (cst[:, crstd:crstd + 1], cst[:, crstd:crstd + 1],
            1e-5), [bST], [bST], fd=1)
        act(cst[:, cscr:cscr + 1], cst[:, crstd:crstd + 1], AF.Sqrt,
            [bST], [bST], fd=1)
        vop("reciprocal", (cst[:, crstd:crstd + 1], cst[:, cscr:cscr + 1]),
            [bST], [bST], fd=1)

    dma("QS", cb[:, :], cbD, [], [bCONST])
    dma("QS", ident[:, :], identD, [], [bCONST])
    dma("QS", utri[:, :], utriD, [], [bCONST])
    dma("QS", onescol[:, :], onescolD, [], [bCONST])
    dma("QS", onesrow[:, :], onesrowD, [], [bCONST])
    for d in range(8):
        dma("QS", bgk_sb[:, d:d + 1], bgk2[128 * d:128 * (d + 1)].unsqueeze(1),
            [], [bCONST])
    dma("QS", gnw_sb[:, :], gnw.unsqueeze(0), [], [bCONST])
    dma("QG", wgk1b[:, :, :], Wgk1.rearrange("(a p) f -> p a f", p=128), [], [bCONST])
    dma("QS", allones[:, :], allonesD, [], [bCONST])
    dma("QG", wgk2b[:, :], Wgk2, [], [bCONST])
    mm(psv(4), onesrow[:, :], gnw_sb[:, :], True, True, [bCONST], [PB[4]])
    vop("tensor_copy", (gnwb[:, :], psv(4)), [PB[4], bCONST], [bCONST], fd=512)
    vop("memset", (S_sb[:, :, :], 0.0), [], list(bS), fd=4096)

    bXLD = [B("xldA"), B("xldB")]
    bHT = [B("htA"), B("htB")]

    def emit_ln_h(xdram, tapit=False):
        for ch in range(4):
            pi = ch % 2
            xt = xld if pi == 0 else xldB
            dma("QS", xt[:, :, :],
                xdram[512 * ch:512 * (ch + 1), :].rearrange("(a p) f -> p a f", p=128),
                [], [bXLD[pi], bA] if ch < 2 else [bXLD[pi]])
            for tt in range(4):
                t = 4 * ch + tt
                vop("reduce_sum", (sums[:, t:t + 1], xt[:, tt, :], AX.X),
                    [bXLD[pi]], [bST], fd=1)
                act(junkA[:, :], xt[:, tt, :], AF.Square, [bXLD[pi]], [bA, bST],
                    accum=sumsq[:, t:t + 1])
        if tapit:
            tap("sumsq", sumsq[:, :], [128, 16], reads=[bST])
        vop("tensor_scalar_mul", (meanb[:, :], sums[:, :], 1.0 / D), [bST], [bST], fd=16)
        vop("tensor_mul", (nta[:, :], meanb[:, :], meanb[:, :]), [bST], [bST], fd=16)
        if tapit:
            tap("nta", nta[:, :], [128, 16], reads=[bST])
        vop("scalar_tensor_tensor",
            (aepsb[:, :], sumsq[:, :], 1.0 / D, nta[:, :], ALU.mult, ALU.subtract),
            [bST], [bST], fd=16)
        vop("tensor_scalar_add", (aepsb[:, :], aepsb[:, :], 1e-5), [bST], [bST], fd=16)
        act(nta[:, :], aepsb[:, :], AF.Sqrt, [bST], [bST], fd=16)
        vop("reciprocal", (rstdb[:, :], nta[:, :]), [bST], [bST], fd=16)
        vop("tensor_mul", (sums[:, :], rstdb[:, :], rstdb[:, :]), [bST], [bST], fd=16)
        vop("tensor_mul", (sums[:, :], sums[:, :], aepsb[:, :]), [bST], [bST], fd=16)
        vop("tensor_scalar", (sums[:, :], sums[:, :], -0.5, 1.5, ALU.mult, ALU.add),
            [bST], [bST], fd=16)
        vop("tensor_mul", (rstdb[:, :], rstdb[:, :], sums[:, :]), [bST], [bST], fd=16)
        if tapit:
            tap("aeps", aepsb[:, :], [128, 16], reads=[bST])
            tap("rstd", rstdb[:, :], [128, 16], reads=[bST])
            tap("meanb", meanb[:, :], [128, 16], reads=[bST])
        for ch in range(4):
            pi = ch % 2
            xt = xld if pi == 0 else xldB
            dma("QS", xt[:, :, :],
                xdram[512 * ch:512 * (ch + 1), :].rearrange("(a p) f -> p a f", p=128),
                [], [bXLD[pi]])
            for tt in range(4):
                t = 4 * ch + tt
                hh = tt % 2
                ht = htmp if hh == 0 else htmpB
                hw = [bHT[hh], bA] if (ch == 0 and tt < 2) else [bHT[hh]]
                vop("tensor_scalar",
                    (ht[:, :], xt[:, tt, :], meanb[:, t:t + 1], rstdb[:, t:t + 1],
                     ALU.subtract, ALU.mult), [bXLD[pi], bST], hw, fd=1024)
                for f in range(8):
                    bank = 4 + (f // 4)
                    mm(psv(bank, 128 * (f % 4), 128 * (f % 4) + 128),
                       ht[:, 128 * f:128 * (f + 1)], ident[:, :], True, True,
                       [bHT[hh], bCONST], [PB[bank]])
                for half in range(2):
                    bank = 4 + half
                    scopy(h_fm[:, 4 * half:4 * half + 4, 128 * t:128 * (t + 1)],
                          psv(bank).rearrange("p (a f) -> p a f", a=4),
                          [PB[bank]], [bC])
        vop("tensor_scalar_mul", (cst[:, 9:10], cst[:, 9:10], 1.0),
            [], [bA, bXLD[0], bXLD[1], bHT[0], bHT[1]], fd=1)

    def emit_gk(g, want_q):
        cols = slice(GRP * g, GRP * (g + 1))
        for it in range(NI):
            mm(PS[2][:, 0:GRP], wgk1b[:, it, :], h_fm[:, it, cols],
               it == 0, it == NI - 1, [bCONST, bC], [PB[4]])
        scopy(t1b[:, :], PS[2][0:16, 0:GRP], [PB[4]], [bT1], fd=GRP)
        for d in range(NI):
            bank = 2 + (d % 2)
            mm(psv(bank, 0, GRP), wgk2b[:, 128 * d:128 * (d + 1)], t1b[:, :],
               True, True, [bCONST, bT1], [PB[bank]])
            act(U0[:, d, :], psv(bank, 0, GRP), AF.Sigmoid, [PB[bank], bCONST],
                [bA], bias=bgk_sb[:, d:d + 1], fd=GRP)
        vop("tensor_scalar_max", (U0[:, :, :], U0[:, :, :], 1e-38), [bA], [bA],
            fd=8 * GRP)
        if g == 0 and want_q:
            tap("sg0", U0[:, :, :], [128, 8, GRP], reads=[bA])
        act(U1[:, :, :], U0[:, :, :], AF.Ln, [bA], [bA], fd=8 * GRP)
        for d in range(NI):
            for lc in range(GRP // 128):
                si = U1[:, d, 128 * lc:128 * (lc + 1)]
                so = U0[:, d, 128 * lc:128 * (lc + 1)]
                vop("tensor_tensor_scan", (so, si, si, 0.0, ALU.add, ALU.bypass),
                    [bA], [bA], fd=128)
        for lc in range(GRP // 128):
            vop("tensor_copy", (blastb[:, :, lc:lc + 1], U0[:, :, 128 * lc + 127:128 * lc + 128]),
                [bA], [bST], fd=8)
        if g == 0 and want_q:
            tap("ls0", U1[:, :, :], [128, 8, GRP], reads=[bA])
            tap("b0", U0[:, :, :], [128, 8, GRP], reads=[bA])
            tap("blast0", blastb[:, :, :], [128, 8, 2], reads=[bST])
        act(expbl[:, :, 2 * g:2 * g + 2], blastb[:, :, :], AF.Exp,
            [bST, bCONST], [bST], scale=cb[:, C_P16:C_P16 + 1], fd=16)
        if want_q:
            act(U2[:, :, :], U0[:, :, :], AF.Exp, [bA, bCONST], [bA],
                scale=cb[:, C_P16:C_P16 + 1], bias=cb[:, C_NLN16:C_NLN16 + 1],
                fd=8 * GRP)
            act(U3[:, :, :], U0[:, :, :], AF.Exp, [bA, bCONST], [bA],
                scale=cb[:, C_N16:C_N16 + 1], fd=8 * GRP)
        for d in range(NI):
            for lc in range(GRP // 128):
                vop("tensor_scalar",
                    (U1[:, d, 128 * lc:128 * (lc + 1)],
                     U0[:, d, 128 * lc:128 * (lc + 1)],
                     blastb[:, d, lc:lc + 1], None, ALU.subtract),
                    [bA, bST], [bA], fd=128)
        act(U0[:, :, :], U1[:, :, :], AF.Exp, [bA, bCONST], [bA],
            scale=cb[:, C_N16:C_N16 + 1], fd=8 * GRP)

    def emit_ks_transpose_group():
        for lc in range(2):
            for dh in range(2):
                bank = 4 + dh
                for dd in range(4):
                    d = 4 * dh + dd
                    mm(psv(bank, 128 * dd, 128 * dd + 128),
                       ksfm[:, d, 128 * lc:128 * (lc + 1)], ident[:, :], True, True,
                       [bA, bCONST], [PB[bank]])
                scopy(kstm[:, lc, 512 * dh:512 * (dh + 1)], psv(bank),
                      [PB[bank]], [bA])

    emit_ln_h(x_prev)
    wv_p0 = qg
    dma("QG", wqk[:, 0:8, :], Wk.rearrange("(a p) f -> p a f", p=128), [bW], [bW])
    dma("QG", wv_p0[:, :, :], Wv.rearrange("(a p) f -> p a f", p=128), [bB], [bB])
    for g in range(NG):
        cols = slice(GRP * g, GRP * (g + 1))
        emit_gk(g, want_q=False)
        for d in range(NI):
            bank = 0 + (d % 2)
            for it in range(NI):
                mm(psv(bank, 0, GRP), wqk[:, it, 128 * d:128 * (d + 1)],
                   h_fm[:, it, cols], it == 0, it == NI - 1, [bW, bC], [PB[bank]])
            vop("tensor_mul", (ksfm[:, d, :], psv(bank, 0, GRP), U0[:, d, :]),
                [PB[bank], bA], [bA], fd=GRP)
        emit_ks_transpose_group()
        for tt in range(2):
            for dvg in range(4):
                bank = 2 + (dvg % 2)
                for it in range(NI):
                    mm(psv(bank),
                       h_fm[:, it, GRP * g + 128 * tt: GRP * g + 128 * (tt + 1)],
                       wv_p0[:, it, 512 * dvg:512 * (dvg + 1)],
                       it == 0, it == NI - 1, [bB, bC], [PB[bank]])
                scopy(vtmp0[:, tt, 512 * dvg:512 * (dvg + 1)], psv(bank),
                      [PB[bank]], [bD2[0]])
        for lc in range(2):
            c = 2 * g + lc
            for hh in range(NH):
                for j in range(2):
                    dd = 2 * hh + j
                    bank = 6 + j
                    mm(psv(bank),
                       kstm[:, lc, 256 * hh + 128 * j: 256 * hh + 128 * (j + 1)],
                       vtmp0[:, lc, 512 * hh:512 * (hh + 1)], True, True,
                       [bA, bD2[0], bS[dd]], [PB[bank]])
                    vop("scalar_tensor_tensor",
                        (S_sb[:, dd, :], S_sb[:, dd, :], expbl[:, dd, c:c + 1],
                         psv(bank), ALU.mult, ALU.add),
                        [bS[dd], bST, PB[bank]], [bS[dd]], fd=512)

    emit_ln_h(x_own, tapit=True)
    tap("h", h_fm[:, :, :], [128, 8, 2048], reads=[bC])

    dma("QG", wqk[:, 0:8, :], Wq.rearrange("(a p) f -> p a f", p=128), [bW], [bW])
    dma("QG", wqk[:, 8:16, :], Wk.rearrange("(a p) f -> p a f", p=128), [bW], [bW])
    for g in range(NG):
        cols = slice(GRP * g, GRP * (g + 1))
        emit_gk(g, want_q=True)
        for d in range(NI):
            bank = 0 + (d % 2)
            for it in range(NI):
                mm(psv(bank, 0, GRP), wqk[:, it, 128 * d:128 * (d + 1)],
                   h_fm[:, it, cols], it == 0, it == NI - 1, [bW, bC], [PB[bank]])
            vop("tensor_mul", (qg[:, d, cols], psv(bank, 0, GRP), U2[:, d, :]),
                [PB[bank], bA], [bB], fd=GRP)
            bank2 = 2 + (d % 2)
            for it in range(NI):
                mm(psv(bank2, 0, GRP), wqk[:, 8 + it, 128 * d:128 * (d + 1)],
                   h_fm[:, it, cols], it == 0, it == NI - 1, [bW, bC], [PB[bank2]])
            vop("tensor_mul", (kigrp[:, d, :], psv(bank2, 0, GRP), U3[:, d, :]),
                [PB[bank2], bA], [bKIG, bD2[0]], fd=GRP)
            vop("tensor_mul", (ksfm[:, d, :], psv(bank2, 0, GRP), U0[:, d, :]),
                [PB[bank2], bA], [bA], fd=GRP)

        for lc in range(2):
            def ki_out(nc, g=g, lc=lc):
                return nc.scalar.dma_start(
                    out=SP_KI[2 * g + lc, :, :, :].rearrange("a p t -> p a t"),
                    in_=kigrp[:, :, 128 * lc:128 * (lc + 1)])
            P.emit("S", ki_out, reads=[bKIG, bD2[0]], writes=[bSP["ki"]], queue="QA")
        emit_ks_transpose_group()

        def ks_out(nc, g=g):
            return nc.scalar.dma_start(
                out=SP_KS[2 * g:2 * g + 2, :, :].rearrange("l p d -> p l d"),
                in_=kstm[:, :, :])
        P.emit("S", ks_out, reads=[bA], writes=[bSP["ks"]], queue="QA")

    tap("qg", qg[:, :, :], [128, 8, 2048], reads=[bB])
    tap("expbl", expbl[:, :, :], [128, 8, 16], reads=[bST])
    dma("QG", wvb[:, :, :], Wv.rearrange("(a p) f -> p a f", p=128), [bW], [bW])
    for g in range(NG):
        for tt in range(2):
            for dvg in range(4):
                bank = (dvg % 2)
                for it in range(NI):
                    mm(psv(bank),
                       h_fm[:, it, GRP * g + 128 * tt: GRP * g + 128 * (tt + 1)],
                       wvb[:, it, 512 * dvg:512 * (dvg + 1)],
                       it == 0, it == NI - 1, [bW, bC], [PB[bank]])
                scopy(vtmp[:, tt, 512 * dvg:512 * (dvg + 1)], psv(bank),
                      [PB[bank]], [bA])

        def v_out(nc, g=g):
            return nc.scalar.dma_start(
                out=SP_V[2 * g:2 * g + 2, :, :].rearrange("l p d -> p l d"),
                in_=vtmp[:, :, :])
        P.emit("S", v_out, reads=[bA], writes=[bSP["v"]], queue="QA")

    dma("QG", wgb[:, :, :], Wg.rearrange("(a p) f -> p a f", p=128), [bW], [bW])
    for g in range(NG):
        for tt in range(2):
            for hh in range(4):
                bank = (hh % 2)
                for it in range(NI):
                    mm(psv(bank),
                       h_fm[:, it, GRP * g + 128 * tt: GRP * g + 128 * (tt + 1)],
                       wgb[:, it, 512 * hh:512 * (hh + 1)],
                       it == 0, it == NI - 1, [bW, bC], [PB[bank]])
                act(silu_t[:, :], psv(bank), AF.Silu, [PB[bank]], [bA], fd=512)
                vop("tensor_mul", (vtmp[:, tt, 512 * hh:512 * (hh + 1)],
                                   silu_t[:, :], gnwb[:, :]),
                    [bA, bCONST], [bA], fd=512)

        def gs_out(nc, g=g):
            return nc.scalar.dma_start(
                out=SP_GS[2 * g:2 * g + 2, :, :].rearrange("l p d -> p l d"),
                in_=vtmp[:, :, :])
        P.emit("S", gs_out, reads=[bA], writes=[bSP["gs"]], queue="QA")

    dma("QG", wob[:, :, :], Wo.rearrange("(a p) f -> p a f", p=128), [bW], [bW])
    for dd in range(8):
        tap(f"Sprev{dd}", S_sb[:, dd, :], [128, 512], reads=[bS[dd]])
        scopy(S_bf[:, dd, :], S_sb[:, dd, :], [bS[dd]], [bS[dd]])

    def c_prefetch(c):
        i = c % 2
        dma("QS", vs[:, i, :], SP_V[c, :, :], [bSP["v"]], [bD2[i]])
        dma("QS", kss[:, i, :], SP_KS[c, :, :], [bSP["ks"]], [bD2[i]])
        dma("QS", kis[:, i, :, :], SP_KI[c, :, :, :].rearrange("a p t -> p a t"),
            [bSP["ki"]], [bD2[i]])
        dma("QS", gss[:, i, :], SP_GS[c, :, :], [bSP["gs"]], [bD2[i]])
        dma("QS", xcs[:, i, :], x_own[128 * c:128 * (c + 1), :], [], [bD2[i]])

    c_prefetch(0)
    for c in range(NCH):
        i = c % 2
        if c + 1 < NCH:
            c_prefetch(c + 1)
        ccols = slice(128 * c, 128 * (c + 1))
        for hh in range(NH):
            for j in range(2):
                dd = 2 * hh + j
                mm(PS[0][:, 0:128], kis[:, i, dd, :], qg[:, dd, ccols],
                   j == 0, j == 1, [bD2[i], bB], [PB[0]])
            vop("tensor_mul", (sc_sb[:, :], PS[0][:, 0:128], utri[:, :]),
                [PB[0], bCONST], [bSC], fd=128)
            mm(psv(1), sc_sb[:, :], vs[:, i, 512 * hh:512 * (hh + 1)],
               True, False, [bSC, bD2[i]], [PB[1]])
            for j in range(2):
                dd = 2 * hh + j
                mm(psv(1), qg[:, dd, ccols], S_bf[:, dd, :],
                   False, j == 1, [bB, bS[dd]], [PB[1]])
            vop("tensor_copy", (osb[:, 512 * hh:512 * (hh + 1)], psv(1)),
                [PB[1]], [bA], fd=512)
            for j in range(2):
                dd = 2 * hh + j
                bank = 2 + j
                mm(psv(bank),
                   kss[:, i, 256 * hh + 128 * j: 256 * hh + 128 * (j + 1)],
                   vs[:, i, 512 * hh:512 * (hh + 1)], True, True,
                   [bD2[i], bS[dd]], [PB[bank]])
                vop("scalar_tensor_tensor",
                    (S_sb[:, dd, :], S_sb[:, dd, :], expbl[:, dd, c:c + 1],
                     psv(bank), ALU.mult, ALU.add),
                    [bS[dd], bST, PB[bank]], [bS[dd]], fd=512)
                scopy(S_bf[:, dd, :], S_sb[:, dd, :], [bS[dd]], [bS[dd]])
            act(junkC[:, :], osb[:, 512 * hh:512 * (hh + 1)], AF.Square,
                [bA], [bA, bST], accum=ssq4[:, hh:hh + 1], fd=512)
        if c == 0:
            tap("osb0", osb[:, :], [128, 2048], reads=[bA])
            tap("ssq40", ssq4[:, :], [128, 4], reads=[bST])
        vop("tensor_scalar", (a4[:, :], ssq4[:, :], 1.0 / 512, 1e-5,
            ALU.mult, ALU.add), [bST], [bST], fd=4)
        act(sd4[:, :], a4[:, :], AF.Sqrt, [bST], [bST], fd=4)
        vop("reciprocal", (rs4[:, :], sd4[:, :]), [bST], [bST], fd=4)
        vop("tensor_mul", (n4[:, :], rs4[:, :], rs4[:, :]), [bST], [bST], fd=4)
        vop("tensor_mul", (n4[:, :], n4[:, :], a4[:, :]), [bST], [bST], fd=4)
        vop("tensor_scalar", (n4[:, :], n4[:, :], -0.5, 1.5, ALU.mult, ALU.add),
            [bST], [bST], fd=4)
        vop("tensor_mul", (rs4[:, :], rs4[:, :], n4[:, :]), [bST], [bST], fd=4)
        for hh in range(NH):
            vop("scalar_tensor_tensor",
                (oft[:, 512 * hh:512 * (hh + 1)], osb[:, 512 * hh:512 * (hh + 1)],
                 rs4[:, hh:hh + 1], gss[:, i, 512 * hh:512 * (hh + 1)],
                 ALU.mult, ALU.mult), [bA, bST, bD2[i]], [bA], fd=512)
        for b4 in range(4):
            bank = 4 + (b4 % 2)
            for dd in range(4):
                dvt = 4 * b4 + dd
                mm(psv(bank, 128 * dd, 128 * dd + 128),
                   oft[:, 128 * dvt:128 * (dvt + 1)], ident[:, :], True, True,
                   [bA, bCONST], [PB[bank]])
            scopy(oTs[:, 4 * b4:4 * b4 + 4, :],
                  psv(bank).rearrange("p (a f) -> p a f", a=4), [PB[bank]], [bA])
        for dh in range(2):
            for dvt in range(16):
                mm(PS[3][:, 512 * dh:512 * (dh + 1)], oTs[:, dvt, :],
                   wob[:, dvt, 512 * dh:512 * (dh + 1)],
                   dvt == 0, dvt == 15, [bA, bW], [PB[6 + dh]])
        vop("tensor_copy", (asb[:, :], PS[3][:, :]), [PB[6], PB[7]], [bA], fd=1024)
        if c == 0:
            tap("asb0", asb[:, :], [128, 1024], reads=[bA])
            tap("oft0", oft[:, :], [128, 2048], reads=[bA])
        vop("reduce_sum", (cst[:, 0:1], asb[:, :], AX.X), [bA], [bST], fd=1)
        act(junkC[:, :], asb[:, 0:512], AF.Square, [bA], [bA, bST],
            accum=cst[:, 1:2], fd=512)
        act(junkC[:, :], asb[:, 512:1024], AF.Square, [bA], [bA, bST],
            accum=cst[:, 2:3], fd=512)
        vop("tensor_add", (cst[:, 1:2], cst[:, 1:2], cst[:, 2:3]), [bST], [bST], fd=1)
        ln_finish(0, 1, 3, 4, 5, D)
        vop("tensor_mul", (cst[:, 5:6], cst[:, 3:4], cst[:, 4:5]), [bST], [bST], fd=1)
        vop("tensor_scalar_mul", (cst[:, 5:6], cst[:, 5:6], -1.0), [bST], [bST], fd=1)
        act(t2x2[:, :], asb[:, :], AF.Identity, [bA, bST], [bA],
            scale=cst[:, 4:5], bias=cst[:, 5:6])
        vop("tensor_add", (t2x2[:, :], t2x2[:, :], xcs[:, i, :]),
            [bA, bD2[i]], [bA], fd=1024)

        if c == 0:
            tap("x20", t2x2[:, :], [128, 1024], reads=[bA])
        def x2_out(nc, c=c):
            return nc.scalar.dma_start(out=SP_X2[c, :, :], in_=t2x2[:, :])
        P.emit("S", x2_out, reads=[bA], writes=[bSP["x2"]], queue="QA")
        vop("reduce_sum", (cst[:, 6:7], t2x2[:, :], AX.X), [bA], [bST], fd=1)
        act(junkC[:, :], t2x2[:, 0:512], AF.Square, [bA], [bA, bST],
            accum=cst[:, 7:8], fd=512)
        act(junkC[:, :], t2x2[:, 512:1024], AF.Square, [bA], [bA, bST],
            accum=cst[:, 8:9], fd=512)
        vop("tensor_add", (cst[:, 7:8], cst[:, 7:8], cst[:, 8:9]), [bST], [bST], fd=1)
        ln_finish(6, 7, 9, 10, 11, D)
        vop("tensor_scalar", (h2b[:, :], t2x2[:, :], cst[:, 9:10], cst[:, 10:11],
            ALU.subtract, ALU.mult), [bA, bST], [bA], fd=1024)

        if c == 0:
            tap("h20", h2b[:, :], [128, 1024], reads=[bA])
            tap("cst0", cst[:, 0:11], [128, 11], reads=[bST])

        def redmax(nc):
            return nc.vector.reduce_max(cst[:, 11:12], h2b[:, :], axis=AX.X,
                                        apply_absolute_value=True)
        P.emit("V", redmax, reads=[bA], writes=[bST], fd=1)
        vop("tensor_scalar_max", (cst[:, 11:12], cst[:, 11:12], 1e-5),
            [bST], [bST], fd=1)
        vop("tensor_scalar_mul", (sinv[:, c:c + 1], cst[:, 11:12], 1.0 / 127.0),
            [bST], [bST], fd=1)
        vop("reciprocal", (cst[:, 11:12], sinv[:, c:c + 1]), [bST], [bST], fd=1)
        act(rt[:, :], h2b[:, :], AF.Identity, [bA, bST], [bA], scale=cst[:, 11:12])
        act(ri[:, :], h2b[:, :], AF.Identity, [bA, bST], [bA],
            scale=cst[:, 11:12])
        act(rf[:, :], ri[:, :], AF.Identity, [bA], [bA])
        vop("tensor_sub", (rt[:, :], rt[:, :], rf[:, :]), [bA], [bA], fd=1024)
        vop("tensor_single_scalar", (ra[:, :], rt[:, :], 0.5, ALU.is_gt),
            [bA], [bA], fd=1024)
        for _k in (1.5, 2.5, 3.5):
            vop("scalar_tensor_tensor", (ra[:, :], rt[:, :], _k, ra[:, :],
                ALU.is_gt, ALU.add), [bA], [bA], fd=1024)
        vop("scalar_tensor_tensor", (ra[:, :], rt[:, :], -0.5, ra[:, :],
            ALU.is_lt, ALU.subtract), [bA], [bA], fd=1024)
        for _k in (-1.5, -2.5, -3.5):
            vop("scalar_tensor_tensor", (ra[:, :], rt[:, :], _k, ra[:, :],
                ALU.is_lt, ALU.add), [bA], [bA], fd=1024)
        vop("tensor_sub", (xqt[:, :], rf[:, :], ra[:, :]), [bA], [bA], fd=1024)
        if c == 0:
            tap("xqt0", xqt[:, :], [128, 1024], reads=[bA])
        for half in range(2):
            bank = 4 + half
            for dd in range(4):
                f = 4 * half + dd
                mm(psv(bank, 128 * dd, 128 * dd + 128),
                   xqt[:, 128 * f:128 * (f + 1)], ident[:, :], True, True,
                   [bA, bCONST], [PB[bank]])
            scopy(xq_fm[:, 4 * half:4 * half + 4, 128 * c:128 * (c + 1)],
                  psv(bank).rearrange("p (a f) -> p a f", a=4),
                  [PB[bank]], [bXQ, bC])

    bWC = [B("wc0"), B("wc1")]
    bF0T2 = [B("f0tb0"), B("f0tb1")]
    bF0B2 = [B("f0bb0"), B("f0bb1")]
    bTRN2 = [B("trnb0"), B("trnb1")]
    bTST2 = [bTST, B("ternTstB")]
    ternTst2 = [ternTst, ternTstB]
    _f0first = set()

    def f0w(key, bufs, extra):
        if key in _f0first:
            return bufs
        _f0first.add(key)
        return bufs + extra

    def emit_wquant(wdram, O, IW, mcol, spill, spkey):
        rows = 524288 // IW
        nchunks = O // rows
        n_it = IW // 128
        n_ot = rows // 128
        for ch in range(nchunks):
            pi = ch % 2
            src = wdram[rows * ch: rows * (ch + 1), :]
            if n_ot > 1:
                src = src.rearrange("(a p) f -> p a f", p=128)
                dst = wch[:, pi, :].rearrange("p (a f) -> p a f", a=n_ot)
            else:
                dst = wch[:, pi, :]
            dma("QS" if pi == 0 else "QG", dst, src, [],
                f0w("wc%d" % pi, [bWC[pi]], [bA]))

            def red(nc, pi=pi, ch=ch):
                return nc.vector.reduce_sum(part4[:, ch:ch + 1], wch[:, pi, :],
                                            axis=AX.X, apply_absolute_value=True)
            P.emit("V", red, reads=[bWC[pi]], writes=[bST], fd=1)
        vop("reduce_sum", (tot1[:, :], part4[:, 0:nchunks], AX.X), [bST], [bST], fd=1)
        mm(PS[2][:, 0:1], allones[:, :], tot1[:, :], True, True,
           [bCONST, bST], [PB[4]])
        vop("tensor_copy", (mbc[:, mcol:mcol + 1], PS[2][:, 0:1]), [PB[4]], [bST], fd=1)
        vop("tensor_scalar_mul", (mbc[:, mcol:mcol + 1], mbc[:, mcol:mcol + 1],
            1.0 / (O * IW)), [bST], [bST], fd=1)
        vop("tensor_scalar_max", (mbc[:, mcol:mcol + 1], mbc[:, mcol:mcol + 1],
            1e-5), [bST], [bST], fd=1)
        vop("reciprocal", (mibc[:, mcol:mcol + 1], mbc[:, mcol:mcol + 1]),
            [bST], [bST], fd=1)
        for ch in range(nchunks):
            pi = ch % 2
            src = wdram[rows * ch: rows * (ch + 1), :]
            if n_ot > 1:
                src = src.rearrange("(a p) f -> p a f", p=128)
                dst = wch[:, pi, :].rearrange("p (a f) -> p a f", a=n_ot)
            else:
                dst = wch[:, pi, :]
            dma("QS" if pi == 0 else "QG", dst, src, [], [bWC[pi]])
            for h in range(2):
                cs = slice(2048 * h, 2048 * (h + 1))
                act(f0t[:, cs], wch[:, pi, cs], AF.Identity, [bWC[pi], bST],
                    f0w("f0t%d" % h, [bF0T2[h]], [bB]),
                    scale=mibc[:, mcol:mcol + 1], fd=2048)
                vop("tensor_single_scalar", (f0bn[:, cs], f0t[:, cs], -0.5,
                    ALU.is_lt), [bF0T2[h]], f0w("f0b%d" % h, [bF0B2[h]], [bB]),
                    fd=2048)
                vop("scalar_tensor_tensor", (ternb[:, cs], f0t[:, cs], 0.5,
                    f0bn[:, cs], ALU.is_gt, ALU.subtract),
                    [bF0T2[h], bF0B2[h]], f0w("trn%d" % h, [bTRN2[h]], [bA]),
                    fd=2048)
            for ot in range(n_ot):
                oglob = n_ot * ch + ot
                for itb in range(n_it // 4):
                    bank = 4 + (itb % 2)
                    ti2 = (ot * (n_it // 4) + itb) % 2
                    hh = (IW * ot + 512 * itb) // 2048
                    for dd in range(4):
                        it = 4 * itb + dd
                        mm(psv(bank, 128 * dd, 128 * dd + 128),
                           ternb[:, IW * ot + 128 * it: IW * ot + 128 * (it + 1)],
                           ident[:, :], True, True, [bTRN2[hh], bCONST],
                           [PB[bank]])
                    if ti2 == 0:
                        scopy(ternTst2[0][:, :, :],
                              psv(bank).rearrange("p (a f) -> p a f", a=4),
                              [PB[bank]], f0w("tst0", [bTST2[0]], [bCONST, bA]))
                    else:
                        vop("tensor_copy",
                            (ternTst2[1][:, :, :],
                             psv(bank).rearrange("p (a f) -> p a f", a=4)),
                            [PB[bank]], f0w("tst1", [bTST2[1]], [bCONST, bA]),
                            fd=512)

                    def tout(nc, itb=itb, oglob=oglob, spill=spill, ti2=ti2):
                        return nc.scalar.dma_start(
                            out=spill[4 * itb:4 * itb + 4, :,
                                      128 * oglob:128 * (oglob + 1)]
                            .rearrange("a p t -> p a t"),
                            in_=ternTst2[ti2][:, :, :])
                    P.emit("S", tout, reads=[bTST2[ti2]], writes=[bSP[spkey]],
                           queue="QA")

    emit_wquant(wgate, INNER, D, 0, SP_TG, "tg")
    emit_wquant(wup, INNER, D, 1, SP_TU, "tu")
    emit_wquant(wdown, D, INNER, 2, SP_TD, "td")

    tap("mbc", mbc[:, 0:3], [128, 3], reads=[bST])
    tap("sinv", sinv[:, :], [128, 16], reads=[bST])
    vop("tensor_scalar_mul", (dqg[:, :], sinv[:, :], mbc[:, 0:1]), [bST], [bST], fd=16)
    vop("tensor_scalar_mul", (dqu[:, :], sinv[:, :], mbc[:, 1:2]), [bST], [bST], fd=16)

    bGU = [bB, B("GUHI")]
    bRT = [B("rt1a"), B("rt1b")]
    bRF = [B("rf1a"), B("rf1b")]
    bRI, bRA = B("ri1q"), B("ra1q")
    bGUQ, bGUT, bSIL = B("guq1"), B("guT1"), B("silF1")
    _first = set()

    def fwr(key, bufs, extra):
        if key in _first:
            return bufs
        _first.add(key)
        return bufs + extra

    def round_pieces(t):
        reg = (t // 2) % 2
        gu_t = gu_lo if reg == 0 else gu_hi
        gb = bGU[reg]
        sc = 10 + (t % 2)

        def p_rmax():
            def rmx(nc, gu_t=gu_t, ti=t % 2, sc=sc):
                return nc.vector.reduce_max(cst[:, sc:sc + 1], gu_t[:, ti, :],
                                            axis=AX.X, apply_absolute_value=True)
            P.emit("V", rmx, reads=[gb], writes=[bST], fd=1)
            vop("tensor_scalar_max", (cst[:, sc:sc + 1], cst[:, sc:sc + 1], 1e-5),
                [bST], [bST], fd=1)
            vop("tensor_scalar_mul", (s2inv[:, t:t + 1], cst[:, sc:sc + 1],
                1.0 / 127.0), [bST], [bST], fd=1)
            vop("reciprocal", (cst[:, sc:sc + 1], s2inv[:, t:t + 1]),
                [bST], [bST], fd=1)

        def p_acts(q):
            pi = q % 2
            qc = slice(1024 * q, 1024 * (q + 1))
            act(rt1p[pi][:, :], gu_t[:, t % 2, qc], AF.Identity,
                [gb, bST], fwr("rt%d" % pi, [bRT[pi]], [bA]),
                scale=cst[:, sc:sc + 1], bias=cb[:, 8:9], fd=1024)
            act(ri1q[:, :], gu_t[:, t % 2, qc], AF.Identity,
                [gb, bST], fwr("ri", [bRI], [bA]),
                scale=cst[:, sc:sc + 1], fd=1024)
            act(rf1p[pi][:, :], ri1q[:, :], AF.Identity,
                [bRI], fwr("rf%d" % pi, [bRF[pi]], [bA]), fd=1024)

        def p_ladder(q):
            pi = q % 2
            qc = slice(1024 * q, 1024 * (q + 1))
            vop("scalar_tensor_tensor", (ra1q[:, :], rf1p[pi][:, :], -1.0,
                rt1p[pi][:, :], ALU.mult, ALU.add),
                [bRF[pi], bRT[pi]], fwr("ra", [bRA], [bA]), fd=1024)
            for _k in (64.0, 32.0, 16.0, 8.0, 4.0, 2.0, 1.0):
                vop("tensor_single_scalar", (rf1p[pi][:, :], ra1q[:, :], _k,
                    ALU.is_ge), [bRA], [bRF[pi]], fd=1024)
                vop("scalar_tensor_tensor", (ra1q[:, :], rf1p[pi][:, :], -_k,
                    ra1q[:, :], ALU.mult, ALU.add), [bRF[pi]], [bRA], fd=1024)
            vop("scalar_tensor_tensor", (guq[:, qc], ra1q[:, :], -1.0,
                rt1p[pi][:, :], ALU.mult, ALU.add),
                [bRA, bRT[pi]], fwr("guq", [bGUQ], [bA]), fd=1024)
            vop("tensor_single_scalar", (guq[:, qc], guq[:, qc], -64.0, ALU.add),
                [bGUQ], [bGUQ], fd=1024)

        def p_trans():
            tap("guqt%d" % t, guq[:, :], [128, 4096], reads=[bGUQ])
            for half in range(2):
                for b4 in range(4 * half, 4 * half + 4):
                    bank = 4 + (b4 % 2)
                    for dd in range(4):
                        it = 4 * b4 + dd
                        mm(psv(bank, 128 * dd, 128 * dd + 128),
                           guq[:, 128 * it:128 * (it + 1)], ident[:, :],
                           True, True, [bGUQ, bCONST], [PB[bank]])
                    scopy(guqT[:, 4 * (b4 % 4):4 * (b4 % 4) + 4, :],
                          psv(bank).rearrange("p (a f) -> p a f", a=4),
                          [PB[bank]], fwr("guT", [bGUT], [bA]))

                def gq_out(nc, t=t, half=half):
                    return nc.scalar.dma_start(
                        out=SP_GQ[t, 16 * half:16 * (half + 1), :, :]
                        .rearrange("a p u -> p a u"),
                        in_=guqT[:, :, :])
                P.emit("S", gq_out, reads=[bGUT], writes=[bSPGQ[t]],
                       queue="QA")

        return [lambda: (p_rmax(), p_acts(0), p_acts(1)),
                lambda: (p_ladder(0), p_acts(2)),
                lambda: (p_ladder(1), p_acts(3)),
                lambda: p_ladder(2),
                lambda: p_ladder(3),
                p_trans]

    pieces = []
    DRAIN = [2, 2, 2, 2, 1, 1, 1, 1]
    for p in range(8):
        preg = p % 2
        gu_t = gu_lo if preg == 0 else gu_hi
        gkey = "gulo" if preg == 0 else "guhi"
        gext = [] if preg == 0 else [bD2[0], bD2[1]]
        for og in range(8):
            pi = og % 2
            wr_extra = [bW] if (p == 0 and og < 2) else []
            dma("QS", tgs[:, pi, :, :],
                SP_TG[:, :, 512 * og:512 * (og + 1)].rearrange("a p t -> p a t"),
                [bSP["tg"]], [bTG2[pi]] + wr_extra)
            dma("QG", tus[:, pi, :, :],
                SP_TU[:, :, 512 * og:512 * (og + 1)].rearrange("a p t -> p a t"),
                [bSP["tu"]], [bTG2[pi]] + wr_extra)
            for ti in range(2):
                t = 2 * p + ti
                gpb = psv(2 * ti)
                upb = psv(1 + 2 * ti)
                for it in range(NI):
                    mm(gpb, xq_fm[:, it, 128 * t:128 * (t + 1)],
                       tgs[:, pi, it, :], it == 0, it == NI - 1,
                       [bXQ, bTG2[pi]], [PB[2 * ti]])
                act(silu_f1[:, :], gpb, AF.Silu, [PB[2 * ti], bST],
                    fwr("sil", [bSIL], [bA]), scale=dqg[:, t:t + 1], fd=512)
                for it in range(NI):
                    mm(upb, xq_fm[:, it, 128 * t:128 * (t + 1)],
                       tus[:, pi, it, :], it == 0, it == NI - 1,
                       [bXQ, bTG2[pi]], [PB[1 + 2 * ti]])
                vop("scalar_tensor_tensor",
                    (gu_t[:, ti, 512 * og:512 * (og + 1)], upb,
                     dqu[:, t:t + 1], silu_f1[:, :], ALU.mult, ALU.mult),
                    [PB[1 + 2 * ti], bST, bSIL],
                    fwr(gkey, [bGU[preg]], gext), fd=512)
            for _ in range(DRAIN[og]):
                if pieces:
                    pieces.pop(0)()
        if p == 0:
            tap("guA", gu_lo[:, 0, :], [128, 4096], reads=[bGU[0]])
        pieces.extend(round_pieces(2 * p))
        pieces.extend(round_pieces(2 * p + 1))
    while pieces:
        pieces.pop(0)()

    vop("tensor_scalar_mul", (dqd[:, :], s2inv[:, :], mbc[:, 2:3]), [bST], [bST], fd=16)
    tap("dqdf", dqd[:, :], [128, 16], reads=[bST])
    tap("xqf", xq_fm[:, :, :], [128, 8, 2048], reads=[bXQ])
    tap("s2inv", s2inv[:, :], [128, 16], reads=[bST])
    tap("dqgf", dqg[:, :], [128, 16], reads=[bST])
    tap("dquf", dqu[:, :], [128, 16], reads=[bST])

    dma("QS", td_lo[:, :, :], SP_TD[0:16, :, :].rearrange("a p t -> p a t"),
        [bSP["td"]], [bB])
    dma("QS", td_hi[:, :, :], SP_TD[16:32, :, :].rearrange("a p t -> p a t"),
        [bSP["td"]], [bD2[0], bD2[1], bGU[1]])

    def f2_prefetch(t, first=False):
        i = t % 2
        extra = ([bA, bRT[0], bRT[1], bRF[0], bRF[1], bRI, bRA,
                  bGUQ, bGUT, bSIL] if first else [])
        dma("QS", guqs[:, i, :, :], SP_GQ[t, :, :, :].rearrange("a p u -> p a u"),
            [bSPGQ[t]], [bF2[i]] + extra)
        dma("QS", x2t[:, i, :], SP_X2[t, :, :], [bSP["x2"]], [bF2[i]])

    f2_prefetch(0, first=True)
    for t in range(NT):
        i = t % 2
        if t + 1 < NT:
            f2_prefetch(t + 1)
        for dh in range(2):
            bank = 2 * dh
            for it in range(32):
                tdt = td_lo if it < 16 else td_hi
                tdb = bB if it < 16 else B("D0")
                mm(psv(bank), guqs[:, i, it, :],
                   tdt[:, it % 16, 512 * dh:512 * (dh + 1)],
                   it == 0, it == 31, [bF2[i], tdb], [PB[bank]])
            act(fsb[:, 512 * dh:512 * (dh + 1)], psv(bank), AF.Copy,
                [PB[bank], bST], [bA], scale=dqd[:, t:t + 1], fd=512)
        if t == 0:
            tap("fsb0", fsb[:, :], [128, 1024], reads=[bA])
            tap("tdl0", td_lo[:, 0, :], [128, 1024], reads=[bB])
            tap("guqs0", guqs[:, 0, :, :], [128, 32, 128], reads=[bF2[0]])
            tap("x2t0", x2t[:, 0, :], [128, 1024], reads=[bF2[0]])
        vop("reduce_sum", (cst[:, 0:1], fsb[:, :], AX.X), [bA], [bST], fd=1)
        act(t3b[:, 0:512], fsb[:, 0:512], AF.Square, [bA], [bA, bST],
            accum=cst[:, 1:2], fd=512)
        act(t3b[:, 0:512], fsb[:, 512:1024], AF.Square, [bA], [bA, bST],
            accum=cst[:, 2:3], fd=512)
        vop("tensor_add", (cst[:, 1:2], cst[:, 1:2], cst[:, 2:3]), [bST], [bST], fd=1)
        ln_finish(0, 1, 3, 4, 5, D)
        vop("tensor_mul", (cst[:, 5:6], cst[:, 3:4], cst[:, 4:5]), [bST], [bST], fd=1)
        vop("tensor_scalar_mul", (cst[:, 5:6], cst[:, 5:6], -1.0), [bST], [bST], fd=1)
        act(t3b[:, :], fsb[:, :], AF.Identity, [bA, bST], [bA],
            scale=cst[:, 4:5], bias=cst[:, 5:6])
        vop("tensor_add", (ysb[:, i, :], t3b[:, :], x2t[:, i, :]),
            [bA, bF2[i]], fwr("ysb%d" % i, [B("ysb%d" % i)], [bXQ]), fd=1024)

        def y_out(nc, t=t, i=i):
            return nc.gpsimd.dma_start(out=yD[128 * t:128 * (t + 1), :],
                                       in_=ysb[:, i, :])
        P.emit("G", y_out, reads=[B("ysb%d" % i)], writes=[bOUT], queue="QO")

    P.finalize()
    sems = {o: nc.alloc_semaphore(name=f"sem_{o}") for o in OWNERS}

    def val(o2, k2):
        v = P.incno[o2][k2 - 1]
        return v * 16 if o2 in QUEUES else v

    with nc.Block() as blk:
        handles = {"T": blk.tensor, "V": blk.vector, "S": blk.scalar,
                   "G": blk.gpsimd, "Y": blk.sync}
        for e in ENGS:
            if not P.estream[e] and e != "Y":
                continue

            def section(eobj, e=e):
                for (owner, k) in P.estream[e]:
                    op = P.ops[owner][k - 1]
                    for (o2, k2) in op.waits:
                        eobj.wait_ge(sems[o2], val(o2, k2))
                    ins = op.build(nc)
                    if op.marked:
                        ins.then_inc(sems[owner], 16 if owner in QUEUES else 1)
                if e == "Y":
                    for q in QUEUES:
                        tot = P.totals[q]
                        if tot:
                            eobj.wait_ge(sems[q], tot * 16)
            handles[e](section)
    return nc


_NC_CACHE = []


def kernel(**inputs):
    x = np.ascontiguousarray(inputs["L0_attn_hidden_states"], dtype=np.float32)
    Bsz, L, Dm = x.shape
    if not _NC_CACHE:
        _NC_CACHE.append(build_program())
    nc = _NC_CACHE[0]

    cbv = np.zeros((128, 16), np.float32)
    cbv[:, C_HALF] = 0.5
    cbv[:, C_NHALF] = -0.5
    cbv[:, C_P16] = 1.0 / 16
    cbv[:, C_N16] = -1.0 / 16
    cbv[:, C_NLN16] = -LN16
    cbv[:, C_TINY] = 1e-38
    cbv[:, C_ONE] = 1.0
    cbv[:, C_EPS] = 1e-5
    cbv[:, 8] = 64.5
    identv = bf16(np.eye(128, dtype=np.float32))
    utriv = bf16(np.tril(np.ones((128, 128), np.float32)).T)
    wgk1pad = np.zeros((D, 128), np.float32)
    wgk1pad[:, 0:16] = np.asarray(inputs["L0_attn_Wgk1"], np.float32)
    common = {
        "cb": cbv, "ident": identv, "utri": utriv,
        "onescol": np.ones((128, 1), np.float32),
        "onesrow": np.ones((1, 128), np.float32),
        "allones": np.ones((128, 128), np.float32),
        "wgk1pad": wgk1pad,
    }
    for nm in ("L0_attn_Wq", "L0_attn_Wk", "L0_attn_Wv", "L0_attn_Wg",
               "L0_attn_Wgk2", "L0_attn_bgk2",
               "L0_attn_g_norm_w", "L0_attn_Wo", "L0_ffn_w_gate",
               "L0_ffn_w_up", "L0_ffn_w_down"):
        common[nm] = np.ascontiguousarray(inputs[nm], np.float32)

    zeros = np.zeros((T, Dm), np.float32)
    in_maps = []
    for c in range(8):
        b, h = c // 2, c % 2
        m = dict(common)
        m["x_own"] = np.ascontiguousarray(x[b, h * T:(h + 1) * T, :])
        m["x_prev"] = np.ascontiguousarray(x[b, 0:T, :]) if h == 1 else zeros
        in_maps.append(m)

    res = run_bass_kernel_spmd(nc, in_maps, core_ids=list(range(8)))
    out = np.empty((Bsz, L, Dm), np.float32)
    for c in range(8):
        b, h = c // 2, c % 2
        out[b, h * T:(h + 1) * T, :] = res.results[c]["y"]
    return out


# version 20
# speedup vs baseline: 1.0577x; 1.0170x over previous
import numpy as np
import ml_dtypes
import concourse.bass as bass
import concourse.mybir as mybir
from concourse.bass_utils import run_bass_kernel_spmd

F32 = mybir.dt.float32
BF16 = mybir.dt.bfloat16
I32 = mybir.dt.int32
AF = mybir.ActivationFunctionType
ALU = mybir.AluOpType
AX = mybir.AxisListType

T = 2048
NT = 16
NCH = 16
D = 1024
NI = 8
DK = 1024
DV = 2048
NH = 4
INNER = 4096
GRP = 256
NG = T // GRP
LN16 = float(np.log(16.0))

C_HALF, C_NHALF, C_P16, C_N16, C_NLN16, C_TINY, C_ONE, C_EPS = range(8)

_TAPS = {}

ENGS = ("T", "V", "S", "G", "Y")
QUEUES = ("QS", "QA", "QG", "QO")
OWNERS = ENGS + QUEUES
QISSUER = {"QS": "Y", "QA": "S", "QG": "G", "QO": "G"}


class Buf:
    __slots__ = ("name", "writer", "readers")

    def __init__(self, name):
        self.name = name
        self.writer = None
        self.readers = []


class Op:
    __slots__ = ("build", "waits", "marked", "inc", "fd")

    def __init__(self, build, fd):
        self.build = build
        self.fd = fd
        self.waits = []
        self.marked = False
        self.inc = 0


class Plan:
    def __init__(self):
        self.ops = {o: [] for o in OWNERS}
        self.estream = {e: [] for e in ENGS}
        self.synced = {e: {} for e in ENGS}
        self.snaps = {o: [] for o in OWNERS}

    def emit(self, eng, build, reads=(), writes=(), queue=None, fd=1024):
        owner = queue if queue else eng
        k = len(self.ops[owner]) + 1
        op = Op(build, fd)

        deps = set()
        for b in reads:
            w = b.writer
            if w is None:
                continue
            o2, k2 = w
            if o2 == eng and queue is None:
                if eng in ("V", "G") and self.ops[o2][k2 - 1].fd < 512:
                    deps.add(w)
                continue
            deps.add(w)
        for b in writes:
            w = b.writer
            if w is not None:
                o2, k2 = w
                if not (o2 == eng and queue is None):
                    deps.add(w)
                elif eng in ("V", "G") and self.ops[o2][k2 - 1].fd < 512:
                    deps.add(w)
            for r in b.readers:
                o2, k2 = r
                if o2 == eng and queue is None:
                    continue
                deps.add(r)

        syn = self.synced[eng]
        final = []
        for (o2, k2) in sorted(deps):
            if syn.get(o2, 0) >= k2:
                continue
            final.append((o2, k2))
            if syn.get(o2, 0) < k2:
                syn[o2] = k2
            for o3, k3 in self.snaps[o2][k2 - 1].items():
                if syn.get(o3, 0) < k3:
                    syn[o3] = k3
            self.ops[o2][k2 - 1].marked = True

        op.waits = final
        self.ops[owner].append(op)
        self.estream[eng].append((owner, k))
        snap = dict(syn)
        snap[owner] = k
        self.snaps[owner].append(snap)

        for b in writes:
            b.writer = (owner, k)
            b.readers = []
        for b in reads:
            b.readers.append((owner, k))
        return op

    def finalize(self):
        incno = {}
        for o in OWNERS:
            cnt = 0
            lst = []
            for op in self.ops[o]:
                if o in QUEUES:
                    op.marked = True
                if op.marked:
                    cnt += 1
                op.inc = cnt
                lst.append(cnt)
            incno[o] = lst
        self.incno = incno
        self.totals = {o: (incno[o][-1] if incno[o] else 0) for o in OWNERS}


def bf16(x):
    return np.asarray(x, dtype=ml_dtypes.bfloat16)


def build_program():
    nc = bass.Bass("TRN2", target_bir_lowering=False,
                   detect_race_conditions=False)
    P = Plan()

    def din(name, shape, dt=F32):
        return nc.dram_tensor(name, list(shape), dt, kind="ExternalInput").ap()

    x_own = din("x_own", [T, D])
    x_prev = din("x_prev", [T, D])
    Wq = din("L0_attn_Wq", [D, DK])
    Wk = din("L0_attn_Wk", [D, DK])
    Wv = din("L0_attn_Wv", [D, DV])
    Wg = din("L0_attn_Wg", [D, DV])
    Wgk1 = din("wgk1pad", [D, 128])
    Wgk2 = din("L0_attn_Wgk2", [16, DK])
    bgk2 = din("L0_attn_bgk2", [DK])
    gnw = din("L0_attn_g_norm_w", [512])
    Wo = din("L0_attn_Wo", [DV, D])
    wgate = din("L0_ffn_w_gate", [INNER, D])
    wup = din("L0_ffn_w_up", [INNER, D])
    wdown = din("L0_ffn_w_down", [D, INNER])
    cbD = din("cb", [128, 16])
    identD = din("ident", [128, 128], BF16)
    utriD = din("utri", [128, 128], BF16)
    onescolD = din("onescol", [128, 1])
    onesrowD = din("onesrow", [1, 128])
    allonesD = din("allones", [128, 128])
    yD = nc.dram_tensor("y", [T, D], F32, kind="ExternalOutput").ap()

    spk = "ExternalOutput" if "_spills" in _TAPS else "Internal"
    SP_KI = nc.dram_tensor("sp_ki", [NCH, NI, 128, 128], BF16, kind=spk).ap()
    SP_KS = nc.dram_tensor("sp_ks", [NCH, 128, DK], BF16, kind=spk).ap()
    SP_V = nc.dram_tensor("sp_v", [NCH, 128, DV], BF16, kind=spk).ap()
    SP_GS = nc.dram_tensor("sp_gs", [NCH, 128, DV], BF16, kind=spk).ap()
    SP_X2 = nc.dram_tensor("sp_x2", [NCH, 128, D], F32, kind=spk).ap()
    SP_TG = nc.dram_tensor("sp_tg", [NI, 128, INNER], BF16, kind=spk).ap()
    SP_TU = nc.dram_tensor("sp_tu", [NI, 128, INNER], BF16, kind=spk).ap()
    SP_TD = nc.dram_tensor("sp_td", [32, 128, D], BF16, kind=spk).ap()
    SP_GQ = nc.dram_tensor("sp_gq", [NT, 32, 128, 128], BF16, kind=spk).ap()

    BASE = (nc.sbuf_base + 512 + 31) & ~31
    off = [BASE]

    _offs = {}

    def place(name, shape, dt, at=None):
        nbytes = int(np.prod(shape[1:])) * mybir.dt.size(dt)
        if at is None:
            at = off[0]
            off[0] = (at + nbytes + 31) & ~31
        _offs[name] = at
        return nc.alloc_sbuf_tensor_at(name, list(shape), dt, offset=at)

    cb = place("cb", [128, 16], F32)
    bgk_sb = place("bgk", [128, 8], F32)
    gnwb = place("gnwb", [128, 512], F32)
    ident = place("ident", [128, 128], BF16)
    utri = place("utri", [128, 128], BF16)
    onescol = place("onescol", [128, 1], F32)
    onesrow = place("onesrow", [1, 128], F32)
    wgk1b = place("wgk1b", [128, 8, 128], BF16)
    allones = place("allones", [128, 128], F32)
    wgk2b = place("wgk2b", [16, DK], BF16)
    t1b = place("t1b", [16, GRP], BF16)
    gnw_sb = place("gnw_sb", [1, 512], F32)
    S_sb = place("S", [128, 8, 512], F32)
    S_bf = place("S_bf", [128, 8, 512], BF16)
    expbl = place("expbl", [128, 8, 16], F32)
    blastb = place("blastb", [128, 8, 2], F32)
    sums = place("sums", [128, 16], F32)
    sumsq = place("sumsq", [128, 16], F32)
    meanb = place("meanb", [128, 16], F32)
    aepsb = place("aepsb", [128, 16], F32)
    rstdb = place("rstdb", [128, 16], F32)
    nta = place("nta", [128, 16], F32)
    sinv = place("sinv", [128, 16], F32)
    s2inv = place("s2inv", [128, 16], F32)
    dqg = place("dqg", [128, 16], F32)
    dqu = place("dqu", [128, 16], F32)
    dqd = place("dqd", [128, 16], F32)
    mbc = place("mbc", [128, 4], F32)
    mibc = place("mibc", [128, 4], F32)
    part4 = place("part4", [128, 16], F32)
    tot1 = place("tot1", [128, 1], F32)
    t11 = place("t11", [1, 4], F32)
    ssq4 = place("ssq4", [128, 4], F32)
    a4 = place("a4", [128, 4], F32)
    sd4 = place("sd4", [128, 4], F32)
    rs4 = place("rs4", [128, 4], F32)
    n4 = place("n4", [128, 4], F32)
    cst = place("cst", [128, 12], F32)
    sc_sb = place("sc_sb", [128, 128], BF16)
    ternTst = place("ternTst", [128, 4, 128], BF16)
    ternTstB = place("ternTstB", [128, 4, 128], BF16, at=_offs["gnwb"])

    WOFF = off[0]
    wqk = place("wqk", [128, 16, 1024], BF16, at=WOFF)
    wvb = place("wvb", [128, 8, 2048], BF16, at=WOFF)
    wgb = place("wgb", [128, 8, 2048], BF16, at=WOFF)
    wob = place("wob", [128, 16, 1024], BF16, at=WOFF)
    tgs = place("tgs", [128, 2, 8, 512], BF16, at=WOFF)
    tus = place("tus", [128, 2, 8, 512], BF16, at=WOFF + 16384)
    off[0] = WOFF + 32768

    AOFF = off[0]
    xld = place("xld", [128, 4, 1024], F32, at=AOFF)
    xldB = place("xldB", [128, 4, 1024], F32, at=AOFF + 24576)
    htmpB = place("htmpB", [128, 1024], BF16, at=AOFF + 22528)
    junkA = place("junkA", [128, 1024], F32, at=AOFF + 16384)
    htmp = place("htmp", [128, 1024], BF16, at=AOFF + 20480)
    U0 = place("U0", [128, 8, GRP], F32, at=AOFF)
    U1 = place("U1", [128, 8, GRP], F32, at=AOFF + 8192)
    U2 = place("U2", [128, 8, GRP], F32, at=AOFF + 16384)
    U3 = place("U3", [128, 8, GRP], F32, at=AOFF + 24576)
    ksfm = place("ksfm", [128, 8, GRP], BF16, at=AOFF + 32768)
    kstm = place("kstm", [128, 2, 1024], BF16, at=AOFF + 36864)
    vtmp = place("vtmp", [128, 2, 2048], BF16, at=AOFF)
    silu_t = place("silu_t", [128, 512], F32, at=AOFF + 8192)
    osb = place("osb", [128, 2048], F32, at=AOFF)
    asb = place("asb", [128, 1024], F32, at=AOFF + 2048)
    oft = place("oft", [128, 2048], BF16, at=AOFF + 8192)
    oTs = place("oTs", [128, 16, 128], BF16, at=AOFF + 12288)
    t2x2 = place("t2x2", [128, 1024], F32, at=AOFF + 16384)
    junkC = place("junkC", [128, 512], F32, at=AOFF + 20480)
    h2b = place("h2b", [128, 1024], F32, at=AOFF + 20480)
    rt = place("rt", [128, 1024], F32, at=AOFF + 24576)
    ri = place("ri", [128, 1024], I32, at=AOFF + 28672)
    ra = place("ra", [128, 1024], F32, at=AOFF + 28672)
    rf = place("rf", [128, 1024], F32, at=AOFF + 32768)
    xqt = place("xqt", [128, 1024], BF16, at=AOFF + 36864)
    wch = place("wch", [128, 2, 4096], F32, at=AOFF)
    ternb = place("ternb", [128, 4096], BF16, at=AOFF + 32768)
    silu_f1 = place("silu_f1", [128, 512], F32, at=AOFF + 24576)
    rt1p = [place("rt1a", [128, 1024], F32, at=AOFF),
            place("rt1b", [128, 1024], F32, at=AOFF + 4096)]
    rf1p = [place("rf1a", [128, 1024], F32, at=AOFF + 8192),
            place("rf1b", [128, 1024], F32, at=AOFF + 12288)]
    ri1q = place("ri1q", [128, 1024], I32, at=AOFF + 16384)
    ra1q = place("ra1q", [128, 1024], F32, at=AOFF + 20480)
    guq = place("guq", [128, 4096], BF16, at=AOFF + 26624)
    guqT = place("guqT", [128, 16, 128], BF16, at=AOFF + 34816)
    guqs = place("guqs", [128, 2, 32, 128], BF16, at=AOFF)
    x2t = place("x2t", [128, 2, 1024], F32, at=AOFF + 16384)
    fsb = place("fsb", [128, 1024], F32, at=AOFF + 24576)
    t3b = place("t3b", [128, 1024], F32, at=AOFF + 28672)
    off[0] = AOFF + 40960

    BOFF = off[0]
    qg = place("qg", [128, 8, 2048], BF16, at=BOFF)
    f0t = place("f0t", [128, 4096], F32, at=BOFF)
    f0bn = place("f0bn", [128, 4096], F32, at=BOFF + 16384)
    gu_lo = place("gu_lo", [128, 2, 4096], F32, at=BOFF)
    td_lo = place("td_lo", [128, 16, 1024], BF16, at=BOFF)
    off[0] = BOFF + 32768

    COFF = off[0]
    h_fm = place("h_fm", [128, 8, 2048], BF16, at=COFF)
    xq_fm = place("xq_fm", [128, 8, 2048], BF16, at=COFF)
    ysb = place("ysb", [128, 2, 1024], F32, at=COFF)
    off[0] = COFF + 32768

    DOFF = off[0]
    vtmp0 = place("vtmp0", [128, 2, 2048], BF16, at=DOFF)
    vs = place("vs", [128, 2, 2048], BF16, at=DOFF)
    kss = place("kss", [128, 2, 1024], BF16, at=DOFF + 8192)
    kis = place("kis", [128, 2, 8, 128], BF16, at=DOFF + 12288)
    gss = place("gss", [128, 2, 2048], BF16, at=DOFF + 16384)
    xcs = place("xcs", [128, 2, 1024], F32, at=DOFF + 24576)
    gu_hi = place("gu_hi", [128, 2, 4096], F32, at=DOFF)
    td_hi = place("td_hi", [128, 16, 1024], BF16, at=DOFF)
    kigrp = place("kigrp", [128, 8, GRP], BF16, at=DOFF)
    off[0] = DOFF + 32768

    assert off[0] <= 227328, f"sbuf overflow: {off[0]}"
    nc.sbuf_base = off[0]

    PS = [nc.alloc_psum_tensor(f"ps{i}", [128, 1024], F32) for i in range(4)]

    def psv(bank, lo=0, hi=512):
        return PS[bank // 2][:, 512 * (bank % 2) + lo: 512 * (bank % 2) + hi]

    bufs = {}

    def B(name):
        if name not in bufs:
            bufs[name] = Buf(name)
        return bufs[name]

    PB = [B(f"pb{i}") for i in range(8)]
    bW, bA, bB, bC = B("W"), B("A"), B("BB"), B("BC")
    bS = [B(f"S{d}") for d in range(8)]
    bST = B("stats")
    bCONST = B("const")
    bSC = B("scores")
    bKIG = B("kigrp")
    bT1 = B("t1")
    bTST = B("ternTst")
    bXQ = B("xq")
    bD2 = [B("D0"), B("D1")]
    bTG2 = [B("TG0"), B("TG1")]
    bF2 = [B("F20"), B("F21")]
    bOUT = B("out")
    bSP = {n: B("sp_" + n) for n in
           ("ki", "ks", "v", "gs", "x2", "tg", "tu", "td", "gq")}
    bSPGQ = [B("spgq%d" % _t) for _t in range(NT)]

    def dma(queue, out, in_, reads, writes):
        eng = QISSUER[queue]
        attr = {"Y": "sync", "S": "scalar", "G": "gpsimd"}[eng]

        def bld(nc, out=out, in_=in_, attr=attr):
            return getattr(nc, attr).dma_start(out=out, in_=in_)
        P.emit(eng, bld, reads=reads, writes=writes, queue=queue)

    def mm(out_ap, lhsT, rhs, start, stop, reads, writes):
        def bld(nc, o=out_ap, l=lhsT, r=rhs, st=start, sp=stop):
            return nc.tensor.matmul(o, l, r, start=st, stop=sp)
        P.emit("T", bld, reads=reads, writes=writes)

    _LUT_FUNCS = (AF.Sqrt, AF.Sigmoid, AF.Ln, AF.Exp, AF.Silu)

    def act(out_ap, in_ap, func, reads, writes, scale=None, bias=None,
            accum=None, fd=1024):
        def bld(nc, o=out_ap, i=in_ap, f=func, s=scale, b=bias, a=accum):
            kw = {}
            if s is not None:
                kw["scale"] = s
            if b is not None:
                kw["bias"] = b
            if a is not None:
                kw["accum_out"] = a
            if f in _LUT_FUNCS:
                nc.scalar.preload_act_table(f)
            return nc.scalar.activation(o, i, f, **kw)
        P.emit("S", bld, reads=reads, writes=writes, fd=fd)

    def scopy(out_ap, in_ap, reads, writes, fd=512):
        def bld(nc, o=out_ap, i=in_ap):
            return nc.scalar.copy(o, i)
        P.emit("S", bld, reads=reads, writes=writes, fd=fd)

    def vop(method, args, reads, writes, fd, eng="V"):
        def bld(nc, m=method, a=args, e=eng):
            eo = nc.vector if e == "V" else nc.gpsimd
            return getattr(eo, m)(*a)
        P.emit(eng, bld, reads=reads, writes=writes, fd=fd)

    def tap(label, ap, shape, dt=None, reads=()):
        if label not in _TAPS:
            return
        dt = dt or ap.dtype
        dbg = nc.dram_tensor("dbg_" + label, list(shape), dt,
                             kind="ExternalOutput").ap()
        dma("QO", dbg, ap, list(reads), [B("dbg_" + label)])

    def ln_finish(csum, cssq, cmean, crstd, cscr, n):
        vop("tensor_scalar_mul", (cst[:, cmean:cmean + 1], cst[:, csum:csum + 1],
            1.0 / n), [bST], [bST], fd=1)
        vop("tensor_mul", (cst[:, cscr:cscr + 1], cst[:, cmean:cmean + 1],
            cst[:, cmean:cmean + 1]), [bST], [bST], fd=1)
        vop("scalar_tensor_tensor",
            (cst[:, crstd:crstd + 1], cst[:, cssq:cssq + 1], 1.0 / n,
             cst[:, cscr:cscr + 1], ALU.mult, ALU.subtract), [bST], [bST], fd=1)
        vop("tensor_scalar_add", (cst[:, crstd:crstd + 1], cst[:, crstd:crstd + 1],
            1e-5), [bST], [bST], fd=1)
        act(cst[:, cscr:cscr + 1], cst[:, crstd:crstd + 1], AF.Sqrt,
            [bST], [bST], fd=1)
        vop("reciprocal", (cst[:, crstd:crstd + 1], cst[:, cscr:cscr + 1]),
            [bST], [bST], fd=1)

    dma("QS", cb[:, :], cbD, [], [bCONST])
    dma("QS", ident[:, :], identD, [], [bCONST])
    dma("QS", utri[:, :], utriD, [], [bCONST])
    dma("QS", onescol[:, :], onescolD, [], [bCONST])
    dma("QS", onesrow[:, :], onesrowD, [], [bCONST])
    for d in range(8):
        dma("QS", bgk_sb[:, d:d + 1], bgk2[128 * d:128 * (d + 1)].unsqueeze(1),
            [], [bCONST])
    dma("QS", gnw_sb[:, :], gnw.unsqueeze(0), [], [bCONST])
    dma("QG", wgk1b[:, :, :], Wgk1.rearrange("(a p) f -> p a f", p=128), [], [bCONST])
    dma("QS", allones[:, :], allonesD, [], [bCONST])
    dma("QG", wgk2b[:, :], Wgk2, [], [bCONST])
    mm(psv(4), onesrow[:, :], gnw_sb[:, :], True, True, [bCONST], [PB[4]])
    vop("tensor_copy", (gnwb[:, :], psv(4)), [PB[4], bCONST], [bCONST], fd=512)
    vop("memset", (S_sb[:, :, :], 0.0), [], list(bS), fd=4096)

    bXLD = [B("xldA"), B("xldB")]
    bHT = [B("htA"), B("htB")]

    def emit_ln_h(xdram, tapit=False):
        for ch in range(4):
            pi = ch % 2
            xt = xld if pi == 0 else xldB
            dma("QS", xt[:, :, :],
                xdram[512 * ch:512 * (ch + 1), :].rearrange("(a p) f -> p a f", p=128),
                [], [bXLD[pi], bA] if ch < 2 else [bXLD[pi]])
            for tt in range(4):
                t = 4 * ch + tt
                vop("reduce_sum", (sums[:, t:t + 1], xt[:, tt, :], AX.X),
                    [bXLD[pi]], [bST], fd=1)
                act(junkA[:, :], xt[:, tt, :], AF.Square, [bXLD[pi]], [bA, bST],
                    accum=sumsq[:, t:t + 1])
        if tapit:
            tap("sumsq", sumsq[:, :], [128, 16], reads=[bST])
        vop("tensor_scalar_mul", (meanb[:, :], sums[:, :], 1.0 / D), [bST], [bST], fd=16)
        vop("tensor_mul", (nta[:, :], meanb[:, :], meanb[:, :]), [bST], [bST], fd=16)
        if tapit:
            tap("nta", nta[:, :], [128, 16], reads=[bST])
        vop("scalar_tensor_tensor",
            (aepsb[:, :], sumsq[:, :], 1.0 / D, nta[:, :], ALU.mult, ALU.subtract),
            [bST], [bST], fd=16)
        vop("tensor_scalar_add", (aepsb[:, :], aepsb[:, :], 1e-5), [bST], [bST], fd=16)
        act(nta[:, :], aepsb[:, :], AF.Sqrt, [bST], [bST], fd=16)
        vop("reciprocal", (rstdb[:, :], nta[:, :]), [bST], [bST], fd=16)
        vop("tensor_mul", (sums[:, :], rstdb[:, :], rstdb[:, :]), [bST], [bST], fd=16)
        vop("tensor_mul", (sums[:, :], sums[:, :], aepsb[:, :]), [bST], [bST], fd=16)
        vop("tensor_scalar", (sums[:, :], sums[:, :], -0.5, 1.5, ALU.mult, ALU.add),
            [bST], [bST], fd=16)
        vop("tensor_mul", (rstdb[:, :], rstdb[:, :], sums[:, :]), [bST], [bST], fd=16)
        if tapit:
            tap("aeps", aepsb[:, :], [128, 16], reads=[bST])
            tap("rstd", rstdb[:, :], [128, 16], reads=[bST])
            tap("meanb", meanb[:, :], [128, 16], reads=[bST])
        for ch in range(4):
            pi = ch % 2
            xt = xld if pi == 0 else xldB
            dma("QS", xt[:, :, :],
                xdram[512 * ch:512 * (ch + 1), :].rearrange("(a p) f -> p a f", p=128),
                [], [bXLD[pi]])
            for tt in range(4):
                t = 4 * ch + tt
                hh = tt % 2
                ht = htmp if hh == 0 else htmpB
                hw = [bHT[hh], bA] if (ch == 0 and tt < 2) else [bHT[hh]]
                vop("tensor_scalar",
                    (ht[:, :], xt[:, tt, :], meanb[:, t:t + 1], rstdb[:, t:t + 1],
                     ALU.subtract, ALU.mult), [bXLD[pi], bST], hw, fd=1024)
                for f in range(8):
                    bank = 4 + (f // 4)
                    mm(psv(bank, 128 * (f % 4), 128 * (f % 4) + 128),
                       ht[:, 128 * f:128 * (f + 1)], ident[:, :], True, True,
                       [bHT[hh], bCONST], [PB[bank]])
                for half in range(2):
                    bank = 4 + half
                    scopy(h_fm[:, 4 * half:4 * half + 4, 128 * t:128 * (t + 1)],
                          psv(bank).rearrange("p (a f) -> p a f", a=4),
                          [PB[bank]], [bC])
        vop("tensor_scalar_mul", (cst[:, 9:10], cst[:, 9:10], 1.0),
            [], [bA, bXLD[0], bXLD[1], bHT[0], bHT[1]], fd=1)

    def emit_gk(g, want_q):
        cols = slice(GRP * g, GRP * (g + 1))
        for it in range(NI):
            mm(PS[2][:, 0:GRP], wgk1b[:, it, :], h_fm[:, it, cols],
               it == 0, it == NI - 1, [bCONST, bC], [PB[4]])
        scopy(t1b[:, :], PS[2][0:16, 0:GRP], [PB[4]], [bT1], fd=GRP)
        for d in range(NI):
            bank = 2 + (d % 2)
            mm(psv(bank, 0, GRP), wgk2b[:, 128 * d:128 * (d + 1)], t1b[:, :],
               True, True, [bCONST, bT1], [PB[bank]])
            act(U0[:, d, :], psv(bank, 0, GRP), AF.Sigmoid, [PB[bank], bCONST],
                [bA], bias=bgk_sb[:, d:d + 1], fd=GRP)
        vop("tensor_scalar_max", (U0[:, :, :], U0[:, :, :], 1e-38), [bA], [bA],
            fd=8 * GRP)
        if g == 0 and want_q:
            tap("sg0", U0[:, :, :], [128, 8, GRP], reads=[bA])
        act(U1[:, :, :], U0[:, :, :], AF.Ln, [bA], [bA], fd=8 * GRP)
        for d in range(NI):
            for lc in range(GRP // 128):
                si = U1[:, d, 128 * lc:128 * (lc + 1)]
                so = U0[:, d, 128 * lc:128 * (lc + 1)]
                vop("tensor_tensor_scan", (so, si, si, 0.0, ALU.add, ALU.bypass),
                    [bA], [bA], fd=128)
        for lc in range(GRP // 128):
            vop("tensor_copy", (blastb[:, :, lc:lc + 1], U0[:, :, 128 * lc + 127:128 * lc + 128]),
                [bA], [bST], fd=8)
        if g == 0 and want_q:
            tap("ls0", U1[:, :, :], [128, 8, GRP], reads=[bA])
            tap("b0", U0[:, :, :], [128, 8, GRP], reads=[bA])
            tap("blast0", blastb[:, :, :], [128, 8, 2], reads=[bST])
        act(expbl[:, :, 2 * g:2 * g + 2], blastb[:, :, :], AF.Exp,
            [bST, bCONST], [bST], scale=cb[:, C_P16:C_P16 + 1], fd=16)
        if want_q:
            act(U2[:, :, :], U0[:, :, :], AF.Exp, [bA, bCONST], [bA],
                scale=cb[:, C_P16:C_P16 + 1], bias=cb[:, C_NLN16:C_NLN16 + 1],
                fd=8 * GRP)
            act(U3[:, :, :], U0[:, :, :], AF.Exp, [bA, bCONST], [bA],
                scale=cb[:, C_N16:C_N16 + 1], fd=8 * GRP)
        for d in range(NI):
            for lc in range(GRP // 128):
                vop("tensor_scalar",
                    (U1[:, d, 128 * lc:128 * (lc + 1)],
                     U0[:, d, 128 * lc:128 * (lc + 1)],
                     blastb[:, d, lc:lc + 1], None, ALU.subtract),
                    [bA, bST], [bA], fd=128)
        act(U0[:, :, :], U1[:, :, :], AF.Exp, [bA, bCONST], [bA],
            scale=cb[:, C_N16:C_N16 + 1], fd=8 * GRP)

    def emit_ks_transpose_group():
        for lc in range(2):
            for dh in range(2):
                bank = 4 + dh
                for dd in range(4):
                    d = 4 * dh + dd
                    mm(psv(bank, 128 * dd, 128 * dd + 128),
                       ksfm[:, d, 128 * lc:128 * (lc + 1)], ident[:, :], True, True,
                       [bA, bCONST], [PB[bank]])
                scopy(kstm[:, lc, 512 * dh:512 * (dh + 1)], psv(bank),
                      [PB[bank]], [bA])

    emit_ln_h(x_prev)
    wv_p0 = qg
    dma("QG", wqk[:, 0:8, :], Wk.rearrange("(a p) f -> p a f", p=128), [bW], [bW])
    dma("QG", wv_p0[:, :, :], Wv.rearrange("(a p) f -> p a f", p=128), [bB], [bB])
    for g in range(NG):
        cols = slice(GRP * g, GRP * (g + 1))
        emit_gk(g, want_q=False)
        for d in range(NI):
            bank = 0 + (d % 2)
            for it in range(NI):
                mm(psv(bank, 0, GRP), wqk[:, it, 128 * d:128 * (d + 1)],
                   h_fm[:, it, cols], it == 0, it == NI - 1, [bW, bC], [PB[bank]])
            vop("tensor_mul", (ksfm[:, d, :], psv(bank, 0, GRP), U0[:, d, :]),
                [PB[bank], bA], [bA], fd=GRP)
        emit_ks_transpose_group()
        for tt in range(2):
            for dvg in range(4):
                bank = 2 + (dvg % 2)
                for it in range(NI):
                    mm(psv(bank),
                       h_fm[:, it, GRP * g + 128 * tt: GRP * g + 128 * (tt + 1)],
                       wv_p0[:, it, 512 * dvg:512 * (dvg + 1)],
                       it == 0, it == NI - 1, [bB, bC], [PB[bank]])
                scopy(vtmp0[:, tt, 512 * dvg:512 * (dvg + 1)], psv(bank),
                      [PB[bank]], [bD2[0]])
        for lc in range(2):
            c = 2 * g + lc
            for hh in range(NH):
                for j in range(2):
                    dd = 2 * hh + j
                    bank = 6 + j
                    mm(psv(bank),
                       kstm[:, lc, 256 * hh + 128 * j: 256 * hh + 128 * (j + 1)],
                       vtmp0[:, lc, 512 * hh:512 * (hh + 1)], True, True,
                       [bA, bD2[0], bS[dd]], [PB[bank]])
                    vop("scalar_tensor_tensor",
                        (S_sb[:, dd, :], S_sb[:, dd, :], expbl[:, dd, c:c + 1],
                         psv(bank), ALU.mult, ALU.add),
                        [bS[dd], bST, PB[bank]], [bS[dd]], fd=512)

    emit_ln_h(x_own, tapit=True)
    tap("h", h_fm[:, :, :], [128, 8, 2048], reads=[bC])

    dma("QG", wqk[:, 0:8, :], Wq.rearrange("(a p) f -> p a f", p=128), [bW], [bW])
    dma("QG", wqk[:, 8:16, :], Wk.rearrange("(a p) f -> p a f", p=128), [bW], [bW])
    for g in range(NG):
        cols = slice(GRP * g, GRP * (g + 1))
        emit_gk(g, want_q=True)
        for d in range(NI):
            bank = 0 + (d % 2)
            for it in range(NI):
                mm(psv(bank, 0, GRP), wqk[:, it, 128 * d:128 * (d + 1)],
                   h_fm[:, it, cols], it == 0, it == NI - 1, [bW, bC], [PB[bank]])
            vop("tensor_mul", (qg[:, d, cols], psv(bank, 0, GRP), U2[:, d, :]),
                [PB[bank], bA], [bB], fd=GRP)
            bank2 = 2 + (d % 2)
            for it in range(NI):
                mm(psv(bank2, 0, GRP), wqk[:, 8 + it, 128 * d:128 * (d + 1)],
                   h_fm[:, it, cols], it == 0, it == NI - 1, [bW, bC], [PB[bank2]])
            vop("tensor_mul", (kigrp[:, d, :], psv(bank2, 0, GRP), U3[:, d, :]),
                [PB[bank2], bA], [bKIG, bD2[0]], fd=GRP)
            vop("tensor_mul", (ksfm[:, d, :], psv(bank2, 0, GRP), U0[:, d, :]),
                [PB[bank2], bA], [bA], fd=GRP)

        for lc in range(2):
            def ki_out(nc, g=g, lc=lc):
                return nc.scalar.dma_start(
                    out=SP_KI[2 * g + lc, :, :, :].rearrange("a p t -> p a t"),
                    in_=kigrp[:, :, 128 * lc:128 * (lc + 1)])
            P.emit("S", ki_out, reads=[bKIG, bD2[0]], writes=[bSP["ki"]], queue="QA")
        emit_ks_transpose_group()

        def ks_out(nc, g=g):
            return nc.scalar.dma_start(
                out=SP_KS[2 * g:2 * g + 2, :, :].rearrange("l p d -> p l d"),
                in_=kstm[:, :, :])
        P.emit("S", ks_out, reads=[bA], writes=[bSP["ks"]], queue="QA")

    tap("qg", qg[:, :, :], [128, 8, 2048], reads=[bB])
    tap("expbl", expbl[:, :, :], [128, 8, 16], reads=[bST])
    dma("QG", wvb[:, :, :], Wv.rearrange("(a p) f -> p a f", p=128), [bW], [bW])
    for g in range(NG):
        for tt in range(2):
            for dvg in range(4):
                bank = (dvg % 2)
                for it in range(NI):
                    mm(psv(bank),
                       h_fm[:, it, GRP * g + 128 * tt: GRP * g + 128 * (tt + 1)],
                       wvb[:, it, 512 * dvg:512 * (dvg + 1)],
                       it == 0, it == NI - 1, [bW, bC], [PB[bank]])
                scopy(vtmp[:, tt, 512 * dvg:512 * (dvg + 1)], psv(bank),
                      [PB[bank]], [bA])

        def v_out(nc, g=g):
            return nc.scalar.dma_start(
                out=SP_V[2 * g:2 * g + 2, :, :].rearrange("l p d -> p l d"),
                in_=vtmp[:, :, :])
        P.emit("S", v_out, reads=[bA], writes=[bSP["v"]], queue="QA")

    dma("QG", wgb[:, :, :], Wg.rearrange("(a p) f -> p a f", p=128), [bW], [bW])
    for g in range(NG):
        for tt in range(2):
            for hh in range(4):
                bank = (hh % 2)
                for it in range(NI):
                    mm(psv(bank),
                       h_fm[:, it, GRP * g + 128 * tt: GRP * g + 128 * (tt + 1)],
                       wgb[:, it, 512 * hh:512 * (hh + 1)],
                       it == 0, it == NI - 1, [bW, bC], [PB[bank]])
                act(silu_t[:, :], psv(bank), AF.Silu, [PB[bank]], [bA], fd=512)
                vop("tensor_mul", (vtmp[:, tt, 512 * hh:512 * (hh + 1)],
                                   silu_t[:, :], gnwb[:, :]),
                    [bA, bCONST], [bA], fd=512)

        def gs_out(nc, g=g):
            return nc.scalar.dma_start(
                out=SP_GS[2 * g:2 * g + 2, :, :].rearrange("l p d -> p l d"),
                in_=vtmp[:, :, :])
        P.emit("S", gs_out, reads=[bA], writes=[bSP["gs"]], queue="QA")

    dma("QG", wob[:, :, :], Wo.rearrange("(a p) f -> p a f", p=128), [bW], [bW])
    for dd in range(8):
        tap(f"Sprev{dd}", S_sb[:, dd, :], [128, 512], reads=[bS[dd]])
        scopy(S_bf[:, dd, :], S_sb[:, dd, :], [bS[dd]], [bS[dd]])

    def c_prefetch(c):
        i = c % 2
        dma("QS", vs[:, i, :], SP_V[c, :, :], [bSP["v"]], [bD2[i]])
        dma("QS", kss[:, i, :], SP_KS[c, :, :], [bSP["ks"]], [bD2[i]])
        dma("QS", kis[:, i, :, :], SP_KI[c, :, :, :].rearrange("a p t -> p a t"),
            [bSP["ki"]], [bD2[i]])
        dma("QS", gss[:, i, :], SP_GS[c, :, :], [bSP["gs"]], [bD2[i]])
        dma("QS", xcs[:, i, :], x_own[128 * c:128 * (c + 1), :], [], [bD2[i]])

    c_prefetch(0)
    for c in range(NCH):
        i = c % 2
        if c + 1 < NCH:
            c_prefetch(c + 1)
        ccols = slice(128 * c, 128 * (c + 1))
        for hh in range(NH):
            for j in range(2):
                dd = 2 * hh + j
                mm(PS[0][:, 0:128], kis[:, i, dd, :], qg[:, dd, ccols],
                   j == 0, j == 1, [bD2[i], bB], [PB[0]])
            vop("tensor_mul", (sc_sb[:, :], PS[0][:, 0:128], utri[:, :]),
                [PB[0], bCONST], [bSC], fd=128)
            mm(psv(1), sc_sb[:, :], vs[:, i, 512 * hh:512 * (hh + 1)],
               True, False, [bSC, bD2[i]], [PB[1]])
            for j in range(2):
                dd = 2 * hh + j
                mm(psv(1), qg[:, dd, ccols], S_bf[:, dd, :],
                   False, j == 1, [bB, bS[dd]], [PB[1]])
            vop("tensor_copy", (osb[:, 512 * hh:512 * (hh + 1)], psv(1)),
                [PB[1]], [bA], fd=512)
            for j in range(2):
                dd = 2 * hh + j
                bank = 2 + j
                mm(psv(bank),
                   kss[:, i, 256 * hh + 128 * j: 256 * hh + 128 * (j + 1)],
                   vs[:, i, 512 * hh:512 * (hh + 1)], True, True,
                   [bD2[i], bS[dd]], [PB[bank]])
                vop("scalar_tensor_tensor",
                    (S_sb[:, dd, :], S_sb[:, dd, :], expbl[:, dd, c:c + 1],
                     psv(bank), ALU.mult, ALU.add),
                    [bS[dd], bST, PB[bank]], [bS[dd]], fd=512)
                scopy(S_bf[:, dd, :], S_sb[:, dd, :], [bS[dd]], [bS[dd]])
            act(junkC[:, :], osb[:, 512 * hh:512 * (hh + 1)], AF.Square,
                [bA], [bA, bST], accum=ssq4[:, hh:hh + 1], fd=512)
        if c == 0:
            tap("osb0", osb[:, :], [128, 2048], reads=[bA])
            tap("ssq40", ssq4[:, :], [128, 4], reads=[bST])
        vop("tensor_scalar", (a4[:, :], ssq4[:, :], 1.0 / 512, 1e-5,
            ALU.mult, ALU.add), [bST], [bST], fd=4)
        act(sd4[:, :], a4[:, :], AF.Sqrt, [bST], [bST], fd=4)
        vop("reciprocal", (rs4[:, :], sd4[:, :]), [bST], [bST], fd=4)
        vop("tensor_mul", (n4[:, :], rs4[:, :], rs4[:, :]), [bST], [bST], fd=4)
        vop("tensor_mul", (n4[:, :], n4[:, :], a4[:, :]), [bST], [bST], fd=4)
        vop("tensor_scalar", (n4[:, :], n4[:, :], -0.5, 1.5, ALU.mult, ALU.add),
            [bST], [bST], fd=4)
        vop("tensor_mul", (rs4[:, :], rs4[:, :], n4[:, :]), [bST], [bST], fd=4)
        for hh in range(NH):
            vop("scalar_tensor_tensor",
                (oft[:, 512 * hh:512 * (hh + 1)], osb[:, 512 * hh:512 * (hh + 1)],
                 rs4[:, hh:hh + 1], gss[:, i, 512 * hh:512 * (hh + 1)],
                 ALU.mult, ALU.mult), [bA, bST, bD2[i]], [bA], fd=512)
        for b4 in range(4):
            bank = 4 + (b4 % 2)
            for dd in range(4):
                dvt = 4 * b4 + dd
                mm(psv(bank, 128 * dd, 128 * dd + 128),
                   oft[:, 128 * dvt:128 * (dvt + 1)], ident[:, :], True, True,
                   [bA, bCONST], [PB[bank]])
            scopy(oTs[:, 4 * b4:4 * b4 + 4, :],
                  psv(bank).rearrange("p (a f) -> p a f", a=4), [PB[bank]], [bA])
        for dh in range(2):
            for dvt in range(16):
                mm(PS[3][:, 512 * dh:512 * (dh + 1)], oTs[:, dvt, :],
                   wob[:, dvt, 512 * dh:512 * (dh + 1)],
                   dvt == 0, dvt == 15, [bA, bW], [PB[6 + dh]])
        vop("tensor_copy", (asb[:, :], PS[3][:, :]), [PB[6], PB[7]], [bA], fd=1024)
        if c == 0:
            tap("asb0", asb[:, :], [128, 1024], reads=[bA])
            tap("oft0", oft[:, :], [128, 2048], reads=[bA])
        vop("reduce_sum", (cst[:, 0:1], asb[:, :], AX.X), [bA], [bST], fd=1)
        act(junkC[:, :], asb[:, 0:512], AF.Square, [bA], [bA, bST],
            accum=cst[:, 1:2], fd=512)
        act(junkC[:, :], asb[:, 512:1024], AF.Square, [bA], [bA, bST],
            accum=cst[:, 2:3], fd=512)
        vop("tensor_add", (cst[:, 1:2], cst[:, 1:2], cst[:, 2:3]), [bST], [bST], fd=1)
        ln_finish(0, 1, 3, 4, 5, D)
        vop("tensor_mul", (cst[:, 5:6], cst[:, 3:4], cst[:, 4:5]), [bST], [bST], fd=1)
        vop("tensor_scalar_mul", (cst[:, 5:6], cst[:, 5:6], -1.0), [bST], [bST], fd=1)
        act(t2x2[:, :], asb[:, :], AF.Identity, [bA, bST], [bA],
            scale=cst[:, 4:5], bias=cst[:, 5:6])
        vop("tensor_add", (t2x2[:, :], t2x2[:, :], xcs[:, i, :]),
            [bA, bD2[i]], [bA], fd=1024)

        if c == 0:
            tap("x20", t2x2[:, :], [128, 1024], reads=[bA])
        def x2_out(nc, c=c):
            return nc.scalar.dma_start(out=SP_X2[c, :, :], in_=t2x2[:, :])
        P.emit("S", x2_out, reads=[bA], writes=[bSP["x2"]], queue="QA")
        vop("reduce_sum", (cst[:, 6:7], t2x2[:, :], AX.X), [bA], [bST], fd=1)
        act(junkC[:, :], t2x2[:, 0:512], AF.Square, [bA], [bA, bST],
            accum=cst[:, 7:8], fd=512)
        act(junkC[:, :], t2x2[:, 512:1024], AF.Square, [bA], [bA, bST],
            accum=cst[:, 8:9], fd=512)
        vop("tensor_add", (cst[:, 7:8], cst[:, 7:8], cst[:, 8:9]), [bST], [bST], fd=1)
        ln_finish(6, 7, 9, 10, 11, D)
        vop("tensor_scalar", (h2b[:, :], t2x2[:, :], cst[:, 9:10], cst[:, 10:11],
            ALU.subtract, ALU.mult), [bA, bST], [bA], fd=1024)

        if c == 0:
            tap("h20", h2b[:, :], [128, 1024], reads=[bA])
            tap("cst0", cst[:, 0:11], [128, 11], reads=[bST])

        def redmax(nc):
            return nc.vector.reduce_max(cst[:, 11:12], h2b[:, :], axis=AX.X,
                                        apply_absolute_value=True)
        P.emit("V", redmax, reads=[bA], writes=[bST], fd=1)
        vop("tensor_scalar_max", (cst[:, 11:12], cst[:, 11:12], 1e-5),
            [bST], [bST], fd=1)
        vop("tensor_scalar_mul", (sinv[:, c:c + 1], cst[:, 11:12], 1.0 / 127.0),
            [bST], [bST], fd=1)
        vop("reciprocal", (cst[:, 11:12], sinv[:, c:c + 1]), [bST], [bST], fd=1)
        act(rt[:, :], h2b[:, :], AF.Identity, [bA, bST], [bA], scale=cst[:, 11:12])
        act(ri[:, :], h2b[:, :], AF.Identity, [bA, bST], [bA],
            scale=cst[:, 11:12])
        act(rf[:, :], ri[:, :], AF.Identity, [bA], [bA])
        vop("tensor_sub", (rt[:, :], rt[:, :], rf[:, :]), [bA], [bA], fd=1024)
        vop("tensor_single_scalar", (ra[:, :], rt[:, :], 0.5, ALU.is_gt),
            [bA], [bA], fd=1024)
        for _k in (1.5, 2.5, 3.5):
            vop("scalar_tensor_tensor", (ra[:, :], rt[:, :], _k, ra[:, :],
                ALU.is_gt, ALU.add), [bA], [bA], fd=1024)
        vop("scalar_tensor_tensor", (ra[:, :], rt[:, :], -0.5, ra[:, :],
            ALU.is_lt, ALU.subtract), [bA], [bA], fd=1024)
        for _k in (-1.5, -2.5, -3.5):
            vop("scalar_tensor_tensor", (ra[:, :], rt[:, :], _k, ra[:, :],
                ALU.is_lt, ALU.add), [bA], [bA], fd=1024)
        vop("tensor_sub", (xqt[:, :], rf[:, :], ra[:, :]), [bA], [bA], fd=1024)
        if c == 0:
            tap("xqt0", xqt[:, :], [128, 1024], reads=[bA])
        for half in range(2):
            bank = 4 + half
            for dd in range(4):
                f = 4 * half + dd
                mm(psv(bank, 128 * dd, 128 * dd + 128),
                   xqt[:, 128 * f:128 * (f + 1)], ident[:, :], True, True,
                   [bA, bCONST], [PB[bank]])
            scopy(xq_fm[:, 4 * half:4 * half + 4, 128 * c:128 * (c + 1)],
                  psv(bank).rearrange("p (a f) -> p a f", a=4),
                  [PB[bank]], [bXQ, bC])

    bWC = [B("wc0"), B("wc1")]
    bF0T2 = [B("f0tb0"), B("f0tb1")]
    bF0B2 = [B("f0bb0"), B("f0bb1")]
    bTRN2 = [B("trnb0"), B("trnb1")]
    bTST2 = [bTST, B("ternTstB")]
    ternTst2 = [ternTst, ternTstB]
    _f0first = set()

    def f0w(key, bufs, extra):
        if key in _f0first:
            return bufs
        _f0first.add(key)
        return bufs + extra

    def emit_wquant(wdram, O, IW, mcol, spill, spkey):
        rows = 524288 // IW
        nchunks = O // rows
        n_it = IW // 128
        n_ot = rows // 128
        for ch in range(nchunks):
            pi = ch % 2
            src = wdram[rows * ch: rows * (ch + 1), :]
            if n_ot > 1:
                src = src.rearrange("(a p) f -> p a f", p=128)
                dst = wch[:, pi, :].rearrange("p (a f) -> p a f", a=n_ot)
            else:
                dst = wch[:, pi, :]
            dma("QS" if pi == 0 else "QG", dst, src, [],
                f0w("wc%d" % pi, [bWC[pi]], [bA]))

            def red(nc, pi=pi, ch=ch):
                return nc.vector.reduce_sum(part4[:, ch:ch + 1], wch[:, pi, :],
                                            axis=AX.X, apply_absolute_value=True)
            P.emit("V", red, reads=[bWC[pi]], writes=[bST], fd=1)
        vop("reduce_sum", (tot1[:, :], part4[:, 0:nchunks], AX.X), [bST], [bST], fd=1)
        mm(PS[2][:, 0:1], allones[:, :], tot1[:, :], True, True,
           [bCONST, bST], [PB[4]])
        vop("tensor_copy", (mbc[:, mcol:mcol + 1], PS[2][:, 0:1]), [PB[4]], [bST], fd=1)
        vop("tensor_scalar_mul", (mbc[:, mcol:mcol + 1], mbc[:, mcol:mcol + 1],
            1.0 / (O * IW)), [bST], [bST], fd=1)
        vop("tensor_scalar_max", (mbc[:, mcol:mcol + 1], mbc[:, mcol:mcol + 1],
            1e-5), [bST], [bST], fd=1)
        vop("reciprocal", (mibc[:, mcol:mcol + 1], mbc[:, mcol:mcol + 1]),
            [bST], [bST], fd=1)
        for ch in range(nchunks):
            pi = ch % 2
            src = wdram[rows * ch: rows * (ch + 1), :]
            if n_ot > 1:
                src = src.rearrange("(a p) f -> p a f", p=128)
                dst = wch[:, pi, :].rearrange("p (a f) -> p a f", a=n_ot)
            else:
                dst = wch[:, pi, :]
            dma("QS" if pi == 0 else "QG", dst, src, [], [bWC[pi]])
            for h in range(2):
                cs = slice(2048 * h, 2048 * (h + 1))
                act(f0t[:, cs], wch[:, pi, cs], AF.Identity, [bWC[pi], bST],
                    f0w("f0t%d" % h, [bF0T2[h]], [bB]),
                    scale=mibc[:, mcol:mcol + 1], fd=2048)
                vop("tensor_single_scalar", (f0bn[:, cs], f0t[:, cs], -0.5,
                    ALU.is_lt), [bF0T2[h]], f0w("f0b%d" % h, [bF0B2[h]], [bB]),
                    fd=2048)
                vop("scalar_tensor_tensor", (ternb[:, cs], f0t[:, cs], 0.5,
                    f0bn[:, cs], ALU.is_gt, ALU.subtract),
                    [bF0T2[h], bF0B2[h]], f0w("trn%d" % h, [bTRN2[h]], [bA]),
                    fd=2048)
            for ot in range(n_ot):
                oglob = n_ot * ch + ot
                for itb in range(n_it // 4):
                    bank = 4 + (itb % 2)
                    ti2 = (ot * (n_it // 4) + itb) % 2
                    hh = (IW * ot + 512 * itb) // 2048
                    for dd in range(4):
                        it = 4 * itb + dd
                        mm(psv(bank, 128 * dd, 128 * dd + 128),
                           ternb[:, IW * ot + 128 * it: IW * ot + 128 * (it + 1)],
                           ident[:, :], True, True, [bTRN2[hh], bCONST],
                           [PB[bank]])
                    if ti2 == 0:
                        scopy(ternTst2[0][:, :, :],
                              psv(bank).rearrange("p (a f) -> p a f", a=4),
                              [PB[bank]], f0w("tst0", [bTST2[0]], [bCONST, bA]))
                    else:
                        vop("tensor_copy",
                            (ternTst2[1][:, :, :],
                             psv(bank).rearrange("p (a f) -> p a f", a=4)),
                            [PB[bank]], f0w("tst1", [bTST2[1]], [bCONST, bA]),
                            fd=512)

                    def tout(nc, itb=itb, oglob=oglob, spill=spill, ti2=ti2):
                        return nc.scalar.dma_start(
                            out=spill[4 * itb:4 * itb + 4, :,
                                      128 * oglob:128 * (oglob + 1)]
                            .rearrange("a p t -> p a t"),
                            in_=ternTst2[ti2][:, :, :])
                    P.emit("S", tout, reads=[bTST2[ti2]], writes=[bSP[spkey]],
                           queue="QA")

    emit_wquant(wgate, INNER, D, 0, SP_TG, "tg")
    emit_wquant(wup, INNER, D, 1, SP_TU, "tu")
    emit_wquant(wdown, D, INNER, 2, SP_TD, "td")

    tap("mbc", mbc[:, 0:3], [128, 3], reads=[bST])
    tap("sinv", sinv[:, :], [128, 16], reads=[bST])
    vop("tensor_scalar_mul", (dqg[:, :], sinv[:, :], mbc[:, 0:1]), [bST], [bST], fd=16)
    vop("tensor_scalar_mul", (dqu[:, :], sinv[:, :], mbc[:, 1:2]), [bST], [bST], fd=16)

    bGU = [bB, B("GUHI")]
    bRT = [B("rt1a"), B("rt1b")]
    bRF = [B("rf1a"), B("rf1b")]
    bRI, bRA = B("ri1q"), B("ra1q")
    bGUQ, bGUT, bSIL = B("guq1"), B("guT1"), B("silF1")
    _first = set()

    def fwr(key, bufs, extra):
        if key in _first:
            return bufs
        _first.add(key)
        return bufs + extra

    def round_pieces(t):
        reg = (t // 2) % 2
        gu_t = gu_lo if reg == 0 else gu_hi
        gb = bGU[reg]
        sc = 10 + (t % 2)

        def p_rmax():
            def rmx(nc, gu_t=gu_t, ti=t % 2, sc=sc):
                return nc.vector.reduce_max(cst[:, sc:sc + 1], gu_t[:, ti, :],
                                            axis=AX.X, apply_absolute_value=True)
            P.emit("V", rmx, reads=[gb], writes=[bST], fd=1)
            vop("tensor_scalar_max", (cst[:, sc:sc + 1], cst[:, sc:sc + 1], 1e-5),
                [bST], [bST], fd=1)
            vop("tensor_scalar_mul", (s2inv[:, t:t + 1], cst[:, sc:sc + 1],
                1.0 / 127.0), [bST], [bST], fd=1)
            vop("reciprocal", (cst[:, sc:sc + 1], s2inv[:, t:t + 1]),
                [bST], [bST], fd=1)

        def p_acts(q):
            pi = q % 2
            qc = slice(1024 * q, 1024 * (q + 1))
            act(rt1p[pi][:, :], gu_t[:, t % 2, qc], AF.Identity,
                [gb, bST], fwr("rt%d" % pi, [bRT[pi]], [bA]),
                scale=cst[:, sc:sc + 1], bias=cb[:, 8:9], fd=1024)
            act(ri1q[:, :], gu_t[:, t % 2, qc], AF.Identity,
                [gb, bST], fwr("ri", [bRI], [bA]),
                scale=cst[:, sc:sc + 1], fd=1024)
            act(rf1p[pi][:, :], ri1q[:, :], AF.Identity,
                [bRI], fwr("rf%d" % pi, [bRF[pi]], [bA]), fd=1024)

        def p_ladder(q):
            pi = q % 2
            qc = slice(1024 * q, 1024 * (q + 1))
            vop("scalar_tensor_tensor", (ra1q[:, :], rf1p[pi][:, :], -1.0,
                rt1p[pi][:, :], ALU.mult, ALU.add),
                [bRF[pi], bRT[pi]], fwr("ra", [bRA], [bA]), fd=1024)
            for _k in (32.0, 16.0, 8.0, 4.0, 2.0, 1.0):
                vop("tensor_single_scalar", (rf1p[pi][:, :], ra1q[:, :], _k,
                    ALU.is_ge), [bRA], [bRF[pi]], fd=1024)
                vop("scalar_tensor_tensor", (ra1q[:, :], rf1p[pi][:, :], -_k,
                    ra1q[:, :], ALU.mult, ALU.add), [bRF[pi]], [bRA], fd=1024)
            vop("scalar_tensor_tensor", (guq[:, qc], ra1q[:, :], -1.0,
                rt1p[pi][:, :], ALU.mult, ALU.add),
                [bRA, bRT[pi]], fwr("guq", [bGUQ], [bA]), fd=1024)
            vop("tensor_single_scalar", (guq[:, qc], guq[:, qc], -32.0, ALU.add),
                [bGUQ], [bGUQ], fd=1024)

        def p_trans():
            tap("guqt%d" % t, guq[:, :], [128, 4096], reads=[bGUQ])
            for half in range(2):
                for b4 in range(4 * half, 4 * half + 4):
                    bank = 4 + (b4 % 2)
                    for dd in range(4):
                        it = 4 * b4 + dd
                        mm(psv(bank, 128 * dd, 128 * dd + 128),
                           guq[:, 128 * it:128 * (it + 1)], ident[:, :],
                           True, True, [bGUQ, bCONST], [PB[bank]])
                    scopy(guqT[:, 4 * (b4 % 4):4 * (b4 % 4) + 4, :],
                          psv(bank).rearrange("p (a f) -> p a f", a=4),
                          [PB[bank]], fwr("guT", [bGUT], [bA]))

                def gq_out(nc, t=t, half=half):
                    return nc.scalar.dma_start(
                        out=SP_GQ[t, 16 * half:16 * (half + 1), :, :]
                        .rearrange("a p u -> p a u"),
                        in_=guqT[:, :, :])
                P.emit("S", gq_out, reads=[bGUT], writes=[bSPGQ[t]],
                       queue="QA")

        return [lambda: (p_rmax(), p_acts(0), p_acts(1)),
                lambda: (p_ladder(0), p_acts(2)),
                lambda: (p_ladder(1), p_acts(3)),
                lambda: p_ladder(2),
                lambda: p_ladder(3),
                p_trans]

    pieces = []
    DRAIN = [2, 2, 2, 2, 1, 1, 1, 1]
    for p in range(8):
        preg = p % 2
        gu_t = gu_lo if preg == 0 else gu_hi
        gkey = "gulo" if preg == 0 else "guhi"
        gext = [] if preg == 0 else [bD2[0], bD2[1]]
        for og in range(8):
            pi = og % 2
            wr_extra = [bW] if (p == 0 and og < 2) else []
            dma("QS", tgs[:, pi, :, :],
                SP_TG[:, :, 512 * og:512 * (og + 1)].rearrange("a p t -> p a t"),
                [bSP["tg"]], [bTG2[pi]] + wr_extra)
            dma("QG", tus[:, pi, :, :],
                SP_TU[:, :, 512 * og:512 * (og + 1)].rearrange("a p t -> p a t"),
                [bSP["tu"]], [bTG2[pi]] + wr_extra)
            for ti in range(2):
                t = 2 * p + ti
                gpb = psv(2 * ti)
                upb = psv(1 + 2 * ti)
                for it in range(NI):
                    mm(gpb, xq_fm[:, it, 128 * t:128 * (t + 1)],
                       tgs[:, pi, it, :], it == 0, it == NI - 1,
                       [bXQ, bTG2[pi]], [PB[2 * ti]])
                act(silu_f1[:, :], gpb, AF.Silu, [PB[2 * ti], bST],
                    fwr("sil", [bSIL], [bA]), scale=dqg[:, t:t + 1], fd=512)
                for it in range(NI):
                    mm(upb, xq_fm[:, it, 128 * t:128 * (t + 1)],
                       tus[:, pi, it, :], it == 0, it == NI - 1,
                       [bXQ, bTG2[pi]], [PB[1 + 2 * ti]])
                vop("scalar_tensor_tensor",
                    (gu_t[:, ti, 512 * og:512 * (og + 1)], upb,
                     dqu[:, t:t + 1], silu_f1[:, :], ALU.mult, ALU.mult),
                    [PB[1 + 2 * ti], bST, bSIL],
                    fwr(gkey, [bGU[preg]], gext), fd=512)
            for _ in range(DRAIN[og]):
                if pieces:
                    pieces.pop(0)()
        if p == 0:
            tap("guA", gu_lo[:, 0, :], [128, 4096], reads=[bGU[0]])
        pieces.extend(round_pieces(2 * p))
        pieces.extend(round_pieces(2 * p + 1))
    while pieces:
        pieces.pop(0)()

    vop("tensor_scalar_mul", (dqd[:, :], s2inv[:, :], mbc[:, 2:3]), [bST], [bST], fd=16)
    tap("dqdf", dqd[:, :], [128, 16], reads=[bST])
    tap("xqf", xq_fm[:, :, :], [128, 8, 2048], reads=[bXQ])
    tap("s2inv", s2inv[:, :], [128, 16], reads=[bST])
    tap("dqgf", dqg[:, :], [128, 16], reads=[bST])
    tap("dquf", dqu[:, :], [128, 16], reads=[bST])

    dma("QS", td_lo[:, :, :], SP_TD[0:16, :, :].rearrange("a p t -> p a t"),
        [bSP["td"]], [bB])
    dma("QS", td_hi[:, :, :], SP_TD[16:32, :, :].rearrange("a p t -> p a t"),
        [bSP["td"]], [bD2[0], bD2[1], bGU[1]])

    def f2_prefetch(t, first=False):
        i = t % 2
        extra = ([bA, bRT[0], bRT[1], bRF[0], bRF[1], bRI, bRA,
                  bGUQ, bGUT, bSIL] if first else [])
        dma("QS", guqs[:, i, :, :], SP_GQ[t, :, :, :].rearrange("a p u -> p a u"),
            [bSPGQ[t]], [bF2[i]] + extra)
        dma("QS", x2t[:, i, :], SP_X2[t, :, :], [bSP["x2"]], [bF2[i]])

    f2_prefetch(0, first=True)
    for t in range(NT):
        i = t % 2
        if t + 1 < NT:
            f2_prefetch(t + 1)
        for dh in range(2):
            bank = 2 * dh
            for it in range(32):
                tdt = td_lo if it < 16 else td_hi
                tdb = bB if it < 16 else B("D0")
                mm(psv(bank), guqs[:, i, it, :],
                   tdt[:, it % 16, 512 * dh:512 * (dh + 1)],
                   it == 0, it == 31, [bF2[i], tdb], [PB[bank]])
            act(fsb[:, 512 * dh:512 * (dh + 1)], psv(bank), AF.Copy,
                [PB[bank], bST], [bA], scale=dqd[:, t:t + 1], fd=512)
        if t == 0:
            tap("fsb0", fsb[:, :], [128, 1024], reads=[bA])
            tap("tdl0", td_lo[:, 0, :], [128, 1024], reads=[bB])
            tap("guqs0", guqs[:, 0, :, :], [128, 32, 128], reads=[bF2[0]])
            tap("x2t0", x2t[:, 0, :], [128, 1024], reads=[bF2[0]])
        vop("reduce_sum", (cst[:, 0:1], fsb[:, :], AX.X), [bA], [bST], fd=1)
        act(t3b[:, 0:512], fsb[:, 0:512], AF.Square, [bA], [bA, bST],
            accum=cst[:, 1:2], fd=512)
        act(t3b[:, 0:512], fsb[:, 512:1024], AF.Square, [bA], [bA, bST],
            accum=cst[:, 2:3], fd=512)
        vop("tensor_add", (cst[:, 1:2], cst[:, 1:2], cst[:, 2:3]), [bST], [bST], fd=1)
        ln_finish(0, 1, 3, 4, 5, D)
        vop("tensor_mul", (cst[:, 5:6], cst[:, 3:4], cst[:, 4:5]), [bST], [bST], fd=1)
        vop("tensor_scalar_mul", (cst[:, 5:6], cst[:, 5:6], -1.0), [bST], [bST], fd=1)
        act(t3b[:, :], fsb[:, :], AF.Identity, [bA, bST], [bA],
            scale=cst[:, 4:5], bias=cst[:, 5:6])
        vop("tensor_add", (ysb[:, i, :], t3b[:, :], x2t[:, i, :]),
            [bA, bF2[i]], fwr("ysb%d" % i, [B("ysb%d" % i)], [bXQ]), fd=1024)

        def y_out(nc, t=t, i=i):
            return nc.gpsimd.dma_start(out=yD[128 * t:128 * (t + 1), :],
                                       in_=ysb[:, i, :])
        P.emit("G", y_out, reads=[B("ysb%d" % i)], writes=[bOUT], queue="QO")

    P.finalize()
    sems = {o: nc.alloc_semaphore(name=f"sem_{o}") for o in OWNERS}

    def val(o2, k2):
        v = P.incno[o2][k2 - 1]
        return v * 16 if o2 in QUEUES else v

    with nc.Block() as blk:
        handles = {"T": blk.tensor, "V": blk.vector, "S": blk.scalar,
                   "G": blk.gpsimd, "Y": blk.sync}
        for e in ENGS:
            if not P.estream[e] and e != "Y":
                continue

            def section(eobj, e=e):
                for (owner, k) in P.estream[e]:
                    op = P.ops[owner][k - 1]
                    for (o2, k2) in op.waits:
                        eobj.wait_ge(sems[o2], val(o2, k2))
                    ins = op.build(nc)
                    if op.marked:
                        ins.then_inc(sems[owner], 16 if owner in QUEUES else 1)
                if e == "Y":
                    for q in QUEUES:
                        tot = P.totals[q]
                        if tot:
                            eobj.wait_ge(sems[q], tot * 16)
            handles[e](section)
    return nc


_NC_CACHE = []


def kernel(**inputs):
    x = np.ascontiguousarray(inputs["L0_attn_hidden_states"], dtype=np.float32)
    Bsz, L, Dm = x.shape
    if not _NC_CACHE:
        _NC_CACHE.append(build_program())
    nc = _NC_CACHE[0]

    cbv = np.zeros((128, 16), np.float32)
    cbv[:, C_HALF] = 0.5
    cbv[:, C_NHALF] = -0.5
    cbv[:, C_P16] = 1.0 / 16
    cbv[:, C_N16] = -1.0 / 16
    cbv[:, C_NLN16] = -LN16
    cbv[:, C_TINY] = 1e-38
    cbv[:, C_ONE] = 1.0
    cbv[:, C_EPS] = 1e-5
    cbv[:, 8] = 32.5
    identv = bf16(np.eye(128, dtype=np.float32))
    utriv = bf16(np.tril(np.ones((128, 128), np.float32)).T)
    wgk1pad = np.zeros((D, 128), np.float32)
    wgk1pad[:, 0:16] = np.asarray(inputs["L0_attn_Wgk1"], np.float32)
    common = {
        "cb": cbv, "ident": identv, "utri": utriv,
        "onescol": np.ones((128, 1), np.float32),
        "onesrow": np.ones((1, 128), np.float32),
        "allones": np.ones((128, 128), np.float32),
        "wgk1pad": wgk1pad,
    }
    for nm in ("L0_attn_Wq", "L0_attn_Wk", "L0_attn_Wv", "L0_attn_Wg",
               "L0_attn_Wgk2", "L0_attn_bgk2",
               "L0_attn_g_norm_w", "L0_attn_Wo", "L0_ffn_w_gate",
               "L0_ffn_w_up", "L0_ffn_w_down"):
        common[nm] = np.ascontiguousarray(inputs[nm], np.float32)

    zeros = np.zeros((T, Dm), np.float32)
    in_maps = []
    for c in range(8):
        b, h = c // 2, c % 2
        m = dict(common)
        m["x_own"] = np.ascontiguousarray(x[b, h * T:(h + 1) * T, :])
        m["x_prev"] = np.ascontiguousarray(x[b, 0:T, :]) if h == 1 else zeros
        in_maps.append(m)

    res = run_bass_kernel_spmd(nc, in_maps, core_ids=list(range(8)))
    out = np.empty((Bsz, L, Dm), np.float32)
    for c in range(8):
        b, h = c // 2, c % 2
        out[b, h * T:(h + 1) * T, :] = res.results[c]["y"]
    return out
